# version 31
# baseline (speedup 1.0000x reference)
import numpy as np
from contextlib import ExitStack

import concourse.bass as bass
import concourse.mybir as mybir
from concourse.bass_utils import run_bass_kernel_spmd

F32 = mybir.dt.float32
BF16 = mybir.dt.bfloat16
I32 = mybir.dt.int32
ALU = mybir.AluOpType
ACTF = mybir.ActivationFunctionType

ENGINES = ("sync", "gpsimd", "vector", "scalar", "tensor")

T = 2048
H = 3
TW = T + H
D = 1024
KT = 8
INTER = 2048
NH, HD, DS = 32, 64, 128
NCH, C = 16, 128
FF = 4096


class Sem:
    def __init__(self, h):
        self.h = h
        self.n = 0


class Prog:
    def __init__(self, nc):
        self.nc = nc
        self.streams = {e: [] for e in ENGINES}
        self.es = ExitStack()
        self.sems = {}
        names = ["VE", "AC", "MM", "GP", "CC"]
        names += ["LQ%d" % i for i in range(8)] + ["GQ%d" % i for i in range(8)]
        for name in names:
            self.sems[name] = Sem(self.es.enter_context(nc.semaphore(name)))
        self.lastw = {(e, s): 0 for e in ENGINES for s in self.sems}
        self.qstate = {
            "sync": {"pool": ["LQ%d" % i for i in range(8)], "epoch": 0, "cur": None, "last": None},
            "gpsimd": {"pool": ["GQ%d" % i for i in range(8)], "epoch": 0, "cur": None, "last": None},
        }

    def sb(self, name, shape, dt):
        return self.es.enter_context(self.nc.sbuf_tensor(name, shape, dt))

    def emit(self, eng, fn):
        self.streams[eng].append(fn)

    def wait(self, eng, sname, val=None):
        s = self.sems[sname]
        v = s.n if val is None else val
        if v <= self.lastw[(eng, sname)]:
            return
        self.lastw[(eng, sname)] = v
        self.emit(eng, lambda e, h=s.h, v=v: e.wait_ge(h, v))

    ENG_SEM = {"vector": "VE", "scalar": "AC", "gpsimd": "GP"}

    def op(self, eng, fn, inc=None, n=1):
        own = self.ENG_SEM.get(eng)
        if own is not None:
            self.wait(eng, own)
            inc = own
        if inc is not None:
            s = self.sems[inc]
            s.n += n
            self.emit(eng, lambda e, f=fn, h=s.h, m=n: f(e).then_inc(h, m))
        else:
            self.emit(eng, fn)

    def dma(self, eng, out, in_, accum=False):
        q = "gpsimd" if eng == "gpsimd" else "sync"
        st = self.qstate[q]
        if eng == "gpsimd":
            self.wait(eng, "GP")
        if st["cur"] is None:
            nm = st["pool"][st["epoch"] % len(st["pool"])]
            st["epoch"] += 1
            st["cur"] = nm
            self.wait(eng, nm)
        s = self.sems[st["cur"]]
        s.n += 16
        if accum:
            self.emit(eng, lambda e, o=out, i=in_, h=s.h: e.dma_start(
                out=o, in_=i, accum_op=ALU.add).then_inc(h, 16))
        else:
            self.emit(eng, lambda e, o=out, i=in_, h=s.h: e.dma_start(
                out=o, in_=i).then_inc(h, 16))

    def dclose(self, q):
        st = self.qstate[q]
        if st["cur"] is not None:
            st["last"] = (st["cur"], self.sems[st["cur"]].n)
            st["cur"] = None
        return st["last"]

    def ewait(self, eng, token):
        if token is None:
            return
        nm, tot = token
        self.wait(eng, nm, tot)

    def dwait(self, eng, q):
        self.ewait(eng, self.dclose(q))

    def dma_barrier(self, *engines):
        self.dclose("sync")
        self.dclose("gpsimd")
        engs = engines if engines else ENGINES
        for e in engs:
            for nm in self.qstate["sync"]["pool"] + self.qstate["gpsimd"]["pool"]:
                self.wait(e, nm)

    def barrier(self, *producers):
        prs = list(producers)
        if "LD" in prs or "GD" in prs:
            self.dma_barrier()
            prs = [x for x in prs if x not in ("LD", "GD")]
        for e in ENGINES:
            for pr in prs:
                self.wait(e, pr)

    def build(self):
        with self.nc.allow_non_contiguous_dma("small strided param loads"), self.nc.Block() as block:
            @block.sync
            def _(e):
                for f in self.streams["sync"]:
                    f(e)

            @block.gpsimd
            def _(e):
                for f in self.streams["gpsimd"]:
                    f(e)

            @block.vector
            def _(e):
                for f in self.streams["vector"]:
                    f(e)

            @block.scalar
            def _(e):
                for f in self.streams["scalar"]:
                    f(e)

            @block.tensor
            def _(e):
                for f in self.streams["tensor"]:
                    f(e)


def _carve(arena, dt, lo_bytes, n_elems, parts=128):
    esz = {F32: 4, BF16: 2, I32: 4}[dt]
    assert lo_bytes % esz == 0
    ap = arena[:, :].bitcast(dt)
    lo = lo_bytes // esz
    return ap[0:parts, lo:lo + n_elems]


def build_nc():
    nc = bass.Bass("TRN2", target_bir_lowering=False)
    d_u = nc.dram_tensor("u_r", [D, TW], F32, kind="ExternalInput")
    d_win = nc.dram_tensor("win", [D, 4384], F32, kind="ExternalInput")
    d_convw = nc.dram_tensor("convw", [2304, 4], F32, kind="ExternalInput")
    d_convb = nc.dram_tensor("convb", [2304], F32, kind="ExternalInput")
    d_dtb = nc.dram_tensor("dtb", [NH], F32, kind="ExternalInput")
    d_alog = nc.dram_tensor("alog", [NH], F32, kind="ExternalInput")
    d_dpar = nc.dram_tensor("dpar", [NH], F32, kind="ExternalInput")
    d_normw = nc.dram_tensor("normw", [INTER], F32, kind="ExternalInput")
    d_w1 = nc.dram_tensor("w1m", [32, 128, KT, 128], F32, kind="ExternalInput")
    d_b1 = nc.dram_tensor("b1", [FF], F32, kind="ExternalInput")
    d_w2 = nc.dram_tensor("w2", [FF, D], F32, kind="ExternalInput")
    d_w1b = nc.dram_tensor("w1b_d", [32, 128, 1024], BF16, kind="Internal")
    d_woutb = nc.dram_tensor("woutb_d", [8, 128, 2048], BF16, kind="Internal")
    d_b2 = nc.dram_tensor("b2", [D], F32, kind="ExternalInput")
    d_wout = nc.dram_tensor("woutm", [8, 128, 16, 128], F32, kind="ExternalInput")
    d_sel = nc.dram_tensor("sel", [8, 1], F32, kind="ExternalInput")
    d_btwT = nc.dram_tensor("btwT", [8, 8], F32, kind="ExternalInput")
    d_out = nc.dram_tensor("out", [D, T], F32, kind="ExternalOutput")
    d_zsil = nc.dram_tensor("zsil_d", [NCH, 16, 128, C], BF16, kind="Internal")
    d_xtok = nc.dram_tensor("xtok_d", [NCH, 128, INTER], BF16, kind="Internal")
    d_sall = nc.dram_tensor("sall_d", [NCH, 128, INTER], BF16, kind="Internal")
    d_rel = nc.dram_tensor("rel_d", [NCH, NH * C], F32, kind="Internal")
    d_scb = nc.dram_tensor("scb_d", [8, INTER], F32, kind="Internal")
    d_ccin = nc.dram_tensor("cc_in", [129, INTER], F32, kind="Internal")
    d_ccout = nc.dram_tensor("cc_out", [8 * 129, INTER], F32, kind="Internal", addr_space="Shared")

    p = Prog(nc)
    sb = p.sb
    xn = sb("xn", [128, KT * TW], BF16)
    AR1 = sb("AR1", [128, 16384], F32)
    AR2 = sb("AR2", [128, 18459], F32)
    Bf = sb("Bf", [128, T], BF16)
    Cf = sb("Cf", [128, T], BF16)
    Bt = sb("Bt", [128, NCH * C], BF16)
    dt_s = sb("dt_s", [NH, T], F32)
    ga = sb("ga", [NH, T], F32)
    dtok = sb("dtok", [128, NCH * NH], F32)
    rtok = sb("rtok", [128, NCH * NH], F32)
    identF = sb("identF", [128, 128], F32)
    maskle = sb("maskle", [128, 128], F32)
    onesb = sb("onesb", [128, 1], BF16)
    onesrB = sb("onesrB", [1, 128], BF16)
    onesrF = sb("onesrF", [1, 128], F32)
    convw_s = sb("convw_s", [128, 72], F32)
    convb_s = sb("convb_s", [128, 18], F32)
    dtb_s = sb("dtb_s", [NH, 1], F32)
    acol = sb("acol", [NH, 1], F32)
    b1c = sb("b1c", [128, 32], F32)
    b2c = sb("b2c", [128, 8], F32)
    c155 = sb("c155", [128, 1], F32)
    c1e5 = sb("c1e5", [128, 1], F32)
    nw_s = sb("nw_s", [128, 16], F32)
    sel_s = sb("sel_s", [8, 1], F32)
    btw_s = sb("btw_s", [8, 8], F32)
    zcol = sb("zcol", [NH, 1], F32)
    gaex = sb("gaex", [NH, NCH], F32)
    cdee = sb("cdee", [NH, NCH], F32)
    crow = sb("crow", [1, 32], F32)
    rrow = sb("rrow", [1, 128], F32)
    smallF = sb("smallF", [NH, C], F32)
    smallF2 = sb("smallF2", [NH, C], F32)
    db_s = sb("db_s", [128, NH], F32)
    gls = sb("gls", [8, 32], F32)
    esc = sb("esc", [8, 32], F32)
    PSA = p.es.enter_context(nc.psum_tensor("PSA", [128, 2048], F32))
    PSB = p.es.enter_context(nc.psum_tensor("PSB", [128, 2048], F32))

    S = "sync"
    G = "gpsimd"

    p.dma(S, convw_s[:, :].rearrange("p (ct k) -> p ct k", ct=18),
          bass.AP(d_convw, 0, [[4, 128], [512, 18], [1, 4]]))
    p.dma(S, convb_s[:, :], bass.AP(d_convb, 0, [[1, 128], [128, 18]]))
    p.dma(S, dtb_s[:, :], d_dtb.ap().unsqueeze(1))
    p.dma(S, acol[:, :], d_alog.ap().unsqueeze(1))
    p.dma(S, b1c[:, :], bass.AP(d_b1, 0, [[1, 128], [128, 32]]))
    p.dma(S, b2c[:, :], bass.AP(d_b2, 0, [[1, 128], [128, 8]]))
    p.dma(S, nw_s[:, :], bass.AP(d_normw, 0, [[1, 128], [128, 16]]))
    p.dma(S, sel_s[:, :], d_sel.ap())
    p.dma(S, btw_s[:, :], d_btwT.ap())
    p.dma(S, db_s[:, :], d_dpar.ap().unsqueeze(0).unsqueeze(1).to_broadcast([1, 128, NH]))
    iot_p = _carve(AR2, F32, 0, 128)
    iot_f = _carve(AR2, F32, 512, 128)
    p.op(G, lambda e: e.iota(iot_p, [[0, 128]], channel_multiplier=1,
                             allow_small_or_imprecise_dtypes=True), inc="GP")
    p.op(G, lambda e: e.iota(iot_f, [[1, 128]], channel_multiplier=0,
                             allow_small_or_imprecise_dtypes=True), inc="GP")
    p.wait("vector", "GP")
    p.dwait("vector", "sync")
    V = "vector"
    p.op(V, lambda e: e.tensor_tensor(identF[:, :], iot_p, iot_f, ALU.is_equal), inc="VE")
    p.op(V, lambda e: e.tensor_tensor(maskle[:, :], iot_p, iot_f, ALU.is_le), inc="VE")
    p.op(V, lambda e: e.tensor_scalar(onesb[:, :], identF[:, 0:1], 0.0, 1.0, ALU.mult, ALU.add), inc="VE")
    p.op(V, lambda e: e.tensor_scalar(onesrB[:, :], identF[0:1, :], 0.0, 1.0, ALU.mult, ALU.add), inc="VE")
    p.op(V, lambda e: e.tensor_scalar(onesrF[:, :], identF[0:1, :], 0.0, 1.0, ALU.mult, ALU.add), inc="VE")
    p.op(V, lambda e: e.tensor_scalar(zcol[:, :], dtb_s[:, :], 0.0, None, ALU.mult), inc="VE")
    p.op(V, lambda e: e.tensor_scalar(b1c[:, :], b1c[:, :], 256.0, None, ALU.mult), inc="VE")
    p.op(V, lambda e: e.tensor_scalar(b2c[:, :], b2c[:, :], 256.0, None, ALU.mult), inc="VE")
    p.op(V, lambda e: e.tensor_scalar(c155[:, :], identF[:, 0:1], 0.0, -0.5, ALU.mult, ALU.add), inc="VE")
    p.op(V, lambda e: e.tensor_scalar(c1e5[:, :], identF[:, 0:1], 0.0, 1e-5, ALU.mult, ALU.add), inc="VE")
    p.ewait("scalar", p.qstate["sync"]["last"])
    p.op("scalar", lambda e: e.activation(acol[:, :], acol[:, :], ACTF.Exp, bias=0.0, scale=1.0), inc="AC")
    p.wait(V, "AC")
    p.op(V, lambda e: e.tensor_scalar(acol[:, :], acol[:, :], -1.0, None, ALU.mult), inc="VE")

    u_t = [_carve(AR1, F32, k * TW * 4, TW) for k in range(7)]
    u_t.append(_carve(AR2, F32, 1024 * 4, TW))
    ub_t = [_carve(AR2, BF16, 16384, TW * KT)[:, k * TW:(k + 1) * TW] for k in range(KT)]
    u2b = _carve(AR2, BF16, 49200, TW)
    xsc = _carve(AR2, F32, 53600, 512)
    mrowS = _carve(AR2, F32, 55648, 512, parts=1)
    rrowS = _carve(AR2, F32, 57696, 512, parts=1)
    for k in range(KT):
        p.dma(S, u_t[k], d_u.ap()[k * 128:(k + 1) * 128, :])
    p.barrier("LD")
    for k in range(KT):
        p.op(V, lambda e, k=k: e.tensor_copy(ub_t[k], u_t[k]), inc="VE")
    ncs = [(0, 512), (512, 512), (1024, 512), (1536, 512), (2048, 3)]
    mrowS2 = _carve(AR2, F32, 59744, 512, parts=1)
    rrowS2 = _carve(AR2, F32, 61792, 512, parts=1)
    ve_ub = p.sems["VE"].n
    ac_sq, mm_stats, ve_var, ac_sqrt, ve_recip, mm_bcast, ve_xn = {}, {}, {}, {}, {}, {}, {}
    u2bB = _carve(AR2, BF16, 63840, 4096)

    def emit_squares(j):
        if j >= len(ncs):
            return
        c0, w = ncs[j]
        if j >= 1:
            p.wait("scalar", "MM", mm_stats[j - 1])
        for k in range(KT):
            p.op("scalar", lambda e, k=k, c0=c0, w=w: e.activation(
                u2bB[:, k * w:(k + 1) * w], u_t[k][:, c0:c0 + w], ACTF.Square, bias=0.0, scale=1.0), inc="AC")
        ac_sq[j] = p.sems["AC"].n

    emit_squares(0)
    for j, (c0, w) in enumerate(ncs):
        sb0 = (j % 2) * 1024
        pa0 = (j % 2) * 1024
        mro = mrowS if j % 2 == 0 else mrowS2
        rro = rrowS if j % 2 == 0 else rrowS2
        p.wait("tensor", "AC", ac_sq[j])
        p.wait("tensor", "VE", ve_ub)
        if j - 2 >= 0:
            p.wait("tensor", "VE", ve_recip[j - 2])
        for k in range(KT):
            p.op("tensor", lambda e, k=k, c0=c0, w=w, sb0=sb0: e.matmul(
                PSB[0:1, sb0:sb0 + w], onesb[:, :], ub_t[k][:, c0:c0 + w],
                start=(k == 0), stop=(k == KT - 1)), inc="MM")
            p.op("tensor", lambda e, k=k, w=w, sb0=sb0: e.matmul(
                PSB[0:1, sb0 + 512:sb0 + 512 + w], onesb[:, :], u2bB[:, k * w:(k + 1) * w],
                start=(k == 0), stop=(k == KT - 1)), inc="MM")
        mm_stats[j] = p.sems["MM"].n
        emit_squares(j + 1)
        p.wait(V, "MM", mm_stats[j])
        p.op(V, lambda e, w=w, sb0=sb0, mro=mro: e.tensor_scalar(
            mro[:, 0:w], PSB[0:1, sb0:sb0 + w], 1.0 / 1024.0, None, ALU.mult), inc="VE")
        p.op(V, lambda e, w=w, mro=mro, rro=rro: e.tensor_tensor(rro[:, 0:w], mro[:, 0:w], mro[:, 0:w], ALU.mult), inc="VE")
        p.op(V, lambda e, w=w, sb0=sb0, rro=rro: e.scalar_tensor_tensor(
            PSB[0:1, sb0 + 512:sb0 + 512 + w], PSB[0:1, sb0 + 512:sb0 + 512 + w], 1.0 / 1024.0,
            rro[:, 0:w], ALU.mult, ALU.subtract), inc="VE")
        p.op(V, lambda e, w=w, sb0=sb0: e.tensor_scalar(
            PSB[0:1, sb0 + 512:sb0 + 512 + w], PSB[0:1, sb0 + 512:sb0 + 512 + w], 1e-5, None, ALU.add), inc="VE")
        ve_var[j] = p.sems["VE"].n
        p.wait("scalar", "VE", ve_var[j])
        p.op("scalar", lambda e, w=w, sb0=sb0: e.activation(
            PSB[0:1, sb0 + 512:sb0 + 512 + w], PSB[0:1, sb0 + 512:sb0 + 512 + w], ACTF.Sqrt, bias=0.0, scale=1.0), inc="AC")
        ac_sqrt[j] = p.sems["AC"].n
        p.wait(V, "AC", ac_sqrt[j])
        p.op(V, lambda e, w=w, sb0=sb0, rro=rro: e.reciprocal(rro[:, 0:w], PSB[0:1, sb0 + 512:sb0 + 512 + w]), inc="VE")
        p.op(V, lambda e, w=w, mro=mro, rro=rro: e.tensor_tensor(mro[:, 0:w], mro[:, 0:w], rro[:, 0:w], ALU.mult), inc="VE")
        ve_recip[j] = p.sems["VE"].n
        p.wait("tensor", "VE", ve_recip[j])
        if j - 2 >= 0:
            p.wait("tensor", "VE", ve_xn[j - 2])
        p.op("tensor", lambda e, w=w, pa0=pa0, mro=mro: e.matmul(
            PSA[:, pa0:pa0 + w], onesrF[:, :], mro[:, 0:w], start=True, stop=True), inc=None)
        p.op("tensor", lambda e, w=w, pa0=pa0, rro=rro: e.matmul(
            PSA[:, pa0 + 512:pa0 + 512 + w], onesrF[:, :], rro[:, 0:w], start=True, stop=True), inc="MM")
        mm_bcast[j] = p.sems["MM"].n
        p.wait(V, "MM", mm_bcast[j])
        for k in range(KT):
            p.op(V, lambda e, k=k, c0=c0, w=w, pa0=pa0: e.tensor_tensor(
                xsc[:, 0:w], u_t[k][:, c0:c0 + w], PSA[:, pa0 + 512:pa0 + 512 + w], ALU.mult), inc="VE")
            p.op(V, lambda e, k=k, c0=c0, w=w, pa0=pa0: e.tensor_tensor(
                xn[:, k * TW + c0:k * TW + c0 + w], xsc[:, 0:w], PSA[:, pa0:pa0 + w], ALU.subtract), inc="VE")
        ve_xn[j] = p.sems["VE"].n
    p.barrier("VE")

    xbc = _carve(AR2, BF16, 0, 18 * TW)
    winbc = _carve(AR1, BF16, 0, KT * 2336)
    wst1 = _carve(AR1, F32, 37376, 2336)
    wst2 = _carve(AR1, F32, 46720, 2336)
    wtok, ve_wc = {}, {}

    def win_dma(k):
        if k >= KT:
            return
        if k - 2 >= 0:
            p.wait(S, "VE", ve_wc[k - 2])
        st = wst1 if k % 2 == 0 else wst2
        toks = []
        for h in range(2):
            p.dma(S, st[:, h * 1168:(h + 1) * 1168],
                  d_win.ap()[k * 128:(k + 1) * 128, 2048 + h * 1168:2048 + (h + 1) * 1168])
            toks.append(p.dclose("sync"))
        wtok[k] = toks

    p.wait(S, "VE")
    win_dma(0)
    win_dma(1)
    for k in range(KT):
        st = wst1 if k % 2 == 0 else wst2
        for tk_ in wtok[k]:
            p.ewait(V, tk_)
        p.op(V, lambda e, k=k, st=st: e.tensor_copy(winbc[:, k * 2336:(k + 1) * 2336], st), inc="VE")
        ve_wc[k] = p.sems["VE"].n
        win_dma(k + 2)
    p.wait("tensor", "VE", ve_wc[KT - 1])
    ve_slot = {}
    for mt in range(18):
        for j, (c0, w) in enumerate(ncs):
            sl = j % 4
            pv = PSA[:, sl * 512:sl * 512 + w]
            if sl in ve_slot:
                p.wait("tensor", "VE", ve_slot[sl])
            for k in range(KT):
                p.op("tensor", lambda e, k=k, mt=mt, c0=c0, w=w, pv=pv: e.matmul(
                    pv, winbc[:, k * 2336 + mt * 128:k * 2336 + mt * 128 + 128],
                    xn[:, k * TW + c0:k * TW + c0 + w],
                    start=(k == 0), stop=(k == KT - 1)), inc="MM" if k == KT - 1 else None)
            p.wait(V, "MM", p.sems["MM"].n)
            p.op(V, lambda e, mt=mt, c0=c0, w=w, pv=pv: e.tensor_copy(
                xbc[:, mt * TW + c0:mt * TW + c0 + w], pv), inc="VE")
            ve_slot[sl] = p.sems["VE"].n
    dtraw = PSB[0:32, 0:2048]
    for j, (c0, w) in enumerate([(3, 512), (515, 512), (1027, 512), (1539, 512)]):
        sl = j % 2
        pv = PSA[0:32, sl * 512:sl * 512 + w]
        if sl in ve_slot:
            p.wait("tensor", "VE", ve_slot[sl])
        for k in range(KT):
            p.op("tensor", lambda e, k=k, c0=c0, w=w, pv=pv: e.matmul(
                pv, winbc[:, k * 2336 + 2304:k * 2336 + 2336],
                xn[:, k * TW + c0:k * TW + c0 + w],
                start=(k == 0), stop=(k == KT - 1)), inc="MM" if k == KT - 1 else None)
        p.wait(V, "MM", p.sems["MM"].n)
        p.op(V, lambda e, j=j, c0=c0, w=w, pv=pv: e.tensor_copy(dtraw[:, c0 - 3:c0 - 3 + w], pv), inc="VE")
        ve_slot[sl] = p.sems["VE"].n
    p.barrier("VE", "MM")
    ab_t = PSA[0:32, 0:2048]
    p.op(V, lambda e: e.tensor_scalar(dt_s[:, :], dtraw, dtb_s[:, :], 0.0, ALU.add, ALU.max), inc="VE")
    p.wait("scalar", "VE")
    p.op("scalar", lambda e: e.activation(ab_t, dtraw, ACTF.Abs, bias=dtb_s[:, :], scale=1.0), inc="AC")
    p.op("scalar", lambda e: e.activation(ab_t, ab_t, ACTF.Exp, bias=0.0, scale=-1.0), inc="AC")
    p.wait(V, "AC")
    p.op(V, lambda e: e.tensor_scalar(ab_t, ab_t, 1.0, None, ALU.add), inc="VE")
    p.wait("scalar", "VE")
    p.op("scalar", lambda e: e.activation(ab_t, ab_t, ACTF.Ln, bias=0.0, scale=1.0), inc="AC")
    p.wait(V, "AC")
    p.op(V, lambda e: e.tensor_tensor(dt_s[:, :], dt_s[:, :], ab_t, ALU.add), inc="VE")
    p.op(V, lambda e: e.tensor_scalar(ab_t, dt_s[:, :], acol[:, :], None, ALU.mult), inc="VE")
    p.op(V, lambda e: e.tensor_tensor_scan(ga[:, :], ab_t, dt_s[:, :], 0.0, ALU.add, ALU.bypass), inc="VE")
    p.op(V, lambda e: e.tensor_copy(gaex[:, :], bass.AP(ga, 127, [[T, NH], [128, NCH]])), inc="VE")
    p.op(V, lambda e: e.memset(smallF[:, 0:NCH], 0.0), inc="VE")
    p.op(V, lambda e: e.tensor_copy(smallF[:, 1:NCH], gaex[:, 0:NCH - 1]), inc="VE")
    p.op(V, lambda e: e.tensor_tensor(cdee[:, :], gaex[:, :], smallF[:, 0:NCH], ALU.subtract), inc="VE")
    p.wait("scalar", "VE")
    p.op("scalar", lambda e: e.activation(cdee[:, :], cdee[:, :], ACTF.Exp, bias=0.0, scale=1.0), inc="AC")
    p.barrier("AC", "VE")

    zbuf1 = _carve(AR1, BF16, 51200, 2048)
    zbuf2 = _carve(AR1, BF16, 47104, 2048)
    zncs = [(3, 512), (515, 512), (1027, 512), (1539, 512)]
    winz1 = _carve(AR1, BF16, 0, KT * 1024)
    winz2 = _carve(AR1, BF16, 24576, KT * 1024)
    zst1 = _carve(AR1, F32, 16384, 1024)
    zst2 = _carve(AR1, F32, 20480, 1024)
    ztok, ve_zc = {}, {}

    def z_dma(kk):
        if kk >= 2 * KT:
            return
        if kk - 2 >= 0:
            p.wait(S, "VE", ve_zc[kk - 2])
        p.dma(S, zst1 if kk % 2 == 0 else zst2,
              d_win.ap()[(kk % 8) * 128:(kk % 8) * 128 + 128, (kk // 8) * 1024:(kk // 8) * 1024 + 1024])
        ztok[kk] = p.dclose("sync")

    p.wait(S, "MM")
    p.wait(S, "VE")
    z_dma(0)
    z_dma(1)
    for kk in range(2 * KT):
        st = zst1 if kk % 2 == 0 else zst2
        wz = winz1 if kk < 8 else winz2
        p.ewait(V, ztok[kk])
        if kk == 0:
            p.wait(V, "MM")
        p.op(V, lambda e, kk=kk, st=st, wz=wz: e.tensor_copy(wz[:, (kk % 8) * 1024:(kk % 8) * 1024 + 1024], st), inc="VE")
        ve_zc[kk] = p.sems["VE"].n
        z_dma(kk + 2)
    ac_zslot = {}
    zs_tok = {}
    for half in range(2):
        wz = winz1 if half == 0 else winz2
        p.wait("tensor", "VE", ve_zc[half * 8 + 7])
        for mi in range(8):
            mt = half * 8 + mi
            zb = zbuf1 if mt % 2 == 0 else zbuf2
            if mt - 2 in zs_tok:
                p.ewait("scalar", zs_tok[mt - 2])
            for j, (c0, w) in enumerate(zncs):
                sl = j % 4
                pv = PSA[:, sl * 512:sl * 512 + w]
                if sl in ac_zslot:
                    p.wait("tensor", "AC", ac_zslot[sl])
                for k in range(KT):
                    p.op("tensor", lambda e, k=k, mi=mi, c0=c0, w=w, pv=pv, wz=wz: e.matmul(
                        pv, wz[:, k * 1024 + mi * 128:k * 1024 + mi * 128 + 128],
                        xn[:, k * TW + c0:k * TW + c0 + w],
                        start=(k == 0), stop=(k == KT - 1)), inc="MM" if k == KT - 1 else None)
                p.wait("scalar", "MM", p.sems["MM"].n)
                p.op("scalar", lambda e, j=j, w=w, pv=pv, zb=zb: e.activation(
                    zb[:, j * 512:j * 512 + w], pv, ACTF.Silu, bias=0.0, scale=1.0), inc="AC")
                ac_zslot[sl] = p.sems["AC"].n
            p.wait(G, "AC", ac_zslot[3])
            p.dma(G, bass.AP(d_zsil, mt * 128 * C, [[C, 128], [16 * 128 * C, NCH], [1, C]]),
                  zb[:, :].rearrange("p (c l) -> p c l", c=NCH))
            zs_tok[mt] = p.dclose("gpsimd")
    p.barrier("GD", "MM", "AC", "VE")

    diag = _carve(AR1, BF16, 0, 72 * 128)
    xtbuf1 = _carve(AR1, BF16, 18432, 2048)
    xtbuf2 = _carve(AR1, BF16, 36352, 2048)
    cbRf = _carve(AR1, F32, 22528, 2304, parts=1)
    cbR = _carve(AR1, BF16, 31744, 2304, parts=1)
    p.dma(S, cbRf, d_convb.ap().unsqueeze(0))
    cb_tok = p.dclose("sync")
    p.ewait(V, cb_tok)
    p.wait(V, "MM")
    p.op(V, lambda e: e.tensor_copy(cbR, cbRf), inc="VE")
    for ct in range(18):
        for tk in range(4):
            p.op(V, lambda e, ct=ct, tk=tk: e.tensor_scalar(
                diag[:, (ct * 4 + tk) * 128:(ct * 4 + tk) * 128 + 128],
                identF[:, :], convw_s[:, ct * 4 + tk:ct * 4 + tk + 1], None, ALU.mult), inc="VE")
    ve_diag = p.sems["VE"].n
    p.wait("tensor", "VE", ve_diag)
    ac_cgrp = {}
    xt_dtok = {}
    ac_B = None
    for c in range(NCH):
        xtb = xtbuf1 if c % 2 == 0 else xtbuf2
        if c - 2 in xt_dtok:
            p.ewait("scalar", xt_dtok[c - 2])
        for grp in range(4):
            if grp in ac_cgrp:
                p.wait("tensor", "AC", ac_cgrp[grp])
            for j in range(4):
                ct = grp * 4 + j
                pvt = PSA[:, grp * 512 + j * 128:grp * 512 + j * 128 + 128]
                for tk in range(4):
                    p.op("tensor", lambda e, c=c, ct=ct, tk=tk, j=j, pvt=pvt: e.matmul(
                        pvt, xbc[:, ct * TW + c * 128 + tk:ct * TW + c * 128 + tk + 128],
                        diag[:, (ct * 4 + tk) * 128:(ct * 4 + tk) * 128 + 128],
                        start=(tk == 0 and j == 0), stop=False), inc=None)
                p.op("tensor", lambda e, ct=ct, pvt=pvt, j=j: e.matmul(
                    pvt, onesrB[:, :], cbR[:, ct * 128:ct * 128 + 128],
                    start=False, stop=(j == 3)), inc="MM" if j == 3 else None)
            p.wait("scalar", "MM", p.sems["MM"].n)
            p.op("scalar", lambda e, grp=grp, xtb=xtb: e.activation(
                xtb[:, grp * 512:grp * 512 + 512], PSA[:, grp * 512:grp * 512 + 512],
                ACTF.Silu, bias=0.0, scale=1.0), inc="AC")
            ac_cgrp[grp] = p.sems["AC"].n
        p.wait(G, "AC", ac_cgrp[3])
        p.dma(G, d_xtok.ap()[c], xtb[:, :])
        xt_dtok[c] = p.dclose("gpsimd")
        pvb = PSB[:, 0:128]
        if ac_B is not None:
            p.wait("tensor", "AC", ac_B)
        for tk in range(4):
            p.op("tensor", lambda e, c=c, tk=tk, pvb=pvb: e.matmul(
                pvb, xbc[:, 16 * TW + c * 128 + tk:16 * TW + c * 128 + tk + 128],
                diag[:, (64 + tk) * 128:(64 + tk) * 128 + 128],
                start=(tk == 0), stop=False), inc=None)
        p.op("tensor", lambda e, pvb=pvb: e.matmul(
            pvb, onesrB[:, :], cbR[:, 2048:2176], start=False, stop=True), inc="MM")
        p.wait("scalar", "MM", p.sems["MM"].n)
        p.op("scalar", lambda e, c=c, pvb=pvb: e.activation(
            Bt[:, c * C:(c + 1) * C], pvb, ACTF.Silu, bias=0.0, scale=1.0), inc="AC")
        ac_B = p.sems["AC"].n
    ac_f = None
    for ct in (16, 17):
        for j, (c0, w) in enumerate(zncs):
            sl = j % 2
            pv = PSB[:, 512 + sl * 512:512 + sl * 512 + w]
            if (16, sl) in ac_cgrp:
                p.wait("tensor", "AC", ac_cgrp[(16, sl)])
            for tk in range(4):
                p.op("tensor", lambda e, ct=ct, tk=tk, c0=c0, w=w, pv=pv: e.matmul(
                    pv, diag[:, (ct * 4 + tk) * 128:(ct * 4 + tk) * 128 + 128],
                    xbc[:, ct * TW + c0 - 3 + tk:ct * TW + c0 - 3 + tk + w],
                    start=(tk == 0), stop=(tk == 3)), inc="MM" if tk == 3 else None)
            p.wait("scalar", "MM", p.sems["MM"].n)
            dst = Bf if ct == 16 else Cf
            p.op("scalar", lambda e, ct=ct, c0=c0, w=w, pv=pv, dst=dst: e.activation(
                dst[:, c0 - 3:c0 - 3 + w], pv, ACTF.Silu, bias=convb_s[:, ct:ct + 1], scale=1.0), inc="AC")
            ac_cgrp[(16, sl)] = p.sems["AC"].n
    p.barrier("AC", "MM", "VE", "GD")

    xbc_mm, xbc_ve, xbc_ac = p.sems["MM"].n, p.sems["VE"].n, p.sems["AC"].n
    s_run = _carve(AR1, F32, 0, 2048)
    crow_all = _carve(AR2, F32, 70144, 512)
    rowAll = _carve(AR2, F32, 65536, 512, parts=1)
    p.wait("tensor", "VE")
    p.wait("tensor", "AC")
    for cc in range(16):
        p.op("tensor", lambda e, cc=cc: e.transpose(
            PSB[0:1, cc * 32:(cc + 1) * 32], cdee[:, cc:cc + 1], identF[0:32, 0:32]),
            inc="MM" if cc == 15 else None)
    p.wait(V, "MM")
    p.op(V, lambda e: e.tensor_copy(rowAll, PSB[0:1, 0:512]), inc="VE")
    p.wait("tensor", "VE")
    for cc in range(16):
        p.op("tensor", lambda e, cc=cc: e.matmul(
            PSB[:, cc * 32:(cc + 1) * 32], onesrF[:, :], rowAll[:, cc * 32:(cc + 1) * 32],
            start=(cc == 0), stop=(cc == 15)), inc="MM" if cc == 15 else None)
    p.wait(V, "MM")
    p.op(V, lambda e: e.tensor_copy(crow_all, PSB[:, 0:512]), inc="VE")
    p.wait("scalar", "MM")
    xts1 = _carve(AR1, BF16, 8192, 2048)
    xts2 = _carve(AR1, BF16, 12288, 2048)
    xtw = _carve(AR1, BF16, 16384, 2048)
    ssc = _carve(AR1, BF16, 20480, 2048)
    w2bf = _carve(AR1, BF16, 0, 32768)
    w2pst = [_carve(AR2, F32, i * 4096, 1024) for i in range(4)]
    w1pst = [_carve(AR2, F32, 16384 + i * 4096, 1024) for i in range(4)]
    w1pob = [_carve(AR2, BF16, 32768 + i * 2048, 1024) for i in range(4)]
    tok_w2, tok_w1i, tok_w1o = {}, {}, {}
    ve_w2c, ve_w1c = {}, {}
    p.wait(S, "MM", xbc_mm)
    p.wait(S, "VE", xbc_ve)
    p.wait(S, "AC", xbc_ac)
    p.wait("scalar", "MM", xbc_mm)
    p.ewait("scalar", zs_tok[14])
    p.ewait("scalar", zs_tok[15])
    p.ewait("scalar", xt_dtok[14])
    p.ewait("scalar", xt_dtok[15])

    def w1_in(j):
        if j > 31:
            return
        if j - 4 >= 0:
            p.wait(S, "AC", ve_w1c[j - 4])
        p.dma(S, w1pst[j % 4], d_w1.ap()[j].rearrange("p k c -> p (k c)"))
        tok_w1i[j] = p.dclose("sync")

    def w1_cast_out(j):
        if j < 0 or j > 31:
            return
        p.ewait("scalar", tok_w1i[j])
        if j - 4 >= 0:
            p.ewait("scalar", tok_w1o[j - 4])
        p.op("scalar", lambda e, j=j: e.activation(
            w1pob[j % 4], w1pst[j % 4], ACTF.Copy, bias=0.0, scale=1.0), inc="AC")
        ve_w1c[j] = p.sems["AC"].n
        p.wait(G, "AC", ve_w1c[j])
        p.dma(G, d_w1b.ap()[j], w1pob[j % 4])
        tok_w1o[j] = p.dclose("gpsimd")

    def w2_in(i):
        if i > 31:
            return
        if i - 4 >= 12:
            p.wait(S, "AC", ve_w2c[i - 4])
        p.dma(S, w2pst[(i - 12) % 4], d_w2.ap()[i * 128:(i + 1) * 128, :])
        tok_w2[i] = p.dclose("sync")

    def w2_cast(i):
        if i < 12 or i > 31:
            return
        p.ewait("scalar", tok_w2[i])
        p.op("scalar", lambda e, i=i: e.activation(
            w2bf[:, i * 1024:(i + 1) * 1024], w2pst[(i - 12) % 4], ACTF.Copy, bias=0.0, scale=1.0), inc="AC")
        ve_w2c[i] = p.sems["AC"].n

    p.op(V, lambda e: e.memset(s_run, 0.0), inc="VE")
    p.dclose("sync")
    p.dma(S, xts1, d_xtok.ap()[0])
    xt_tok = [p.dclose("sync")]
    smallFB = _carve(AR2, F32, 45056, 128, parts=32)
    smallF2B = _carve(AR2, F32, 45568, 128, parts=32)
    xtwB = _carve(AR2, BF16, 46080, 2048)
    ve_xtw, mm_tp, ve_sF, ac_sF, mm_st, ve_sr = {}, {}, {}, {}, {}, {}
    rel_tok, sall_tok = {}, {}

    def d_front(c):
        if c >= NCH:
            return
        sF = smallF if c % 2 == 0 else smallFB
        sF2 = smallF2 if c % 2 == 0 else smallF2B
        if c >= 2:
            p.ewait("scalar", rel_tok[c - 2])
            p.wait("scalar", "MM", mm_tp[c - 2])
        p.op("scalar", lambda e, c=c, sF=sF: e.activation(
            sF[:, :], ga[:, c * C:(c + 1) * C], ACTF.Exp, bias=gaex[:, c:c + 1], scale=-1.0), inc="AC")
        ac_sF[c] = p.sems["AC"].n
        p.wait(V, "AC", ac_sF[c])
        if c >= 2:
            p.wait(V, "MM", mm_tp[c - 2])
        p.op(V, lambda e, c=c, sF=sF, sF2=sF2: e.tensor_tensor(
            sF2[:, :], dt_s[:, c * C:(c + 1) * C], sF[:, :], ALU.mult), inc="VE")
        if c == 0:
            p.op(V, lambda e, c=c, sF=sF: e.tensor_scalar(
                sF[:, :], ga[:, c * C:(c + 1) * C], zcol[:, :], None, ALU.subtract), inc="VE")
        else:
            p.op(V, lambda e, c=c, sF=sF: e.tensor_scalar(
                sF[:, :], ga[:, c * C:(c + 1) * C], gaex[:, c - 1:c], None, ALU.subtract), inc="VE")
        ve_sF[c] = p.sems["VE"].n

    def d_trans(c):
        if c >= NCH:
            return
        sF = smallF if c % 2 == 0 else smallFB
        sF2 = smallF2 if c % 2 == 0 else smallF2B
        t0 = 512 if c % 2 == 0 else 608
        p.wait("tensor", "VE", ve_sF[c])
        if c >= 2:
            p.wait("tensor", "VE", ve_xtw[c - 2])
        p.op("tensor", lambda e, t0=t0, sF2=sF2: e.transpose(PSB[:, t0:t0 + 32], sF2[:, :], identF[0:32, 0:32]), inc=None)
        p.op("tensor", lambda e, t0=t0, sF=sF: e.transpose(PSB[:, t0 + 32:t0 + 64], sF[:, :], identF[0:32, 0:32]), inc=None)
        p.op("tensor", lambda e, c=c, t0=t0: e.transpose(PSB[:, t0 + 64:t0 + 96], dt_s[:, c * C:(c + 1) * C], identF[0:32, 0:32]), inc="MM")
        mm_tp[c] = p.sems["MM"].n

    def d_copies(c):
        if c >= NCH:
            return
        sF = smallF if c % 2 == 0 else smallFB
        xtw_c = xtw if c % 2 == 0 else xtwB
        xts = xts1 if c % 2 == 0 else xts2
        t0 = 512 if c % 2 == 0 else 608
        p.ewait(V, xt_tok[c])
        p.wait(V, "MM", mm_tp[c])
        if c >= 2:
            p.wait(V, "MM", mm_st[c - 2])
        p.op(V, lambda e, c=c, t0=t0: e.tensor_copy(rtok[:, c * NH:(c + 1) * NH], PSB[:, t0 + 32:t0 + 64]), inc="VE")
        p.op(V, lambda e, c=c, t0=t0: e.tensor_copy(dtok[:, c * NH:(c + 1) * NH], PSB[:, t0 + 64:t0 + 96]), inc="VE")
        p.op(V, lambda e, xts=xts, xtw_c=xtw_c, t0=t0: e.tensor_tensor(
            xtw_c.rearrange("p (h q) -> p h q", h=NH),
            xts.rearrange("p (h q) -> p h q", h=NH),
            PSB[:, t0:t0 + 32].unsqueeze(2).to_broadcast([128, NH, HD]), ALU.mult), inc="VE")
        ve_xtw[c] = p.sems["VE"].n
        p.wait(G, "VE", ve_xtw[c])
        p.dma(G, d_rel.ap()[c], sF[:, :])
        rel_tok[c] = p.dclose("gpsimd")

    def d_states(c):
        xtw_c = xtw if c % 2 == 0 else xtwB
        p.wait("tensor", "VE", ve_xtw[c])
        if c >= 1:
            p.wait("tensor", "VE", ve_sr[c - 1])
        for j in range(4):
            p.op("tensor", lambda e, c=c, j=j, xtw_c=xtw_c: e.matmul(
                PSA[:, j * 512:(j + 1) * 512], Bt[:, c * C:(c + 1) * C], xtw_c[:, j * 512:(j + 1) * 512],
                start=True, stop=True), inc="MM" if j == 3 else None)
        mm_st[c] = p.sems["MM"].n

    def d_tail(c):
        if c - 1 in sall_tok:
            p.ewait(V, sall_tok[c - 1])
        p.op(V, lambda e: e.tensor_copy(ssc, s_run), inc="VE")
        p.wait(G, "VE")
        p.dma(G, d_sall.ap()[c], ssc)
        sall_tok[c] = p.dclose("gpsimd")
        p.wait(V, "MM", mm_st[c])
        p.op(V, lambda e, c=c: e.tensor_tensor(
            s_run.rearrange("p (h q) -> p h q", h=NH),
            s_run.rearrange("p (h q) -> p h q", h=NH),
            crow_all[:, c * 32:(c + 1) * 32].unsqueeze(2).to_broadcast([128, NH, HD]), ALU.mult), inc="VE")
        p.op(V, lambda e: e.tensor_tensor(s_run, s_run, PSA[:, :], ALU.add), inc="VE")
        ve_sr[c] = p.sems["VE"].n

    d_front(0)
    d_trans(0)
    d_copies(0)
    for c in range(NCH):
        if c + 1 < NCH:
            if c >= 1:
                p.wait(S, "VE", ve_xtw[c - 1])
            p.dma(S, xts2 if c % 2 == 0 else xts1, d_xtok.ap()[c + 1])
            xt_tok.append(p.dclose("sync"))
        d_front(c + 1)
        d_states(c)
        d_trans(c + 1)
        d_copies(c + 1)
        d_tail(c)
        w1_in(2 * c)
        w1_in(2 * c + 1)
        w1_cast_out(2 * c - 2)
        w1_cast_out(2 * c - 1)
        w2_in(12 + 2 * c)
        w2_in(13 + 2 * c)
        w2_cast(10 + 2 * c)
        w2_cast(11 + 2 * c)
    w1_cast_out(30)
    w1_cast_out(31)
    p.barrier("VE")
    zrow = _carve(AR1, F32, 8192, 2016, parts=1)
    p.op(V, lambda e: e.memset(zrow, 0.0), inc="VE")
    p.barrier("VE")
    p.dma(G, d_ccin.ap()[0:128, :], s_run)
    p.dma(G, d_ccin.ap()[128:129, 0:32], ga[:, T - 1:T])
    p.dma(G, d_ccin.ap()[128:129, 32:2048], zrow)
    p.dwait(G, "gpsimd")
    p.op(G, lambda e: e.collective_compute(
        "AllGather", ALU.bypass, replica_groups=[list(range(8))],
        ins=[d_ccin.ap().opt()], outs=[d_ccout.ap().opt()]), inc="CC")

    ve_p1d, ac_p1d, mm_p1d = p.sems["VE"].n, p.sems["AC"].n, p.sems["MM"].n
    w2h = [_carve(AR2, F32, 40960 + i * 4096, 1024) for i in range(4)]
    p.wait(S, "MM", mm_p1d)
    p.ewait(S, rel_tok[15])
    p.dma_barrier("scalar")
    p.wait("scalar", "MM", mm_p1d)
    tok_w2h = {}
    for j in range(14):
        if j < 12:
            if j >= 4:
                p.wait(S, "AC", ve_w2c[j - 4])
            p.dma(S, w2h[j % 4], d_w2.ap()[j * 128:(j + 1) * 128, :])
            tok_w2h[j] = p.dclose("sync")
        if j >= 2:
            i = j - 2
            p.ewait("scalar", tok_w2h[i])
            p.op("scalar", lambda e, i=i: e.activation(
                w2bf[:, i * 1024:(i + 1) * 1024], w2h[i % 4], ACTF.Copy, bias=0.0, scale=1.0), inc="AC")
            ve_w2c[i] = p.sems["AC"].n
    ve_prepass = p.sems["AC"].n

    NPB = 5
    stgp = ([_carve(AR2, BF16, 8192 + i * 4096, 2048) for i in range(3)] +
            [_carve(AR2, BF16, 61440 + i * 4096, 2048) for i in range(2)])
    qbuf = [_carve(AR2, I32, 20480, 2048), _carve(AR2, I32, 0, 2048)]
    scA = _carve(AR2, F32, 28672, 2048)
    scB = _carve(AR2, F32, 36864, 2048)
    scC = _carve(AR2, F32, 45056, 2048)
    hgq = [_carve(AR2, BF16, 53248, 2048), _carve(AR2, BF16, 57344, 2048)]
    tok_stg, mm_after, evac_ac = {}, {}, {}
    ve_V1, ve_xhi, ve_y = {}, {}, {}
    ac_A1, ac_A3, ac_A5 = {}, {}, {}
    mm_w2_after, ve_obuf, tok_out = {}, {}, {}

    def issue_stage(gm):
        if gm >= 256 or gm % 2 != 0:
            return
        pg = gm // 2
        p.wait(S, "AC", ve_prepass)
        if pg - NPB >= 0:
            p.wait(S, "MM", mm_after[2 * (pg - NPB) + 1])
        if gm < 32:
            p.ewait(S, tok_w1o[gm])
            p.ewait(S, tok_w1o[gm + 1])
        m0 = gm % 32
        p.dma(S, stgp[pg % NPB].rearrange("p (m c) -> p m c", m=2),
              bass.AP(d_w1b, m0 * 131072, [[1024, 128], [131072, 2], [1, 1024]]))
        tok_stg[pg] = p.dclose("sync")

    def emit_chain1(ms, h, q, qo, hg):
        c0, c1 = h * 1024, (h + 1) * 1024
        p.op("scalar", lambda e, q=q, qo=qo, c0=c0, c1=c1: e.activation(
            qo[:, c0:c1], q[:, c0:c1], ACTF.Identity, bias=c155[:, 0:1], scale=1.0 / 64.0), inc="AC")
        ac_A1[(ms, h)] = p.sems["AC"].n
        p.wait(V, "AC", ac_A1[(ms, h)])
        p.op(V, lambda e, qo=qo, c0=c0, c1=c1: e.tensor_copy(scA[:, c0:c1], qo[:, c0:c1]), inc="VE")
        ve_V1[(ms, h)] = p.sems["VE"].n
        p.op(V, lambda e, q=q, c0=c0, c1=c1: e.scalar_tensor_tensor(
            scB[:, c0:c1], q[:, c0:c1], 1.0 / 64.0, scA[:, c0:c1], ALU.mult, ALU.subtract), inc="VE")
        if ms % 4 == 0 and ms >= 4:
            p.ewait(V, tok_out[ms // 4 - 1][h])
        p.op(V, lambda e, c0=c0, c1=c1: e.tensor_scalar(scC[:, c0:c1], scA[:, c0:c1], 0.25, None, ALU.mult), inc="VE")
        p.op(V, lambda e, c0=c0, c1=c1: e.tensor_scalar(scA[:, c0:c1], scC[:, c0:c1], 0.25, None, ALU.add), inc="VE")
        ve_xhi[(ms, h)] = p.sems["VE"].n

    def emit_chain2(ms, h):
        c0, c1 = h * 1024, (h + 1) * 1024
        if ms % 4 == 0 and ms >= 4:
            p.ewait("scalar", tok_out[ms // 4 - 1][h])
        p.wait("scalar", "VE", ve_xhi[(ms, h)])
        p.op("scalar", lambda e, c0=c0, c1=c1: e.activation(
            scC[:, c0:c1], scC[:, c0:c1], ACTF.Gelu, bias=0.0, scale=1.0), inc="AC")
        p.op("scalar", lambda e, c0=c0, c1=c1: e.activation(
            scA[:, c0:c1], scA[:, c0:c1], ACTF.Gelu, bias=0.0, scale=1.0), inc="AC")
        ac_A3[(ms, h)] = p.sems["AC"].n
        p.wait(V, "AC", ac_A3[(ms, h)])
        p.op(V, lambda e, c0=c0, c1=c1: e.tensor_tensor(scA[:, c0:c1], scA[:, c0:c1], scC[:, c0:c1], ALU.subtract), inc="VE")
        p.op(V, lambda e, c0=c0, c1=c1: e.tensor_tensor(scB[:, c0:c1], scB[:, c0:c1], scA[:, c0:c1], ALU.mult), inc="VE")
        p.op(V, lambda e, c0=c0, c1=c1: e.tensor_tensor(scC[:, c0:c1], scC[:, c0:c1], scB[:, c0:c1], ALU.add), inc="VE")
        ve_y[(ms, h)] = p.sems["VE"].n

    def emit_chain3(ms, h, q, hg):
        c0, c1 = h * 1024, (h + 1) * 1024
        p.wait("scalar", "VE", ve_y[(ms, h)])
        p.op("scalar", lambda e, q=q, c0=c0, c1=c1: e.activation(
            q[:, c0:c1], scC[:, c0:c1], ACTF.Copy, bias=0.0, scale=256.0), inc="AC")
        if h == 0 and ms - 2 >= 0:
            p.wait("scalar", "MM", mm_w2_after[ms - 2])
        p.op("scalar", lambda e, q=q, hg=hg, c0=c0, c1=c1: e.activation(
            hg[:, c0:c1], q[:, c0:c1], ACTF.Copy, bias=0.0, scale=1.0), inc="AC")
        ac_A5[(ms, h)] = p.sems["AC"].n

    def emit_w2_half(pend_w2, half):
        ms, hg = pend_w2
        msub = ms % 4
        if half == 0:
            p.wait("tensor", "AC", ve_w2c[msub * 8 + 7])
        p.wait("tensor", "AC", ac_A5[(ms, half)])
        for m8w in range(4 * half, 4 * half + 4):
            mw = msub * 8 + m8w
            for m2 in range(8):
                stt = (msub == 0 and m8w == 0 and m2 % 2 == 0)
                sp = (msub == 3 and m8w == 7 and m2 % 2 == 1)
                last = (m8w == 7 and m2 == 7)
                p.op("tensor", lambda e, mw=mw, m2=m2, hg=hg, m8w=m8w, stt=stt, sp=sp: e.matmul(
                    PSA[:, m2 * 256:(m2 + 1) * 256],
                    w2bf[:, mw * 1024 + m2 * 128:mw * 1024 + m2 * 128 + 128],
                    hg[:, m8w * 256:(m8w + 1) * 256],
                    start=stt, stop=sp), inc="MM" if last else None)
        if half == 1:
            mm_w2_after[ms] = p.sems["MM"].n

    p.wait("tensor", "VE", ve_p1d)
    p.wait("tensor", "AC", ac_p1d)
    for gm0 in range(0, 8, 2):
        issue_stage(gm0)
    pend_w2 = None
    pend = None
    for j in (28, 29, 30, 31):
        p.ewait(V, tok_w1o[j])
    for g in range(8):
        gc0 = 3 + g * 256
        for msub in range(4):
            ms = g * 4 + msub
            par = msub % 2
            q, qo, hg = qbuf[par], qbuf[1 - par], hgq[ms % 2]
            for m8 in range(8):
                m = msub * 8 + m8
                gm = g * 32 + m
                issue_stage(gm + 8)
                bank = m8 % 4
                p.ewait("tensor", tok_stg[gm // 2])
                if bank in evac_ac:
                    p.wait("tensor", "AC", evac_ac[bank])
                pv = PSB[:, bank * 512:bank * 512 + 256]
                sg = stgp[(gm // 2) % NPB]
                sc0 = (gm % 2) * 1024
                for k in range(KT):
                    p.op("tensor", lambda e, k=k, sg=sg, pv=pv, gc0=gc0, sc0=sc0: e.matmul(
                        pv, sg[:, sc0 + k * 128:sc0 + (k + 1) * 128], xn[:, k * TW + gc0:k * TW + gc0 + 256],
                        start=(k == 0), stop=(k == KT - 1)), inc="MM" if k == KT - 1 else None)
                mm_after[gm] = p.sems["MM"].n
                if m8 in (0, 4):
                    h = m8 // 4
                    if msub == 0:
                        if g > 0:
                            p.wait("scalar", "VE", ve_obuf[g - 1])
                    else:
                        p.wait("scalar", "VE", ve_V1[(ms - 1, h)])
                p.wait("scalar", "MM", mm_after[gm])
                p.op("scalar", lambda e, m=m, m8=m8, pv=pv, q=q: e.activation(
                    q[:, m8 * 256:m8 * 256 + 256], pv, ACTF.Identity, bias=b1c[:, m:m + 1], scale=256.0), inc="AC")
                evac_ac[bank] = p.sems["AC"].n
                if m8 == 0 and pend is not None:
                    emit_chain2(pend[0], 0)
                if m8 == 1 and pend is not None:
                    emit_chain2(pend[0], 1)
                if m8 == 2 and pend is not None:
                    emit_chain3(pend[0], 0, pend[1], pend[3])
                if m8 == 4:
                    emit_chain1(ms, 0, q, qo, hg)
                if m8 == 3 and pend is not None:
                    emit_chain3(pend[0], 1, pend[1], pend[3])
                    pend = None
            emit_chain1(ms, 1, q, qo, hg)
            if pend_w2 is not None:
                emit_w2_half(pend_w2, 0)
                emit_w2_half(pend_w2, 1)
                pend_w2 = None
            if msub == 3:
                emit_chain2(ms, 0)
                emit_chain2(ms, 1)
                emit_chain3(ms, 0, q, hg)
                emit_chain3(ms, 1, q, hg)
            else:
                pend = (ms, q, qo, hg)
            pend_w2 = (ms, hg)
        emit_w2_half(pend_w2, 0)
        emit_w2_half(pend_w2, 1)
        pend_w2 = None
        p.wait("scalar", "MM", mm_w2_after[g * 4 + 3])
        tok_out[g] = []
        for hf in range(2):
            for m2 in range(4 * hf, 4 * hf + 4):
                p.op("scalar", lambda e, m2=m2: e.activation(
                    qbuf[0][:, m2 * 256:m2 * 256 + 256], PSA[:, m2 * 256:(m2 + 1) * 256],
                    ACTF.Identity, bias=b2c[:, m2:m2 + 1], scale=1.0), inc="AC")
            p.wait(V, "AC", p.sems["AC"].n)
            p.op(V, lambda e, hf=hf: e.tensor_scalar(
                scC[:, hf * 1024:(hf + 1) * 1024], qbuf[0][:, hf * 1024:(hf + 1) * 1024],
                1.0 / 256.0, None, ALU.mult), inc="VE")
            p.wait(G, "VE", p.sems["VE"].n)
            p.dma(G, bass.AP(d_out, g * 256 + hf * 4 * 128 * T, [[T, 128], [128 * T, 4], [1, 256]]),
                  scC[:, hf * 1024:(hf + 1) * 1024].rearrange("p (m l) -> p m l", m=4))
            tok_out[g].append(p.dclose("gpsimd"))
        ve_obuf[g] = p.sems["VE"].n
    p.barrier("GD", "MM", "AC", "VE", "GP")


    p.barrier("CC", "GD", "LD", "VE", "MM", "AC", "GP")
    scb8 = _carve(AR2, F32, 37376, 2048, parts=8)
    srs = _carve(AR1, F32, 0, 2048)
    scbc = _carve(AR1, F32, 8192, 2048)
    tmp2 = _carve(AR1, F32, 16384, 2048)
    s_i2 = _carve(AR2, F32, 8704, 2048)
    s_i2b = _carve(AR2, BF16, 16896, 2048)
    dii = _carve(AR2, BF16, 512, NH * C)
    p.op(V, lambda e: e.tensor_tensor(
        dii.rearrange("p (h c) -> p h c", h=NH),
        identF[:, :].unsqueeze(1).to_broadcast([128, NH, 128]),
        db_s[:, :].unsqueeze(2).to_broadcast([128, NH, 128]), ALU.mult), inc="VE")
    p.dma(S, gls[:, :], bass.AP(d_ccout, 128 * INTER, [[129 * INTER, 8], [1, 32]]))
    p.barrier("LD")
    p.ewait("tensor", p.qstate["sync"]["last"])
    p.op("tensor", lambda e: e.matmul(PSB[0:8, 0:32], btw_s[:, :], gls[:, :], start=True, stop=True), inc="MM")
    p.wait("scalar", "MM")
    p.op("scalar", lambda e: e.activation(esc[:, :], PSB[0:8, 0:32], ACTF.Exp, bias=0.0, scale=1.0), inc="AC")
    p.wait(V, "AC")
    p.op(V, lambda e: e.tensor_scalar(esc[:, :], esc[:, :], sel_s[:, :], None, ALU.mult), inc="VE")
    p.op(V, lambda e: e.tensor_copy(
        scb8.rearrange("r (h q) -> r h q", h=NH),
        esc[:, :].unsqueeze(2).to_broadcast([8, NH, HD])), inc="VE")
    p.wait(G, "VE")
    p.dma(G, d_scb.ap(), scb8)
    p.dwait(S, "gpsimd")
    p.op(V, lambda e: e.memset(s_i2, 0.0), inc="VE")
    srs2 = _carve(AR1, F32, 24576, 2048)
    scbc2 = _carve(AR1, F32, 32768, 2048)
    ri_tok = {}
    ve_rr = {}
    for r in range(9):
        if r < 8:
            sb_s = srs if r % 2 == 0 else srs2
            sb_c = scbc if r % 2 == 0 else scbc2
            if r >= 2:
                p.wait(S, "VE", ve_rr[r - 2])
            p.dma(S, sb_s, d_ccout.ap()[129 * r:129 * r + 128, :])
            p.dma(S, sb_c, d_scb.ap()[r].unsqueeze(0).unsqueeze(1).to_broadcast([1, 128, INTER]))
            ri_tok[r] = p.dclose("sync")
        if r >= 1:
            rr = r - 1
            sb_s = srs if rr % 2 == 0 else srs2
            sb_c = scbc if rr % 2 == 0 else scbc2
            p.ewait(V, ri_tok[rr])
            p.op(V, lambda e, sb_s=sb_s, sb_c=sb_c: e.tensor_tensor(tmp2, sb_s, sb_c, ALU.mult), inc="VE")
            p.op(V, lambda e: e.tensor_tensor(s_i2, s_i2, tmp2, ALU.add), inc="VE")
            ve_rr[rr] = p.sems["VE"].n

    xts1p = _carve(AR1, BF16, 0, 2048)
    xts2p = _carve(AR1, BF16, 4096, 2048)
    sl1 = _carve(AR1, BF16, 8192, 2048)
    sl2 = _carve(AR1, BF16, 12288, 2048)
    zs1 = _carve(AR1, BF16, 16384, 2048)
    zs2 = _carve(AR1, BF16, 20480, 2048)
    relb = _carve(AR1, F32, 24576, 4096)
    bufA = _carve(AR1, F32, 40960, 4096)
    wobf1 = _carve(AR1, BF16, 57344, 2048)
    wobf2 = _carve(AR1, BF16, 61440, 2048)
    bufB = _carve(AR2, F32, 20992, 4096)
    mt_b = _carve(AR2, BF16, 37376, 4096)
    ct_b = _carve(AR2, BF16, 45568, 4096)
    xtq = _carve(AR2, BF16, 53760, 2048)
    graw = _carve(AR2, F32, 57856, 2048)
    g2b = _carve(AR2, BF16, 66048, 2048)
    scm = _carve(AR2, F32, 0, 128)
    ggrp = _carve(xn, BF16, 0, 8192)
    wostg = _carve(xn, F32, 16384, 2048)
    wos3 = _carve(xn, BF16, 16384, 2048)
    wos4 = _carve(xn, BF16, 20480, 2048)
    obuf2 = _carve(xn, F32, 24576, 2048)

    p.barrier("VE", "GD", "LD")
    p.dclose("sync")
    p.dclose("gpsimd")
    st_tok, rl_tok = {}, {}
    wi_tok, wb_tok, ve_wfold = {}, {}, {}
    mm_scores, mm_y = {}, {}
    ac_expg, ve_bufA, ac_bufB, ac_L, ac_si2b, ac_g2b, ve_dec = {}, {}, {}, {}, {}, {}, {}
    ve_bufA_h, ac_expg_h, ve_mt_h, ve_ct_h, wo_pre = {}, {}, {}, {}, {}
    ve_scm, ve_ctb, ve_graw, ve_g2b = {}, {}, {}, {}
    mm_sumsq, ac_rsq, ve_rrec, mm_rsb, ve_ggrp = {}, {}, {}, {}, {}
    ve_oc_slot, mm_wo_slot = {}, {}
    mm_op_last = [None]
    out_tok = [None, None]

    def streams(c):
        if c >= NCH:
            return
        if c >= 2:
            p.wait(S, "MM", mm_y[c - 2])
            p.wait(S, "VE", ve_graw[c - 2])
        p.dma(S, xts1p if c % 2 == 0 else xts2p, d_xtok.ap()[c])
        p.dma(S, sl1 if c % 2 == 0 else sl2, d_sall.ap()[c])
        p.dma(S, zs1 if c % 2 == 0 else zs2,
              bass.AP(d_zsil, c * 16 * 128 * C, [[C, 128], [128 * C, 16], [1, C]]))
        st_tok[c] = p.dclose("sync")

    def relb_pre(c):
        if c >= NCH:
            return
        rl_tok[c] = []
        if c >= 1:
            p.wait(S, "VE", ve_bufA[c - 1])
            p.wait(S, "AC", ac_expg[c - 1])
        p.dma(S, relb[:, 0:2048],
              d_rel.ap()[c][0:2048].unsqueeze(0).unsqueeze(1).to_broadcast([1, 128, 2048]))
        rl_tok[c].append(p.dclose("sync"))
        if c >= 1:
            p.wait(G, "VE", ve_bufA[c - 1])
            p.wait(G, "AC", ac_expg[c - 1])
        p.dma(G, relb[:, 2048:4096],
              d_rel.ap()[c][2048:4096].unsqueeze(0).unsqueeze(1).to_broadcast([1, 128, 2048]))
        rl_tok[c].append(p.dclose("gpsimd"))

    def emitF1(c):
        if c >= NCH:
            return
        for tk in rl_tok[c]:
            p.ewait(V, tk)
        if c >= 1:
            p.wait(V, "AC", ac_L[(c - 1, 1)])
        for hh in range(2):
            hc = hh * 2048
            p.op(V, lambda e, c=c, hh=hh, hc=hc: e.tensor_tensor(
                bufA[:, hc:hc + 2048].rearrange("p (h l) -> p h l", h=16),
                rtok[:, c * NH + hh * 16:c * NH + hh * 16 + 16].unsqueeze(2).to_broadcast([128, 16, C]),
                relb[:, hc:hc + 2048].rearrange("p (h l) -> p h l", h=16), ALU.subtract), inc="VE")
            ve_bufA_h[(c, hh)] = p.sems["VE"].n
        ve_bufA[c] = p.sems["VE"].n
        p.wait(V, "MM", mm_scores[c])
        p.op(V, lambda e: e.tensor_tensor(scm, PSB[:, 0:128], maskle[:, :], ALU.mult), inc="VE")
        ve_scm[c] = p.sems["VE"].n

    def emitAC(c):
        if c >= NCH:
            return
        for tk in rl_tok[c]:
            p.ewait("scalar", tk)
        for hh in range(2):
            hc = hh * 2048
            if c >= 1:
                p.wait("scalar", "VE", ve_ct_h[(c - 1, hh)])
            p.op("scalar", lambda e, hc=hc: e.activation(
                bufB[:, hc:hc + 2048], relb[:, hc:hc + 2048], ACTF.Exp, bias=0.0, scale=1.0), inc="AC")
            ac_expg_h[(c, hh)] = p.sems["AC"].n
            p.wait("scalar", "VE", ve_bufA_h[(c, hh)])
            p.op("scalar", lambda e, hc=hc: e.activation(
                bufA[:, hc:hc + 2048], bufA[:, hc:hc + 2048], ACTF.Relu, bias=0.0, scale=1.0), inc="AC")
            p.op("scalar", lambda e, hc=hc: e.activation(
                bufA[:, hc:hc + 2048], bufA[:, hc:hc + 2048], ACTF.Exp, bias=0.0, scale=-1.0), inc="AC")
            ac_L[(c, hh)] = p.sems["AC"].n
        ac_expg[c] = p.sems["AC"].n

    def emitF2(c):
        if c >= NCH:
            return
        p.ewait(V, st_tok[c])
        if c >= 1:
            p.wait(V, "MM", mm_y[c - 1])
            p.wait("scalar", "MM", mm_y[c - 1])
            p.wait("scalar", "VE", ve_dec[c - 1])
        p.op("scalar", lambda e: e.activation(s_i2b, s_i2, ACTF.Copy, bias=0.0, scale=1.0), inc="AC")
        ac_si2b[c] = p.sems["AC"].n
        xts = xts1p if c % 2 == 0 else xts2p
        p.op(V, lambda e, c=c, xts=xts: e.tensor_tensor(
            xtq.rearrange("p (h q) -> p h q", h=NH),
            xts.rearrange("p (h q) -> p h q", h=NH),
            dtok[:, c * NH:(c + 1) * NH].unsqueeze(2).to_broadcast([128, NH, HD]), ALU.mult), inc="VE")
        if c + 1 < NCH:
            p.wait(V, "AC", ac_si2b[c])
            p.op(V, lambda e, c=c: e.tensor_tensor(
                s_i2.rearrange("p (h q) -> p h q", h=NH),
                s_i2.rearrange("p (h q) -> p h q", h=NH),
                crow_all[:, c * 32:(c + 1) * 32].unsqueeze(2).to_broadcast([128, NH, HD]), ALU.mult), inc="VE")
        ve_dec[c] = p.sems["VE"].n

    def emitMT(c):
        for hh in range(2):
            hc = hh * 2048
            p.wait(V, "AC", ac_expg_h[(c, hh)])
            p.op(V, lambda e, c=c, hc=hc: e.tensor_tensor(
                ct_b[:, hc:hc + 2048].rearrange("p (h l) -> p h l", h=16),
                Cf[:, c * C:(c + 1) * C].unsqueeze(1).to_broadcast([128, 16, C]),
                bufB[:, hc:hc + 2048].rearrange("p (h l) -> p h l", h=16), ALU.mult), inc="VE")
            ve_ct_h[(c, hh)] = p.sems["VE"].n
            p.wait(V, "AC", ac_L[(c, hh)])
            p.op(V, lambda e, hc=hc: e.tensor_tensor(
                mt_b[:, hc:hc + 2048].rearrange("p (h l) -> p h l", h=16),
                bufA[:, hc:hc + 2048].rearrange("p (h l) -> p h l", h=16),
                scm.unsqueeze(1).to_broadcast([128, 16, C]), ALU.mult), inc="VE")
            ve_mt_h[(c, hh)] = p.sems["VE"].n
        ve_ctb[c] = p.sems["VE"].n

    def emitScores(c):
        if c >= NCH:
            return
        if c >= 1:
            p.wait("tensor", "VE", ve_scm[c - 1])
        else:
            p.wait("tensor", "VE")
            p.wait("tensor", "AC")
        p.op("tensor", lambda e, c=c: e.matmul(
            PSB[:, 0:128], Bf[:, c * C:(c + 1) * C], Cf[:, c * C:(c + 1) * C],
            start=True, stop=True), inc="MM")
        mm_scores[c] = p.sems["MM"].n

    def emitY(c):
        slc = sl1 if c % 2 == 0 else sl2
        xts = xts1p if c % 2 == 0 else xts2p
        p.wait("tensor", "VE", ve_mt_h[(c, 0)])
        p.wait("tensor", "AC", ac_si2b[c])
        for t in range(16):
            if t == 8:
                p.wait("tensor", "VE", ve_mt_h[(c, 1)])
            bank = t // 4
            col0 = bank * 512 + (t % 4) * 128
            for hh in range(2):
                h = 2 * t + hh
                pvh = PSA[hh * 64:hh * 64 + 64, col0:col0 + 128]
                stt = (t % 4 == 0)
                sp = (t % 4 == 3)
                tp = (0, hh * 64)
                p.op("tensor", lambda e, h=h, pvh=pvh, stt=stt, tp=tp: e.matmul(
                    pvh, xtq[:, h * HD:(h + 1) * HD], mt_b[:, h * C:(h + 1) * C],
                    start=stt, stop=False, tile_position=tp, skip_group_check=True), inc=None)
                p.op("tensor", lambda e, h=h, pvh=pvh, slc=slc, tp=tp: e.matmul(
                    pvh, slc[:, h * HD:(h + 1) * HD], ct_b[:, h * C:(h + 1) * C],
                    start=False, stop=False, tile_position=tp, skip_group_check=True), inc=None)
                p.op("tensor", lambda e, h=h, pvh=pvh, tp=tp: e.matmul(
                    pvh, s_i2b[:, h * HD:(h + 1) * HD], ct_b[:, h * C:(h + 1) * C],
                    start=False, stop=False, tile_position=tp, skip_group_check=True), inc=None)
                p.op("tensor", lambda e, h=h, pvh=pvh, xts=xts, sp=sp, tp=tp: e.matmul(
                    pvh, xts[:, h * HD:(h + 1) * HD], dii[:, h * C:(h + 1) * C],
                    start=False, stop=sp, tile_position=tp, skip_group_check=True), inc="MM" if (sp and hh == 1) else None)
        mm_y[c] = p.sems["MM"].n

    def emitG(c):
        zsc = zs1 if c % 2 == 0 else zs2
        p.wait(V, "MM", mm_y[c])
        for bank in range(4):
            p.op(V, lambda e, bank=bank, zsc=zsc: e.tensor_tensor(
                graw[:, bank * 512:(bank + 1) * 512], PSA[:, bank * 512:(bank + 1) * 512],
                zsc[:, bank * 512:(bank + 1) * 512], ALU.mult), inc="VE")
        ve_graw[c] = p.sems["VE"].n
        p.op(V, lambda e: e.tensor_tensor(g2b, graw, graw, ALU.mult), inc="VE")
        ve_g2b[c] = p.sems["VE"].n

    def emitSumsq(c):
        p.wait("tensor", "VE", ve_g2b[c])
        if c >= 1:
            p.wait("tensor", "AC", ac_rsq[c - 1])
        for tt in range(16):
            p.op("tensor", lambda e, tt=tt: e.matmul(
                PSB[0:1, 672:800], onesb[:, :], g2b[:, tt * 128:(tt + 1) * 128],
                start=(tt == 0), stop=(tt == 15)), inc="MM" if tt == 15 else None)
        mm_sumsq[c] = p.sems["MM"].n
        p.wait("scalar", "MM", mm_sumsq[c])
        p.op("scalar", lambda e: e.activation(
            PSB[0:1, 672:800], PSB[0:1, 672:800], ACTF.Sqrt, bias=c1e5[0:1, 0:1], scale=1.0 / 2048.0), inc="AC")
        ac_rsq[c] = p.sems["AC"].n

    def emitTail(c):
        p.wait(V, "AC", ac_rsq[c])
        p.op(V, lambda e: e.reciprocal(rrow[:, :], PSB[0:1, 672:800]), inc="VE")
        ve_rrec[c] = p.sems["VE"].n
        p.wait("tensor", "VE", ve_rrec[c])
        p.op("tensor", lambda e: e.matmul(PSB[:, 800:928], onesrF[:, :], rrow[:, :], start=True, stop=True), inc="MM")
        mm_rsb[c] = p.sems["MM"].n
        p.wait(V, "MM", mm_rsb[c])
        if mm_op_last[0] is not None:
            p.wait(V, "MM", mm_op_last[0])
        p.op(V, lambda e, c=c: e.tensor_tensor(
            ggrp.rearrange("p (t g l) -> p t g l", t=16, g=4)[:, :, c % 4, :],
            graw.rearrange("p (t l) -> p t l", t=16),
            PSB[:, 800:928].unsqueeze(1).to_broadcast([128, 16, 128]), ALU.mult), inc="VE")
        ve_ggrp[c] = p.sems["VE"].n

    def bounce(c):
        if c >= 4:
            return
        for mi in (2 * c, 2 * c + 1):
            inb = wostg if mi % 2 == 0 else obuf2
            outb = wobf1 if mi % 2 == 0 else wobf2
            p.ewait(V, wi_tok[mi])
            if mi - 2 >= 0:
                p.ewait(V, wb_tok[mi - 2])
            p.op(V, lambda e, inb=inb, outb=outb: e.tensor_tensor(
                outb.rearrange("p (k c2) -> p k c2", k=16),
                inb.rearrange("p (k c2) -> p k c2", k=16),
                nw_s[:, :].unsqueeze(2).to_broadcast([128, 16, 128]), ALU.mult), inc="VE")
            ve_wfold[mi] = p.sems["VE"].n
            p.wait(G, "VE", ve_wfold[mi])
            p.dma(G, d_woutb.ap()[mi], outb)
            wb_tok[mi] = p.dclose("gpsimd")
        for mi in (2 * c + 2, 2 * c + 3):
            if mi < 8:
                p.wait(S, "VE", ve_wfold[mi - 2])
                p.dma(S, wostg if mi % 2 == 0 else obuf2, d_wout.ap()[mi].rearrange("p k c -> p (k c)"))
                wi_tok[mi] = p.dclose("sync")

    def wo_dma_issue(gidx, m, wo_tok, mm_op):
        if m >= 8:
            return
        slot = m % 4
        if gidx == 0:
            p.ewait(S, wb_tok[6])
            p.ewait(S, wb_tok[7])
            p.ewait(S, wb_tok[m])
            if slot >= 2:
                p.wait(S, "VE", ve_wfold[6])
                p.wait(G, "VE", ve_wfold[6])
        if m - 4 in mm_op:
            p.wait(S, "MM", mm_op[m - 4])
        elif slot in mm_wo_slot:
            p.wait(S, "MM", mm_wo_slot[slot])
        wob = WOBS[slot]
        if slot % 2 == 1:
            if m - 4 in mm_op:
                p.wait(G, "MM", mm_op[m - 4])
            elif slot in mm_wo_slot:
                p.wait(G, "MM", mm_wo_slot[slot])
            p.dma(G, wob, d_woutb.ap()[m])
            wo_tok[m] = [p.dclose("gpsimd")]
        else:
            p.dma(S, wob, d_woutb.ap()[m])
            wo_tok[m] = [p.dclose("sync")]

    def emitOutproj(c):
        gidx = c // 4
        mm_op = {}
        if ("tok", gidx) in wo_pre:
            wo_tok = wo_pre.pop(("tok", gidx))
        else:
            wo_tok = {}
            for mq in range(4):
                wo_dma_issue(gidx, mq, wo_tok, mm_op)

        def wo_dma(m):
            wo_dma_issue(gidx, m, wo_tok, mm_op)
        for m in range(8):
            wob = WOBS[m % 4]
            for tk in wo_tok[m]:
                p.ewait("tensor", tk)
            if m == 0:
                p.wait("tensor", "VE", ve_ggrp[c])
            if (m % 2) in ve_oc_slot:
                p.wait("tensor", "VE", ve_oc_slot[m % 2])
            pv = PSB[:, 1024 + (m % 2) * 512:1024 + (m % 2) * 512 + 512]
            for k in range(16):
                p.op("tensor", lambda e, k=k, pv=pv, wob=wob: e.matmul(
                    pv, wob[:, k * 128:(k + 1) * 128], ggrp[:, k * 512:(k + 1) * 512],
                    start=(k == 0), stop=(k == 15)), inc="MM" if k == 15 else None)
            mm_op[m] = p.sems["MM"].n
            mm_wo_slot[m % 4] = mm_op[m]
            wo_dma(m + 4)
            p.wait(V, "MM", mm_op[m])
            if m == 0:
                p.ewait(V, out_tok[1])
            if m == 4:
                p.ewait(V, out_tok[0])
            p.op(V, lambda e, m=m, pv=pv: e.tensor_copy(obuf2[:, (m % 4) * 512:(m % 4) * 512 + 512], pv), inc="VE")
            ve_oc_slot[m % 2] = p.sems["VE"].n
            if m % 4 == 3:
                p.wait(G, "VE", ve_oc_slot[m % 2])
                p.dma(G, bass.AP(d_out, gidx * 512 + (m // 4) * 4 * 128 * T, [[T, 128], [128 * T, 4], [1, 512]]),
                      obuf2.rearrange("p (m l) -> p m l", m=4), accum=True)
                out_tok[m // 4] = p.dclose("gpsimd")
        mm_op_last[0] = mm_op[7]

    WOBS = [wobf1, wobf2, wos3, wos4]
    streams(0)
    streams(1)
    relb_pre(0)
    for mi in range(2):
        p.dma(S, wostg if mi % 2 == 0 else obuf2, d_wout.ap()[mi].rearrange("p k c -> p (k c)"))
        wi_tok[mi] = p.dclose("sync")
    emitScores(0)
    emitF1(0)
    emitAC(0)
    relb_pre(1)
    emitF2(0)
    for c in range(NCH):
        bounce(c)
        emitMT(c)
        emitScores(c + 1)
        emitY(c)
        emitF1(c + 1)
        emitAC(c + 1)
        relb_pre(c + 2)
        emitG(c)
        streams(c + 2)
        emitSumsq(c)
        emitF2(c + 1)
        if c >= 1 and (c - 1) % 4 == 3:
            emitOutproj(c - 1)
        emitTail(c)
        if c % 4 == 2 and c // 4 >= 1:
            wo_tok_pre = {}
            wo_pre[("tok", c // 4)] = wo_tok_pre
            for mq in range(4):
                wo_dma_issue(c // 4, mq, wo_tok_pre, {})
    emitOutproj(NCH - 1)
    p.barrier("GD", "LD", "VE", "MM", "AC", "GP")
    p.dma(G, d_out.ap(), d_u.ap()[:, H:TW], accum=True)
    p.dma_barrier(S)
    for snm in ("VE", "MM", "AC", "GP", "CC"):
        p.wait(S, snm)

    p.build()
    p.es.close()
    return nc


def kernel(**inputs):
    u = np.ascontiguousarray(inputs["L0_attn_u"])
    win = inputs["L0_attn_in_proj_w"]
    w1 = inputs["L0_ffn_w1"]
    wout = inputs["L0_attn_out_proj_w"]
    w1m = np.ascontiguousarray(w1.reshape(KT, 128, 32, 128).transpose(2, 1, 0, 3))
    woutm = np.ascontiguousarray(wout.reshape(16, 128, 8, 128).transpose(2, 1, 0, 3))
    in_maps = []
    for core in range(8):
        b, q = core // 4, core % 4
        t0 = q * T
        halo = np.zeros((H, D), np.float32) if q == 0 else u[b, t0 - H:t0]
        u_r = np.ascontiguousarray(np.concatenate([halo, u[b, t0:t0 + T]], axis=0).T)
        sel = np.zeros((8, 1), np.float32)
        btwT = np.zeros((8, 8), np.float32)
        for r in range(8):
            rb, rq = r // 4, r % 4
            if rb == b and rq < q:
                sel[r, 0] = 1.0
            for j in range(8):
                jb, jq = j // 4, j % 4
                if jb == b and rb == b and rq < jq < q:
                    btwT[j, r] = 1.0
        in_maps.append({
            "u_r": u_r, "win": win, "convw": inputs["L0_attn_conv_w"],
            "convb": inputs["L0_attn_conv_b"], "dtb": inputs["L0_attn_dt_bias"],
            "alog": inputs["L0_attn_A_log"], "dpar": inputs["L0_attn_D_param"],
            "normw": inputs["L0_attn_norm_w"], "w1m": w1m, "b1": inputs["L0_ffn_b1"],
            "w2": inputs["L0_ffn_w2"], "b2": inputs["L0_ffn_b2"], "woutm": woutm,
            "sel": sel, "btwT": btwT,
        })
    nc = build_nc()
    res = run_bass_kernel_spmd(nc, in_maps, core_ids=list(range(8)))
    out = np.zeros((2, 8192, D), np.float32)
    for core in range(8):
        b, q = core // 4, core % 4
        out[b, q * T:(q + 1) * T] = np.ascontiguousarray(res.results[core]["out"]).T
    return out




# version 32
# speedup vs baseline: 1.0209x; 1.0209x over previous
import numpy as np
from contextlib import ExitStack

import concourse.bass as bass
import concourse.mybir as mybir
from concourse.bass_utils import run_bass_kernel_spmd

F32 = mybir.dt.float32
BF16 = mybir.dt.bfloat16
I32 = mybir.dt.int32
ALU = mybir.AluOpType
ACTF = mybir.ActivationFunctionType

ENGINES = ("sync", "gpsimd", "vector", "scalar", "tensor")

T = 2048
H = 3
TW = T + H
D = 1024
KT = 8
INTER = 2048
NH, HD, DS = 32, 64, 128
NCH, C = 16, 128
FF = 4096


class Sem:
    def __init__(self, h):
        self.h = h
        self.n = 0


class Prog:
    def __init__(self, nc):
        self.nc = nc
        self.streams = {e: [] for e in ENGINES}
        self.es = ExitStack()
        self.sems = {}
        names = ["VE", "AC", "MM", "GP", "CC"]
        names += ["LQ%d" % i for i in range(8)] + ["GQ%d" % i for i in range(8)]
        for name in names:
            self.sems[name] = Sem(self.es.enter_context(nc.semaphore(name)))
        self.lastw = {(e, s): 0 for e in ENGINES for s in self.sems}
        self.qstate = {
            "sync": {"pool": ["LQ%d" % i for i in range(8)], "epoch": 0, "cur": None, "last": None},
            "gpsimd": {"pool": ["GQ%d" % i for i in range(8)], "epoch": 0, "cur": None, "last": None},
        }

    def sb(self, name, shape, dt):
        return self.es.enter_context(self.nc.sbuf_tensor(name, shape, dt))

    def emit(self, eng, fn):
        self.streams[eng].append(fn)

    def wait(self, eng, sname, val=None):
        s = self.sems[sname]
        v = s.n if val is None else val
        if v <= self.lastw[(eng, sname)]:
            return
        self.lastw[(eng, sname)] = v
        self.emit(eng, lambda e, h=s.h, v=v: e.wait_ge(h, v))

    ENG_SEM = {"vector": "VE", "scalar": "AC", "gpsimd": "GP"}

    def op(self, eng, fn, inc=None, n=1):
        own = self.ENG_SEM.get(eng)
        if own is not None:
            self.wait(eng, own)
            inc = own
        if inc is not None:
            s = self.sems[inc]
            s.n += n
            self.emit(eng, lambda e, f=fn, h=s.h, m=n: f(e).then_inc(h, m))
        else:
            self.emit(eng, fn)

    def dma(self, eng, out, in_, accum=False):
        q = "gpsimd" if eng == "gpsimd" else "sync"
        st = self.qstate[q]
        if eng == "gpsimd":
            self.wait(eng, "GP")
        if st["cur"] is None:
            nm = st["pool"][st["epoch"] % len(st["pool"])]
            st["epoch"] += 1
            st["cur"] = nm
            self.wait(eng, nm)
        s = self.sems[st["cur"]]
        s.n += 16
        if accum:
            self.emit(eng, lambda e, o=out, i=in_, h=s.h: e.dma_start(
                out=o, in_=i, accum_op=ALU.add).then_inc(h, 16))
        else:
            self.emit(eng, lambda e, o=out, i=in_, h=s.h: e.dma_start(
                out=o, in_=i).then_inc(h, 16))

    def dclose(self, q):
        st = self.qstate[q]
        if st["cur"] is not None:
            st["last"] = (st["cur"], self.sems[st["cur"]].n)
            st["cur"] = None
        return st["last"]

    def ewait(self, eng, token):
        if token is None:
            return
        nm, tot = token
        self.wait(eng, nm, tot)

    def dwait(self, eng, q):
        self.ewait(eng, self.dclose(q))

    def dma_barrier(self, *engines):
        self.dclose("sync")
        self.dclose("gpsimd")
        engs = engines if engines else ENGINES
        for e in engs:
            for nm in self.qstate["sync"]["pool"] + self.qstate["gpsimd"]["pool"]:
                self.wait(e, nm)

    def barrier(self, *producers):
        prs = list(producers)
        if "LD" in prs or "GD" in prs:
            self.dma_barrier()
            prs = [x for x in prs if x not in ("LD", "GD")]
        for e in ENGINES:
            for pr in prs:
                self.wait(e, pr)

    def build(self):
        with self.nc.allow_non_contiguous_dma("small strided param loads"), self.nc.Block() as block:
            @block.sync
            def _(e):
                for f in self.streams["sync"]:
                    f(e)

            @block.gpsimd
            def _(e):
                for f in self.streams["gpsimd"]:
                    f(e)

            @block.vector
            def _(e):
                for f in self.streams["vector"]:
                    f(e)

            @block.scalar
            def _(e):
                for f in self.streams["scalar"]:
                    f(e)

            @block.tensor
            def _(e):
                for f in self.streams["tensor"]:
                    f(e)


def _carve(arena, dt, lo_bytes, n_elems, parts=128):
    esz = {F32: 4, BF16: 2, I32: 4}[dt]
    assert lo_bytes % esz == 0
    ap = arena[:, :].bitcast(dt)
    lo = lo_bytes // esz
    return ap[0:parts, lo:lo + n_elems]


def build_nc():
    nc = bass.Bass("TRN2", target_bir_lowering=False)
    d_u = nc.dram_tensor("u_r", [D, TW], F32, kind="ExternalInput")
    d_win = nc.dram_tensor("win", [D, 4384], F32, kind="ExternalInput")
    d_convw = nc.dram_tensor("convw", [2304, 4], F32, kind="ExternalInput")
    d_convb = nc.dram_tensor("convb", [2304], F32, kind="ExternalInput")
    d_dtb = nc.dram_tensor("dtb", [NH], F32, kind="ExternalInput")
    d_alog = nc.dram_tensor("alog", [NH], F32, kind="ExternalInput")
    d_dpar = nc.dram_tensor("dpar", [NH], F32, kind="ExternalInput")
    d_normw = nc.dram_tensor("normw", [INTER], F32, kind="ExternalInput")
    d_w1 = nc.dram_tensor("w1m", [32, 128, KT, 128], F32, kind="ExternalInput")
    d_b1 = nc.dram_tensor("b1", [FF], F32, kind="ExternalInput")
    d_w2 = nc.dram_tensor("w2", [FF, D], F32, kind="ExternalInput")
    d_w1b = nc.dram_tensor("w1b_d", [32, 128, 1024], BF16, kind="Internal")
    d_woutb = nc.dram_tensor("woutb_d", [8, 128, 2048], BF16, kind="Internal")
    d_b2 = nc.dram_tensor("b2", [D], F32, kind="ExternalInput")
    d_wout = nc.dram_tensor("woutm", [8, 128, 16, 128], F32, kind="ExternalInput")
    d_sel = nc.dram_tensor("sel", [8, 1], F32, kind="ExternalInput")
    d_btwT = nc.dram_tensor("btwT", [8, 8], F32, kind="ExternalInput")
    d_out = nc.dram_tensor("out", [D, T], F32, kind="ExternalOutput")
    d_zsil = nc.dram_tensor("zsil_d", [NCH, 16, 128, C], BF16, kind="Internal")
    d_xtok = nc.dram_tensor("xtok_d", [NCH, 128, INTER], BF16, kind="Internal")
    d_sall = nc.dram_tensor("sall_d", [NCH, 128, INTER], BF16, kind="Internal")
    d_rel = nc.dram_tensor("rel_d", [NCH, NH * C], F32, kind="Internal")
    d_scb = nc.dram_tensor("scb_d", [8, INTER], F32, kind="Internal")
    d_ccin = nc.dram_tensor("cc_in", [129, INTER], F32, kind="Internal")
    d_ccout = nc.dram_tensor("cc_out", [8 * 129, INTER], F32, kind="Internal", addr_space="Shared")

    p = Prog(nc)
    sb = p.sb
    xn = sb("xn", [128, KT * TW], BF16)
    AR1 = sb("AR1", [128, 16384], F32)
    AR2 = sb("AR2", [128, 18459], F32)
    Bf = sb("Bf", [128, T], BF16)
    Cf = sb("Cf", [128, T], BF16)
    Bt = sb("Bt", [128, NCH * C], BF16)
    dt_s = sb("dt_s", [NH, T], F32)
    ga = sb("ga", [NH, T], F32)
    dtok = sb("dtok", [128, NCH * NH], F32)
    rtok = sb("rtok", [128, NCH * NH], F32)
    identF = sb("identF", [128, 128], F32)
    maskle = sb("maskle", [128, 128], F32)
    onesb = sb("onesb", [128, 1], BF16)
    onesrB = sb("onesrB", [1, 128], BF16)
    onesrF = sb("onesrF", [1, 128], F32)
    convw_s = sb("convw_s", [128, 72], F32)
    convb_s = sb("convb_s", [128, 18], F32)
    dtb_s = sb("dtb_s", [NH, 1], F32)
    acol = sb("acol", [NH, 1], F32)
    b1c = sb("b1c", [128, 32], F32)
    b2c = sb("b2c", [128, 8], F32)
    c155 = sb("c155", [128, 1], F32)
    c1e5 = sb("c1e5", [128, 1], F32)
    nw_s = sb("nw_s", [128, 16], F32)
    sel_s = sb("sel_s", [8, 1], F32)
    btw_s = sb("btw_s", [8, 8], F32)
    zcol = sb("zcol", [NH, 1], F32)
    gaex = sb("gaex", [NH, NCH], F32)
    cdee = sb("cdee", [NH, NCH], F32)
    crow = sb("crow", [1, 32], F32)
    rrow = sb("rrow", [1, 128], F32)
    smallF = sb("smallF", [NH, C], F32)
    smallF2 = sb("smallF2", [NH, C], F32)
    db_s = sb("db_s", [128, NH], F32)
    gls = sb("gls", [8, 32], F32)
    esc = sb("esc", [8, 32], F32)
    PSA = p.es.enter_context(nc.psum_tensor("PSA", [128, 2048], F32))
    PSB = p.es.enter_context(nc.psum_tensor("PSB", [128, 2048], F32))

    S = "sync"
    G = "gpsimd"

    p.dma(S, convw_s[:, :].rearrange("p (ct k) -> p ct k", ct=18),
          bass.AP(d_convw, 0, [[4, 128], [512, 18], [1, 4]]))
    p.dma(S, convb_s[:, :], bass.AP(d_convb, 0, [[1, 128], [128, 18]]))
    p.dma(S, dtb_s[:, :], d_dtb.ap().unsqueeze(1))
    p.dma(S, acol[:, :], d_alog.ap().unsqueeze(1))
    p.dma(S, b1c[:, :], bass.AP(d_b1, 0, [[1, 128], [128, 32]]))
    p.dma(S, b2c[:, :], bass.AP(d_b2, 0, [[1, 128], [128, 8]]))
    p.dma(S, nw_s[:, :], bass.AP(d_normw, 0, [[1, 128], [128, 16]]))
    p.dma(S, sel_s[:, :], d_sel.ap())
    p.dma(S, btw_s[:, :], d_btwT.ap())
    p.dma(S, db_s[:, :], d_dpar.ap().unsqueeze(0).unsqueeze(1).to_broadcast([1, 128, NH]))
    iot_p = _carve(AR2, F32, 0, 128)
    iot_f = _carve(AR2, F32, 512, 128)
    p.op(G, lambda e: e.iota(iot_p, [[0, 128]], channel_multiplier=1,
                             allow_small_or_imprecise_dtypes=True), inc="GP")
    p.op(G, lambda e: e.iota(iot_f, [[1, 128]], channel_multiplier=0,
                             allow_small_or_imprecise_dtypes=True), inc="GP")
    p.wait("vector", "GP")
    p.dwait("vector", "sync")
    V = "vector"
    p.op(V, lambda e: e.tensor_tensor(identF[:, :], iot_p, iot_f, ALU.is_equal), inc="VE")
    p.op(V, lambda e: e.tensor_tensor(maskle[:, :], iot_p, iot_f, ALU.is_le), inc="VE")
    p.op(V, lambda e: e.tensor_scalar(onesb[:, :], identF[:, 0:1], 0.0, 1.0, ALU.mult, ALU.add), inc="VE")
    p.op(V, lambda e: e.tensor_scalar(onesrB[:, :], identF[0:1, :], 0.0, 1.0, ALU.mult, ALU.add), inc="VE")
    p.op(V, lambda e: e.tensor_scalar(onesrF[:, :], identF[0:1, :], 0.0, 1.0, ALU.mult, ALU.add), inc="VE")
    p.op(V, lambda e: e.tensor_scalar(zcol[:, :], dtb_s[:, :], 0.0, None, ALU.mult), inc="VE")
    p.op(V, lambda e: e.tensor_scalar(b1c[:, :], b1c[:, :], 256.0, None, ALU.mult), inc="VE")
    p.op(V, lambda e: e.tensor_scalar(b2c[:, :], b2c[:, :], 256.0, None, ALU.mult), inc="VE")
    p.op(V, lambda e: e.tensor_scalar(c155[:, :], identF[:, 0:1], 0.0, -0.5, ALU.mult, ALU.add), inc="VE")
    p.op(V, lambda e: e.tensor_scalar(c1e5[:, :], identF[:, 0:1], 0.0, 1e-5, ALU.mult, ALU.add), inc="VE")
    p.ewait("scalar", p.qstate["sync"]["last"])
    p.op("scalar", lambda e: e.activation(acol[:, :], acol[:, :], ACTF.Exp, bias=0.0, scale=1.0), inc="AC")
    p.wait(V, "AC")
    p.op(V, lambda e: e.tensor_scalar(acol[:, :], acol[:, :], -1.0, None, ALU.mult), inc="VE")

    u_t = [_carve(AR1, F32, k * TW * 4, TW) for k in range(7)]
    u_t.append(_carve(AR2, F32, 1024 * 4, TW))
    ub_t = [_carve(AR2, BF16, 16384, TW * KT)[:, k * TW:(k + 1) * TW] for k in range(KT)]
    u2b = _carve(AR2, BF16, 49200, TW)
    xsc = _carve(AR2, F32, 53600, 512)
    mrowS = _carve(AR2, F32, 55648, 512, parts=1)
    rrowS = _carve(AR2, F32, 57696, 512, parts=1)
    for k in range(KT):
        p.dma(S, u_t[k], d_u.ap()[k * 128:(k + 1) * 128, :])
    p.barrier("LD")
    for k in range(KT):
        p.op(V, lambda e, k=k: e.tensor_copy(ub_t[k], u_t[k]), inc="VE")
    ncs = [(0, 512), (512, 512), (1024, 512), (1536, 512), (2048, 3)]
    mrowS2 = _carve(AR2, F32, 59744, 512, parts=1)
    rrowS2 = _carve(AR2, F32, 61792, 512, parts=1)
    ve_ub = p.sems["VE"].n
    ac_sq, mm_stats, ve_var, ac_sqrt, ve_recip, mm_bcast, ve_xn = {}, {}, {}, {}, {}, {}, {}
    u2bB = _carve(AR2, BF16, 63840, 4096)

    def emit_squares(j):
        if j >= len(ncs):
            return
        c0, w = ncs[j]
        if j >= 1:
            p.wait("scalar", "MM", mm_stats[j - 1])
        for k in range(KT):
            p.op("scalar", lambda e, k=k, c0=c0, w=w: e.activation(
                u2bB[:, k * w:(k + 1) * w], u_t[k][:, c0:c0 + w], ACTF.Square, bias=0.0, scale=1.0), inc="AC")
        ac_sq[j] = p.sems["AC"].n

    emit_squares(0)
    for j, (c0, w) in enumerate(ncs):
        sb0 = (j % 2) * 1024
        pa0 = (j % 2) * 1024
        mro = mrowS if j % 2 == 0 else mrowS2
        rro = rrowS if j % 2 == 0 else rrowS2
        p.wait("tensor", "AC", ac_sq[j])
        p.wait("tensor", "VE", ve_ub)
        if j - 2 >= 0:
            p.wait("tensor", "VE", ve_recip[j - 2])
        for k in range(KT):
            p.op("tensor", lambda e, k=k, c0=c0, w=w, sb0=sb0: e.matmul(
                PSB[0:1, sb0:sb0 + w], onesb[:, :], ub_t[k][:, c0:c0 + w],
                start=(k == 0), stop=(k == KT - 1)), inc="MM")
            p.op("tensor", lambda e, k=k, w=w, sb0=sb0: e.matmul(
                PSB[0:1, sb0 + 512:sb0 + 512 + w], onesb[:, :], u2bB[:, k * w:(k + 1) * w],
                start=(k == 0), stop=(k == KT - 1)), inc="MM")
        mm_stats[j] = p.sems["MM"].n
        emit_squares(j + 1)
        p.wait(V, "MM", mm_stats[j])
        p.op(V, lambda e, w=w, sb0=sb0, mro=mro: e.tensor_scalar(
            mro[:, 0:w], PSB[0:1, sb0:sb0 + w], 1.0 / 1024.0, None, ALU.mult), inc="VE")
        p.op(V, lambda e, w=w, mro=mro, rro=rro: e.tensor_tensor(rro[:, 0:w], mro[:, 0:w], mro[:, 0:w], ALU.mult), inc="VE")
        p.op(V, lambda e, w=w, sb0=sb0, rro=rro: e.scalar_tensor_tensor(
            PSB[0:1, sb0 + 512:sb0 + 512 + w], PSB[0:1, sb0 + 512:sb0 + 512 + w], 1.0 / 1024.0,
            rro[:, 0:w], ALU.mult, ALU.subtract), inc="VE")
        p.op(V, lambda e, w=w, sb0=sb0: e.tensor_scalar(
            PSB[0:1, sb0 + 512:sb0 + 512 + w], PSB[0:1, sb0 + 512:sb0 + 512 + w], 1e-5, None, ALU.add), inc="VE")
        ve_var[j] = p.sems["VE"].n
        p.wait("scalar", "VE", ve_var[j])
        p.op("scalar", lambda e, w=w, sb0=sb0: e.activation(
            PSB[0:1, sb0 + 512:sb0 + 512 + w], PSB[0:1, sb0 + 512:sb0 + 512 + w], ACTF.Sqrt, bias=0.0, scale=1.0), inc="AC")
        ac_sqrt[j] = p.sems["AC"].n
        p.wait(V, "AC", ac_sqrt[j])
        p.op(V, lambda e, w=w, sb0=sb0, rro=rro: e.reciprocal(rro[:, 0:w], PSB[0:1, sb0 + 512:sb0 + 512 + w]), inc="VE")
        p.op(V, lambda e, w=w, mro=mro, rro=rro: e.tensor_tensor(mro[:, 0:w], mro[:, 0:w], rro[:, 0:w], ALU.mult), inc="VE")
        ve_recip[j] = p.sems["VE"].n
        p.wait("tensor", "VE", ve_recip[j])
        if j - 2 >= 0:
            p.wait("tensor", "VE", ve_xn[j - 2])
        p.op("tensor", lambda e, w=w, pa0=pa0, mro=mro: e.matmul(
            PSA[:, pa0:pa0 + w], onesrF[:, :], mro[:, 0:w], start=True, stop=True), inc=None)
        p.op("tensor", lambda e, w=w, pa0=pa0, rro=rro: e.matmul(
            PSA[:, pa0 + 512:pa0 + 512 + w], onesrF[:, :], rro[:, 0:w], start=True, stop=True), inc="MM")
        mm_bcast[j] = p.sems["MM"].n
        p.wait(V, "MM", mm_bcast[j])
        for k in range(KT):
            p.op(V, lambda e, k=k, c0=c0, w=w, pa0=pa0: e.tensor_tensor(
                xsc[:, 0:w], u_t[k][:, c0:c0 + w], PSA[:, pa0 + 512:pa0 + 512 + w], ALU.mult), inc="VE")
            p.op(V, lambda e, k=k, c0=c0, w=w, pa0=pa0: e.tensor_tensor(
                xn[:, k * TW + c0:k * TW + c0 + w], xsc[:, 0:w], PSA[:, pa0:pa0 + w], ALU.subtract), inc="VE")
        ve_xn[j] = p.sems["VE"].n
    p.barrier("VE")

    xbc = _carve(AR2, BF16, 0, 18 * TW)
    winbc = _carve(AR1, BF16, 0, KT * 2336)
    wst1 = _carve(AR1, F32, 37376, 2336)
    wst2 = _carve(AR1, F32, 46720, 2336)
    wtok, ve_wc = {}, {}

    def win_dma(k):
        if k >= KT:
            return
        if k - 2 >= 0:
            p.wait(S, "VE", ve_wc[k - 2])
        st = wst1 if k % 2 == 0 else wst2
        toks = []
        for h in range(2):
            p.dma(S, st[:, h * 1168:(h + 1) * 1168],
                  d_win.ap()[k * 128:(k + 1) * 128, 2048 + h * 1168:2048 + (h + 1) * 1168])
            toks.append(p.dclose("sync"))
        wtok[k] = toks

    p.wait(S, "VE")
    win_dma(0)
    win_dma(1)
    for k in range(KT):
        st = wst1 if k % 2 == 0 else wst2
        for tk_ in wtok[k]:
            p.ewait(V, tk_)
        p.op(V, lambda e, k=k, st=st: e.tensor_copy(winbc[:, k * 2336:(k + 1) * 2336], st), inc="VE")
        ve_wc[k] = p.sems["VE"].n
        win_dma(k + 2)
    p.wait("tensor", "VE", ve_wc[KT - 1])
    ve_slot = {}
    for mt in range(18):
        for j, (c0, w) in enumerate(ncs):
            sl = j % 4
            pv = PSA[:, sl * 512:sl * 512 + w]
            if sl in ve_slot:
                p.wait("tensor", "VE", ve_slot[sl])
            for k in range(KT):
                p.op("tensor", lambda e, k=k, mt=mt, c0=c0, w=w, pv=pv: e.matmul(
                    pv, winbc[:, k * 2336 + mt * 128:k * 2336 + mt * 128 + 128],
                    xn[:, k * TW + c0:k * TW + c0 + w],
                    start=(k == 0), stop=(k == KT - 1)), inc="MM" if k == KT - 1 else None)
            p.wait(V, "MM", p.sems["MM"].n)
            p.op(V, lambda e, mt=mt, c0=c0, w=w, pv=pv: e.tensor_copy(
                xbc[:, mt * TW + c0:mt * TW + c0 + w], pv), inc="VE")
            ve_slot[sl] = p.sems["VE"].n
    dtraw = PSB[0:32, 0:2048]
    for j, (c0, w) in enumerate([(3, 512), (515, 512), (1027, 512), (1539, 512)]):
        sl = j % 2
        pv = PSA[0:32, sl * 512:sl * 512 + w]
        if sl in ve_slot:
            p.wait("tensor", "VE", ve_slot[sl])
        for k in range(KT):
            p.op("tensor", lambda e, k=k, c0=c0, w=w, pv=pv: e.matmul(
                pv, winbc[:, k * 2336 + 2304:k * 2336 + 2336],
                xn[:, k * TW + c0:k * TW + c0 + w],
                start=(k == 0), stop=(k == KT - 1)), inc="MM" if k == KT - 1 else None)
        p.wait(V, "MM", p.sems["MM"].n)
        p.op(V, lambda e, j=j, c0=c0, w=w, pv=pv: e.tensor_copy(dtraw[:, c0 - 3:c0 - 3 + w], pv), inc="VE")
        ve_slot[sl] = p.sems["VE"].n
    p.barrier("VE", "MM")
    ab_t = PSA[0:32, 0:2048]
    p.op(V, lambda e: e.tensor_scalar(dt_s[:, :], dtraw, dtb_s[:, :], 0.0, ALU.add, ALU.max), inc="VE")
    p.wait("scalar", "VE")
    p.op("scalar", lambda e: e.activation(ab_t, dtraw, ACTF.Abs, bias=dtb_s[:, :], scale=1.0), inc="AC")
    p.op("scalar", lambda e: e.activation(ab_t, ab_t, ACTF.Exp, bias=0.0, scale=-1.0), inc="AC")
    p.wait(V, "AC")
    p.op(V, lambda e: e.tensor_scalar(ab_t, ab_t, 1.0, None, ALU.add), inc="VE")
    p.wait("scalar", "VE")
    p.op("scalar", lambda e: e.activation(ab_t, ab_t, ACTF.Ln, bias=0.0, scale=1.0), inc="AC")
    p.wait(V, "AC")
    p.op(V, lambda e: e.tensor_tensor(dt_s[:, :], dt_s[:, :], ab_t, ALU.add), inc="VE")
    p.op(V, lambda e: e.tensor_scalar(ab_t, dt_s[:, :], acol[:, :], None, ALU.mult), inc="VE")
    p.op(V, lambda e: e.tensor_tensor_scan(ga[:, :], ab_t, dt_s[:, :], 0.0, ALU.add, ALU.bypass), inc="VE")
    p.op(V, lambda e: e.tensor_copy(gaex[:, :], bass.AP(ga, 127, [[T, NH], [128, NCH]])), inc="VE")
    p.op(V, lambda e: e.memset(smallF[:, 0:NCH], 0.0), inc="VE")
    p.op(V, lambda e: e.tensor_copy(smallF[:, 1:NCH], gaex[:, 0:NCH - 1]), inc="VE")
    p.op(V, lambda e: e.tensor_tensor(cdee[:, :], gaex[:, :], smallF[:, 0:NCH], ALU.subtract), inc="VE")
    p.wait("scalar", "VE")
    p.op("scalar", lambda e: e.activation(cdee[:, :], cdee[:, :], ACTF.Exp, bias=0.0, scale=1.0), inc="AC")
    p.barrier("AC", "VE")

    zbuf1 = _carve(AR1, BF16, 51200, 2048)
    zbuf2 = _carve(AR1, BF16, 47104, 2048)
    zncs = [(3, 512), (515, 512), (1027, 512), (1539, 512)]
    winz1 = _carve(AR1, BF16, 0, KT * 1024)
    winz2 = _carve(AR1, BF16, 24576, KT * 1024)
    zst1 = _carve(AR1, F32, 16384, 1024)
    zst2 = _carve(AR1, F32, 20480, 1024)
    ztok, ve_zc = {}, {}

    def z_dma(kk):
        if kk >= 2 * KT:
            return
        if kk - 2 >= 0:
            p.wait(S, "VE", ve_zc[kk - 2])
        p.dma(S, zst1 if kk % 2 == 0 else zst2,
              d_win.ap()[(kk % 8) * 128:(kk % 8) * 128 + 128, (kk // 8) * 1024:(kk // 8) * 1024 + 1024])
        ztok[kk] = p.dclose("sync")

    p.wait(S, "MM")
    p.wait(S, "VE")
    z_dma(0)
    z_dma(1)
    for kk in range(2 * KT):
        st = zst1 if kk % 2 == 0 else zst2
        wz = winz1 if kk < 8 else winz2
        p.ewait(V, ztok[kk])
        if kk == 0:
            p.wait(V, "MM")
        p.op(V, lambda e, kk=kk, st=st, wz=wz: e.tensor_copy(wz[:, (kk % 8) * 1024:(kk % 8) * 1024 + 1024], st), inc="VE")
        ve_zc[kk] = p.sems["VE"].n
        z_dma(kk + 2)
    ac_zslot = {}
    zs_tok = {}
    for half in range(2):
        wz = winz1 if half == 0 else winz2
        p.wait("tensor", "VE", ve_zc[half * 8 + 7])
        for mi in range(8):
            mt = half * 8 + mi
            zb = zbuf1 if mt % 2 == 0 else zbuf2
            if mt - 2 in zs_tok:
                p.ewait("scalar", zs_tok[mt - 2])
            for j, (c0, w) in enumerate(zncs):
                sl = j % 4
                pv = PSA[:, sl * 512:sl * 512 + w]
                if sl in ac_zslot:
                    p.wait("tensor", "AC", ac_zslot[sl])
                for k in range(KT):
                    p.op("tensor", lambda e, k=k, mi=mi, c0=c0, w=w, pv=pv, wz=wz: e.matmul(
                        pv, wz[:, k * 1024 + mi * 128:k * 1024 + mi * 128 + 128],
                        xn[:, k * TW + c0:k * TW + c0 + w],
                        start=(k == 0), stop=(k == KT - 1)), inc="MM" if k == KT - 1 else None)
                p.wait("scalar", "MM", p.sems["MM"].n)
                p.op("scalar", lambda e, j=j, w=w, pv=pv, zb=zb: e.activation(
                    zb[:, j * 512:j * 512 + w], pv, ACTF.Silu, bias=0.0, scale=1.0), inc="AC")
                ac_zslot[sl] = p.sems["AC"].n
            p.wait(G, "AC", ac_zslot[3])
            p.dma(G, bass.AP(d_zsil, mt * 128 * C, [[C, 128], [16 * 128 * C, NCH], [1, C]]),
                  zb[:, :].rearrange("p (c l) -> p c l", c=NCH))
            zs_tok[mt] = p.dclose("gpsimd")
    p.barrier("GD", "MM", "AC", "VE")

    diag = _carve(AR1, BF16, 0, 72 * 128)
    xtbuf1 = _carve(AR1, BF16, 18432, 2048)
    xtbuf2 = _carve(AR1, BF16, 36352, 2048)
    cbRf = _carve(AR1, F32, 22528, 2304, parts=1)
    cbR = _carve(AR1, BF16, 31744, 2304, parts=1)
    p.dma(S, cbRf, d_convb.ap().unsqueeze(0))
    cb_tok = p.dclose("sync")
    p.ewait(V, cb_tok)
    p.wait(V, "MM")
    p.op(V, lambda e: e.tensor_copy(cbR, cbRf), inc="VE")
    for ct in range(18):
        for tk in range(4):
            p.op(V, lambda e, ct=ct, tk=tk: e.tensor_scalar(
                diag[:, (ct * 4 + tk) * 128:(ct * 4 + tk) * 128 + 128],
                identF[:, :], convw_s[:, ct * 4 + tk:ct * 4 + tk + 1], None, ALU.mult), inc="VE")
    ve_diag = p.sems["VE"].n
    p.wait("tensor", "VE", ve_diag)
    ac_cgrp = {}
    xt_dtok = {}
    ac_B = None
    for c in range(NCH):
        xtb = xtbuf1 if c % 2 == 0 else xtbuf2
        if c - 2 in xt_dtok:
            p.ewait("scalar", xt_dtok[c - 2])
        for grp in range(4):
            if grp in ac_cgrp:
                p.wait("tensor", "AC", ac_cgrp[grp])
            for j in range(4):
                ct = grp * 4 + j
                pvt = PSA[:, grp * 512 + j * 128:grp * 512 + j * 128 + 128]
                for tk in range(4):
                    p.op("tensor", lambda e, c=c, ct=ct, tk=tk, j=j, pvt=pvt: e.matmul(
                        pvt, xbc[:, ct * TW + c * 128 + tk:ct * TW + c * 128 + tk + 128],
                        diag[:, (ct * 4 + tk) * 128:(ct * 4 + tk) * 128 + 128],
                        start=(tk == 0 and j == 0), stop=False), inc=None)
                p.op("tensor", lambda e, ct=ct, pvt=pvt, j=j: e.matmul(
                    pvt, onesrB[:, :], cbR[:, ct * 128:ct * 128 + 128],
                    start=False, stop=(j == 3)), inc="MM" if j == 3 else None)
            p.wait("scalar", "MM", p.sems["MM"].n)
            p.op("scalar", lambda e, grp=grp, xtb=xtb: e.activation(
                xtb[:, grp * 512:grp * 512 + 512], PSA[:, grp * 512:grp * 512 + 512],
                ACTF.Silu, bias=0.0, scale=1.0), inc="AC")
            ac_cgrp[grp] = p.sems["AC"].n
        p.wait(G, "AC", ac_cgrp[3])
        p.dma(G, d_xtok.ap()[c], xtb[:, :])
        xt_dtok[c] = p.dclose("gpsimd")
        pvb = PSB[:, 0:128]
        if ac_B is not None:
            p.wait("tensor", "AC", ac_B)
        for tk in range(4):
            p.op("tensor", lambda e, c=c, tk=tk, pvb=pvb: e.matmul(
                pvb, xbc[:, 16 * TW + c * 128 + tk:16 * TW + c * 128 + tk + 128],
                diag[:, (64 + tk) * 128:(64 + tk) * 128 + 128],
                start=(tk == 0), stop=False), inc=None)
        p.op("tensor", lambda e, pvb=pvb: e.matmul(
            pvb, onesrB[:, :], cbR[:, 2048:2176], start=False, stop=True), inc="MM")
        p.wait("scalar", "MM", p.sems["MM"].n)
        p.op("scalar", lambda e, c=c, pvb=pvb: e.activation(
            Bt[:, c * C:(c + 1) * C], pvb, ACTF.Silu, bias=0.0, scale=1.0), inc="AC")
        ac_B = p.sems["AC"].n
    ac_f = None
    for ct in (16, 17):
        for j, (c0, w) in enumerate(zncs):
            sl = j % 2
            pv = PSB[:, 512 + sl * 512:512 + sl * 512 + w]
            if (16, sl) in ac_cgrp:
                p.wait("tensor", "AC", ac_cgrp[(16, sl)])
            for tk in range(4):
                p.op("tensor", lambda e, ct=ct, tk=tk, c0=c0, w=w, pv=pv: e.matmul(
                    pv, diag[:, (ct * 4 + tk) * 128:(ct * 4 + tk) * 128 + 128],
                    xbc[:, ct * TW + c0 - 3 + tk:ct * TW + c0 - 3 + tk + w],
                    start=(tk == 0), stop=(tk == 3)), inc="MM" if tk == 3 else None)
            p.wait("scalar", "MM", p.sems["MM"].n)
            dst = Bf if ct == 16 else Cf
            p.op("scalar", lambda e, ct=ct, c0=c0, w=w, pv=pv, dst=dst: e.activation(
                dst[:, c0 - 3:c0 - 3 + w], pv, ACTF.Silu, bias=convb_s[:, ct:ct + 1], scale=1.0), inc="AC")
            ac_cgrp[(16, sl)] = p.sems["AC"].n
    p.barrier("AC", "MM", "VE", "GD")

    xbc_mm, xbc_ve, xbc_ac = p.sems["MM"].n, p.sems["VE"].n, p.sems["AC"].n
    s_run = _carve(AR1, F32, 0, 2048)
    crow_all = _carve(AR2, F32, 70144, 512)
    rowAll = _carve(AR2, F32, 65536, 512, parts=1)
    p.wait("tensor", "VE")
    p.wait("tensor", "AC")
    for cc in range(16):
        p.op("tensor", lambda e, cc=cc: e.transpose(
            PSB[0:1, cc * 32:(cc + 1) * 32], cdee[:, cc:cc + 1], identF[0:32, 0:32]),
            inc="MM" if cc == 15 else None)
    p.wait(V, "MM")
    p.op(V, lambda e: e.tensor_copy(rowAll, PSB[0:1, 0:512]), inc="VE")
    p.wait("tensor", "VE")
    for cc in range(16):
        p.op("tensor", lambda e, cc=cc: e.matmul(
            PSB[:, cc * 32:(cc + 1) * 32], onesrF[:, :], rowAll[:, cc * 32:(cc + 1) * 32],
            start=(cc == 0), stop=(cc == 15)), inc="MM" if cc == 15 else None)
    p.wait(V, "MM")
    p.op(V, lambda e: e.tensor_copy(crow_all, PSB[:, 0:512]), inc="VE")
    p.wait("scalar", "MM")
    xts1 = _carve(AR1, BF16, 8192, 2048)
    xts2 = _carve(AR1, BF16, 12288, 2048)
    xtw = _carve(AR1, BF16, 16384, 2048)
    ssc = _carve(AR1, BF16, 20480, 2048)
    w2bf = _carve(AR1, BF16, 0, 32768)
    w2pst = [_carve(AR2, F32, i * 4096, 1024) for i in range(4)]
    w1pst = [_carve(AR2, F32, 16384 + i * 4096, 1024) for i in range(4)]
    w1pob = [_carve(AR2, BF16, 32768 + i * 2048, 1024) for i in range(4)]
    tok_w2, tok_w1i, tok_w1o = {}, {}, {}
    ve_w2c, ve_w1c = {}, {}
    p.wait(S, "MM", xbc_mm)
    p.wait(S, "VE", xbc_ve)
    p.wait(S, "AC", xbc_ac)
    p.wait("scalar", "MM", xbc_mm)
    p.ewait("scalar", zs_tok[14])
    p.ewait("scalar", zs_tok[15])
    p.ewait("scalar", xt_dtok[14])
    p.ewait("scalar", xt_dtok[15])

    def w1_in(j):
        if j > 31:
            return
        if j - 4 >= 0:
            p.wait(S, "AC", ve_w1c[j - 4])
        p.dma(S, w1pst[j % 4], d_w1.ap()[j].rearrange("p k c -> p (k c)"))
        tok_w1i[j] = p.dclose("sync")

    def w1_cast_out(j):
        if j < 0 or j > 31:
            return
        p.ewait("scalar", tok_w1i[j])
        if j - 4 >= 0:
            p.ewait("scalar", tok_w1o[j - 4])
        p.op("scalar", lambda e, j=j: e.activation(
            w1pob[j % 4], w1pst[j % 4], ACTF.Copy, bias=0.0, scale=1.0), inc="AC")
        ve_w1c[j] = p.sems["AC"].n
        p.wait(G, "AC", ve_w1c[j])
        p.dma(G, d_w1b.ap()[j], w1pob[j % 4])
        tok_w1o[j] = p.dclose("gpsimd")

    def w2_in(i):
        if i > 31:
            return
        if i - 4 >= 12:
            p.wait(S, "AC", ve_w2c[i - 4])
        p.dma(S, w2pst[(i - 12) % 4], d_w2.ap()[i * 128:(i + 1) * 128, :])
        tok_w2[i] = p.dclose("sync")

    def w2_cast(i):
        if i < 12 or i > 31:
            return
        p.ewait("scalar", tok_w2[i])
        p.op("scalar", lambda e, i=i: e.activation(
            w2bf[:, i * 1024:(i + 1) * 1024], w2pst[(i - 12) % 4], ACTF.Copy, bias=0.0, scale=1.0), inc="AC")
        ve_w2c[i] = p.sems["AC"].n

    p.op(V, lambda e: e.memset(s_run, 0.0), inc="VE")
    p.dclose("sync")
    p.dma(S, xts1, d_xtok.ap()[0])
    xt_tok = [p.dclose("sync")]
    smallFB = _carve(AR2, F32, 45056, 128, parts=32)
    smallF2B = _carve(AR2, F32, 45568, 128, parts=32)
    xtwB = _carve(AR2, BF16, 46080, 2048)
    ve_xtw, mm_tp, ve_sF, ac_sF, mm_st, ve_sr = {}, {}, {}, {}, {}, {}
    rel_tok, sall_tok = {}, {}

    def d_front(c):
        if c >= NCH:
            return
        sF = smallF if c % 2 == 0 else smallFB
        sF2 = smallF2 if c % 2 == 0 else smallF2B
        if c >= 2:
            p.ewait("scalar", rel_tok[c - 2])
            p.wait("scalar", "MM", mm_tp[c - 2])
        p.op("scalar", lambda e, c=c, sF=sF: e.activation(
            sF[:, :], ga[:, c * C:(c + 1) * C], ACTF.Exp, bias=gaex[:, c:c + 1], scale=-1.0), inc="AC")
        ac_sF[c] = p.sems["AC"].n
        p.wait(V, "AC", ac_sF[c])
        if c >= 2:
            p.wait(V, "MM", mm_tp[c - 2])
        p.op(V, lambda e, c=c, sF=sF, sF2=sF2: e.tensor_tensor(
            sF2[:, :], dt_s[:, c * C:(c + 1) * C], sF[:, :], ALU.mult), inc="VE")
        if c == 0:
            p.op(V, lambda e, c=c, sF=sF: e.tensor_scalar(
                sF[:, :], ga[:, c * C:(c + 1) * C], zcol[:, :], None, ALU.subtract), inc="VE")
        else:
            p.op(V, lambda e, c=c, sF=sF: e.tensor_scalar(
                sF[:, :], ga[:, c * C:(c + 1) * C], gaex[:, c - 1:c], None, ALU.subtract), inc="VE")
        ve_sF[c] = p.sems["VE"].n

    def d_trans(c):
        if c >= NCH:
            return
        sF = smallF if c % 2 == 0 else smallFB
        sF2 = smallF2 if c % 2 == 0 else smallF2B
        t0 = 512 if c % 2 == 0 else 608
        p.wait("tensor", "VE", ve_sF[c])
        if c >= 2:
            p.wait("tensor", "VE", ve_xtw[c - 2])
        p.op("tensor", lambda e, t0=t0, sF2=sF2: e.transpose(PSB[:, t0:t0 + 32], sF2[:, :], identF[0:32, 0:32]), inc=None)
        p.op("tensor", lambda e, t0=t0, sF=sF: e.transpose(PSB[:, t0 + 32:t0 + 64], sF[:, :], identF[0:32, 0:32]), inc=None)
        p.op("tensor", lambda e, c=c, t0=t0: e.transpose(PSB[:, t0 + 64:t0 + 96], dt_s[:, c * C:(c + 1) * C], identF[0:32, 0:32]), inc="MM")
        mm_tp[c] = p.sems["MM"].n

    def d_copies(c):
        if c >= NCH:
            return
        sF = smallF if c % 2 == 0 else smallFB
        xtw_c = xtw if c % 2 == 0 else xtwB
        xts = xts1 if c % 2 == 0 else xts2
        t0 = 512 if c % 2 == 0 else 608
        p.ewait(V, xt_tok[c])
        p.wait(V, "MM", mm_tp[c])
        if c >= 2:
            p.wait(V, "MM", mm_st[c - 2])
        p.op(V, lambda e, c=c, t0=t0: e.tensor_copy(rtok[:, c * NH:(c + 1) * NH], PSB[:, t0 + 32:t0 + 64]), inc="VE")
        p.op(V, lambda e, c=c, t0=t0: e.tensor_copy(dtok[:, c * NH:(c + 1) * NH], PSB[:, t0 + 64:t0 + 96]), inc="VE")
        p.op(V, lambda e, xts=xts, xtw_c=xtw_c, t0=t0: e.tensor_tensor(
            xtw_c.rearrange("p (h q) -> p h q", h=NH),
            xts.rearrange("p (h q) -> p h q", h=NH),
            PSB[:, t0:t0 + 32].unsqueeze(2).to_broadcast([128, NH, HD]), ALU.mult), inc="VE")
        ve_xtw[c] = p.sems["VE"].n
        p.wait(G, "VE", ve_xtw[c])
        p.dma(G, d_rel.ap()[c], sF[:, :])
        rel_tok[c] = p.dclose("gpsimd")

    def d_states(c):
        xtw_c = xtw if c % 2 == 0 else xtwB
        p.wait("tensor", "VE", ve_xtw[c])
        if c >= 1:
            p.wait("tensor", "VE", ve_sr[c - 1])
        for j in range(4):
            p.op("tensor", lambda e, c=c, j=j, xtw_c=xtw_c: e.matmul(
                PSA[:, j * 512:(j + 1) * 512], Bt[:, c * C:(c + 1) * C], xtw_c[:, j * 512:(j + 1) * 512],
                start=True, stop=True), inc="MM" if j == 3 else None)
        mm_st[c] = p.sems["MM"].n

    def d_tail(c):
        if c - 1 in sall_tok:
            p.ewait(V, sall_tok[c - 1])
        p.op(V, lambda e: e.tensor_copy(ssc, s_run), inc="VE")
        p.wait(G, "VE")
        p.dma(G, d_sall.ap()[c], ssc)
        sall_tok[c] = p.dclose("gpsimd")
        p.wait(V, "MM", mm_st[c])
        p.op(V, lambda e, c=c: e.tensor_tensor(
            s_run.rearrange("p (h q) -> p h q", h=NH),
            s_run.rearrange("p (h q) -> p h q", h=NH),
            crow_all[:, c * 32:(c + 1) * 32].unsqueeze(2).to_broadcast([128, NH, HD]), ALU.mult), inc="VE")
        p.op(V, lambda e: e.tensor_tensor(s_run, s_run, PSA[:, :], ALU.add), inc="VE")
        ve_sr[c] = p.sems["VE"].n

    d_front(0)
    d_trans(0)
    d_copies(0)
    for c in range(NCH):
        if c + 1 < NCH:
            if c >= 1:
                p.wait(S, "VE", ve_xtw[c - 1])
            p.dma(S, xts2 if c % 2 == 0 else xts1, d_xtok.ap()[c + 1])
            xt_tok.append(p.dclose("sync"))
        d_front(c + 1)
        d_states(c)
        d_trans(c + 1)
        d_copies(c + 1)
        d_tail(c)
        w1_in(2 * c)
        w1_in(2 * c + 1)
        w1_cast_out(2 * c - 2)
        w1_cast_out(2 * c - 1)
        w2_in(12 + 2 * c)
        w2_in(13 + 2 * c)
        w2_cast(10 + 2 * c)
        w2_cast(11 + 2 * c)
    w1_cast_out(30)
    w1_cast_out(31)
    p.barrier("VE")
    zrow = _carve(AR1, F32, 8192, 2016, parts=1)
    p.op(V, lambda e: e.memset(zrow, 0.0), inc="VE")
    p.barrier("VE")
    p.dma(G, d_ccin.ap()[0:128, :], s_run)
    p.dma(G, d_ccin.ap()[128:129, 0:32], ga[:, T - 1:T])
    p.dma(G, d_ccin.ap()[128:129, 32:2048], zrow)
    p.dwait(G, "gpsimd")
    p.op(G, lambda e: e.collective_compute(
        "AllGather", ALU.bypass, replica_groups=[list(range(8))],
        ins=[d_ccin.ap().opt()], outs=[d_ccout.ap().opt()]), inc="CC")

    ve_p1d, ac_p1d, mm_p1d = p.sems["VE"].n, p.sems["AC"].n, p.sems["MM"].n
    w2h = [_carve(AR2, F32, 40960 + i * 4096, 1024) for i in range(4)]
    p.wait(S, "MM", mm_p1d)
    p.ewait(S, rel_tok[15])
    p.dma_barrier("scalar")
    p.wait("scalar", "MM", mm_p1d)
    tok_w2h = {}
    for j in range(14):
        if j < 12:
            if j >= 4:
                p.wait(S, "AC", ve_w2c[j - 4])
            p.dma(S, w2h[j % 4], d_w2.ap()[j * 128:(j + 1) * 128, :])
            tok_w2h[j] = p.dclose("sync")
        if j >= 2:
            i = j - 2
            p.ewait("scalar", tok_w2h[i])
            p.op("scalar", lambda e, i=i: e.activation(
                w2bf[:, i * 1024:(i + 1) * 1024], w2h[i % 4], ACTF.Copy, bias=0.0, scale=1.0), inc="AC")
            ve_w2c[i] = p.sems["AC"].n
    ve_prepass = p.sems["AC"].n

    NPB = 5
    stgp = ([_carve(AR2, BF16, 8192 + i * 4096, 2048) for i in range(3)] +
            [_carve(AR2, BF16, 61440 + i * 4096, 2048) for i in range(2)])
    qbuf = [_carve(AR2, I32, 20480, 2048), _carve(AR2, I32, 0, 2048)]
    scA = _carve(AR2, F32, 28672, 2048)
    scB = _carve(AR2, F32, 36864, 2048)
    scC = _carve(AR2, F32, 45056, 2048)
    hgq = [_carve(AR2, BF16, 53248, 2048), _carve(AR2, BF16, 57344, 2048)]
    tok_stg, mm_after, evac_ac = {}, {}, {}
    ve_V1, ve_xhi, ve_y = {}, {}, {}
    ac_A1, ac_A3, ac_A5 = {}, {}, {}
    mm_w2_after, ve_obuf, tok_out = {}, {}, {}

    def issue_stage(gm):
        if gm >= 256 or gm % 2 != 0:
            return
        pg = gm // 2
        p.wait(S, "AC", ve_prepass)
        if pg - NPB >= 0:
            p.wait(S, "MM", mm_after[2 * (pg - NPB) + 1])
        if gm < 32:
            p.ewait(S, tok_w1o[gm])
            p.ewait(S, tok_w1o[gm + 1])
        m0 = gm % 32
        p.dma(S, stgp[pg % NPB].rearrange("p (m c) -> p m c", m=2),
              bass.AP(d_w1b, m0 * 131072, [[1024, 128], [131072, 2], [1, 1024]]))
        tok_stg[pg] = p.dclose("sync")

    def emit_chain1(ms, h, q, qo, hg):
        c0, c1 = h * 1024, (h + 1) * 1024
        p.op("scalar", lambda e, q=q, qo=qo, c0=c0, c1=c1: e.activation(
            qo[:, c0:c1], q[:, c0:c1], ACTF.Identity, bias=c155[:, 0:1], scale=1.0 / 64.0), inc="AC")
        ac_A1[(ms, h)] = p.sems["AC"].n
        p.wait(V, "AC", ac_A1[(ms, h)])
        p.op(V, lambda e, qo=qo, c0=c0, c1=c1: e.tensor_copy(scA[:, c0:c1], qo[:, c0:c1]), inc="VE")
        ve_V1[(ms, h)] = p.sems["VE"].n
        p.op(V, lambda e, q=q, c0=c0, c1=c1: e.scalar_tensor_tensor(
            scB[:, c0:c1], q[:, c0:c1], 1.0 / 64.0, scA[:, c0:c1], ALU.mult, ALU.subtract), inc="VE")
        if ms % 4 == 0 and ms >= 4:
            p.ewait(V, tok_out[ms // 4 - 1][h])
        p.op(V, lambda e, c0=c0, c1=c1: e.tensor_scalar(scC[:, c0:c1], scA[:, c0:c1], 0.25, None, ALU.mult), inc="VE")
        p.op(V, lambda e, c0=c0, c1=c1: e.tensor_scalar(scA[:, c0:c1], scC[:, c0:c1], 0.25, None, ALU.add), inc="VE")
        ve_xhi[(ms, h)] = p.sems["VE"].n

    def emit_chain2(ms, h):
        c0, c1 = h * 1024, (h + 1) * 1024
        if ms % 4 == 0 and ms >= 4:
            p.ewait("scalar", tok_out[ms // 4 - 1][h])
        p.wait("scalar", "VE", ve_xhi[(ms, h)])
        p.op("scalar", lambda e, c0=c0, c1=c1: e.activation(
            scC[:, c0:c1], scC[:, c0:c1], ACTF.Gelu, bias=0.0, scale=1.0), inc="AC")
        p.op("scalar", lambda e, c0=c0, c1=c1: e.activation(
            scA[:, c0:c1], scA[:, c0:c1], ACTF.Gelu, bias=0.0, scale=1.0), inc="AC")
        ac_A3[(ms, h)] = p.sems["AC"].n
        p.wait(V, "AC", ac_A3[(ms, h)])
        p.op(V, lambda e, c0=c0, c1=c1: e.tensor_tensor(scA[:, c0:c1], scA[:, c0:c1], scC[:, c0:c1], ALU.subtract), inc="VE")
        p.op(V, lambda e, c0=c0, c1=c1: e.tensor_tensor(scB[:, c0:c1], scB[:, c0:c1], scA[:, c0:c1], ALU.mult), inc="VE")
        p.op(V, lambda e, c0=c0, c1=c1: e.tensor_tensor(scC[:, c0:c1], scC[:, c0:c1], scB[:, c0:c1], ALU.add), inc="VE")
        ve_y[(ms, h)] = p.sems["VE"].n

    def emit_chain3(ms, h, q, hg):
        c0, c1 = h * 1024, (h + 1) * 1024
        p.wait("scalar", "VE", ve_y[(ms, h)])
        p.op("scalar", lambda e, q=q, c0=c0, c1=c1: e.activation(
            q[:, c0:c1], scC[:, c0:c1], ACTF.Copy, bias=0.0, scale=256.0), inc="AC")
        if h == 0 and ms - 2 >= 0:
            p.wait("scalar", "MM", mm_w2_after[ms - 2])
        p.op("scalar", lambda e, q=q, hg=hg, c0=c0, c1=c1: e.activation(
            hg[:, c0:c1], q[:, c0:c1], ACTF.Copy, bias=0.0, scale=1.0), inc="AC")
        ac_A5[(ms, h)] = p.sems["AC"].n

    def emit_w2_half(pend_w2, half):
        ms, hg = pend_w2
        msub = ms % 4
        if half == 0:
            p.wait("tensor", "AC", ve_w2c[msub * 8 + 7])
        p.wait("tensor", "AC", ac_A5[(ms, half)])
        for m8w in range(4 * half, 4 * half + 4):
            mw = msub * 8 + m8w
            for m2 in range(8):
                stt = (msub == 0 and m8w == 0 and m2 % 2 == 0)
                sp = (msub == 3 and m8w == 7 and m2 % 2 == 1)
                last = (m8w == 7 and m2 == 7)
                p.op("tensor", lambda e, mw=mw, m2=m2, hg=hg, m8w=m8w, stt=stt, sp=sp: e.matmul(
                    PSA[:, m2 * 256:(m2 + 1) * 256],
                    w2bf[:, mw * 1024 + m2 * 128:mw * 1024 + m2 * 128 + 128],
                    hg[:, m8w * 256:(m8w + 1) * 256],
                    start=stt, stop=sp), inc="MM" if last else None)
        if half == 1:
            mm_w2_after[ms] = p.sems["MM"].n

    p.wait("tensor", "VE", ve_p1d)
    p.wait("tensor", "AC", ac_p1d)
    for gm0 in range(0, 8, 2):
        issue_stage(gm0)
    pend_w2 = None
    pend = None
    for j in (28, 29, 30, 31):
        p.ewait(V, tok_w1o[j])
    for g in range(8):
        gc0 = 3 + g * 256
        for msub in range(4):
            ms = g * 4 + msub
            par = msub % 2
            q, qo, hg = qbuf[par], qbuf[1 - par], hgq[ms % 2]
            for m8 in range(8):
                m = msub * 8 + m8
                gm = g * 32 + m
                issue_stage(gm + 8)
                bank = m8 % 4
                p.ewait("tensor", tok_stg[gm // 2])
                if bank in evac_ac:
                    p.wait("tensor", "AC", evac_ac[bank])
                pv = PSB[:, bank * 512:bank * 512 + 256]
                sg = stgp[(gm // 2) % NPB]
                sc0 = (gm % 2) * 1024
                for k in range(KT):
                    p.op("tensor", lambda e, k=k, sg=sg, pv=pv, gc0=gc0, sc0=sc0: e.matmul(
                        pv, sg[:, sc0 + k * 128:sc0 + (k + 1) * 128], xn[:, k * TW + gc0:k * TW + gc0 + 256],
                        start=(k == 0), stop=(k == KT - 1)), inc="MM" if k == KT - 1 else None)
                mm_after[gm] = p.sems["MM"].n
                if m8 in (0, 4):
                    h = m8 // 4
                    if msub == 0:
                        if g > 0:
                            p.wait("scalar", "VE", ve_obuf[g - 1])
                    else:
                        p.wait("scalar", "VE", ve_V1[(ms - 1, h)])
                p.wait("scalar", "MM", mm_after[gm])
                p.op("scalar", lambda e, m=m, m8=m8, pv=pv, q=q: e.activation(
                    q[:, m8 * 256:m8 * 256 + 256], pv, ACTF.Identity, bias=b1c[:, m:m + 1], scale=256.0), inc="AC")
                evac_ac[bank] = p.sems["AC"].n
                if m8 == 1 and pend is not None:
                    emit_chain2(pend[0], 0)
                if m8 == 2 and pend is not None:
                    emit_chain2(pend[0], 1)
                if m8 == 3 and pend is not None:
                    emit_chain3(pend[0], 0, pend[1], pend[3])
                if m8 == 4:
                    emit_chain1(ms, 0, q, qo, hg)
                if m8 == 5 and pend is not None:
                    emit_chain3(pend[0], 1, pend[1], pend[3])
                    pend = None
            emit_chain1(ms, 1, q, qo, hg)
            if pend_w2 is not None:
                emit_w2_half(pend_w2, 0)
                emit_w2_half(pend_w2, 1)
                pend_w2 = None
            if msub == 3:
                emit_chain2(ms, 0)
                emit_chain2(ms, 1)
                emit_chain3(ms, 0, q, hg)
                emit_chain3(ms, 1, q, hg)
            else:
                pend = (ms, q, qo, hg)
            pend_w2 = (ms, hg)
        emit_w2_half(pend_w2, 0)
        emit_w2_half(pend_w2, 1)
        pend_w2 = None
        p.wait("scalar", "MM", mm_w2_after[g * 4 + 3])
        tok_out[g] = []
        for hf in range(2):
            for m2 in range(4 * hf, 4 * hf + 4):
                p.op("scalar", lambda e, m2=m2: e.activation(
                    qbuf[0][:, m2 * 256:m2 * 256 + 256], PSA[:, m2 * 256:(m2 + 1) * 256],
                    ACTF.Identity, bias=b2c[:, m2:m2 + 1], scale=1.0), inc="AC")
            p.wait(V, "AC", p.sems["AC"].n)
            p.op(V, lambda e, hf=hf: e.tensor_scalar(
                scC[:, hf * 1024:(hf + 1) * 1024], qbuf[0][:, hf * 1024:(hf + 1) * 1024],
                1.0 / 256.0, None, ALU.mult), inc="VE")
            p.wait(G, "VE", p.sems["VE"].n)
            p.dma(G, bass.AP(d_out, g * 256 + hf * 4 * 128 * T, [[T, 128], [128 * T, 4], [1, 256]]),
                  scC[:, hf * 1024:(hf + 1) * 1024].rearrange("p (m l) -> p m l", m=4))
            tok_out[g].append(p.dclose("gpsimd"))
        ve_obuf[g] = p.sems["VE"].n
    p.barrier("GD", "MM", "AC", "VE", "GP")


    p.barrier("CC", "GD", "LD", "VE", "MM", "AC", "GP")
    scb8 = _carve(AR2, F32, 37376, 2048, parts=8)
    srs = _carve(AR1, F32, 0, 2048)
    scbc = _carve(AR1, F32, 8192, 2048)
    tmp2 = _carve(AR1, F32, 16384, 2048)
    s_i2 = _carve(AR2, F32, 8704, 2048)
    s_i2b = _carve(AR2, BF16, 16896, 2048)
    dii = _carve(AR2, BF16, 512, NH * C)
    p.op(V, lambda e: e.tensor_tensor(
        dii.rearrange("p (h c) -> p h c", h=NH),
        identF[:, :].unsqueeze(1).to_broadcast([128, NH, 128]),
        db_s[:, :].unsqueeze(2).to_broadcast([128, NH, 128]), ALU.mult), inc="VE")
    p.dma(S, gls[:, :], bass.AP(d_ccout, 128 * INTER, [[129 * INTER, 8], [1, 32]]))
    p.barrier("LD")
    p.ewait("tensor", p.qstate["sync"]["last"])
    p.op("tensor", lambda e: e.matmul(PSB[0:8, 0:32], btw_s[:, :], gls[:, :], start=True, stop=True), inc="MM")
    p.wait("scalar", "MM")
    p.op("scalar", lambda e: e.activation(esc[:, :], PSB[0:8, 0:32], ACTF.Exp, bias=0.0, scale=1.0), inc="AC")
    p.wait(V, "AC")
    p.op(V, lambda e: e.tensor_scalar(esc[:, :], esc[:, :], sel_s[:, :], None, ALU.mult), inc="VE")
    p.op(V, lambda e: e.tensor_copy(
        scb8.rearrange("r (h q) -> r h q", h=NH),
        esc[:, :].unsqueeze(2).to_broadcast([8, NH, HD])), inc="VE")
    p.wait(G, "VE")
    p.dma(G, d_scb.ap(), scb8)
    p.dwait(S, "gpsimd")
    p.op(V, lambda e: e.memset(s_i2, 0.0), inc="VE")
    srs2 = _carve(AR1, F32, 24576, 2048)
    scbc2 = _carve(AR1, F32, 32768, 2048)
    ri_tok = {}
    ve_rr = {}
    for r in range(9):
        if r < 8:
            sb_s = srs if r % 2 == 0 else srs2
            sb_c = scbc if r % 2 == 0 else scbc2
            if r >= 2:
                p.wait(S, "VE", ve_rr[r - 2])
            p.dma(S, sb_s, d_ccout.ap()[129 * r:129 * r + 128, :])
            p.dma(S, sb_c, d_scb.ap()[r].unsqueeze(0).unsqueeze(1).to_broadcast([1, 128, INTER]))
            ri_tok[r] = p.dclose("sync")
        if r >= 1:
            rr = r - 1
            sb_s = srs if rr % 2 == 0 else srs2
            sb_c = scbc if rr % 2 == 0 else scbc2
            p.ewait(V, ri_tok[rr])
            p.op(V, lambda e, sb_s=sb_s, sb_c=sb_c: e.tensor_tensor(tmp2, sb_s, sb_c, ALU.mult), inc="VE")
            p.op(V, lambda e: e.tensor_tensor(s_i2, s_i2, tmp2, ALU.add), inc="VE")
            ve_rr[rr] = p.sems["VE"].n

    xts1p = _carve(AR1, BF16, 0, 2048)
    xts2p = _carve(AR1, BF16, 4096, 2048)
    sl1 = _carve(AR1, BF16, 8192, 2048)
    sl2 = _carve(AR1, BF16, 12288, 2048)
    zs1 = _carve(AR1, BF16, 16384, 2048)
    zs2 = _carve(AR1, BF16, 20480, 2048)
    relb = _carve(AR1, F32, 24576, 4096)
    bufA = _carve(AR1, F32, 40960, 4096)
    wobf1 = _carve(AR1, BF16, 57344, 2048)
    wobf2 = _carve(AR1, BF16, 61440, 2048)
    bufB = _carve(AR2, F32, 20992, 4096)
    mt_b = _carve(AR2, BF16, 37376, 4096)
    ct_b = _carve(AR2, BF16, 45568, 4096)
    xtq = _carve(AR2, BF16, 53760, 2048)
    graw = _carve(AR2, F32, 57856, 2048)
    g2b = _carve(AR2, BF16, 66048, 2048)
    scm = _carve(AR2, F32, 0, 128)
    ggrp = _carve(xn, BF16, 0, 8192)
    wostg = _carve(xn, F32, 16384, 2048)
    wos3 = _carve(xn, BF16, 16384, 2048)
    wos4 = _carve(xn, BF16, 20480, 2048)
    obuf2 = _carve(xn, F32, 24576, 2048)

    p.barrier("VE", "GD", "LD")
    p.dclose("sync")
    p.dclose("gpsimd")
    st_tok, rl_tok = {}, {}
    wi_tok, wb_tok, ve_wfold = {}, {}, {}
    mm_scores, mm_y = {}, {}
    ac_expg, ve_bufA, ac_bufB, ac_L, ac_si2b, ac_g2b, ve_dec = {}, {}, {}, {}, {}, {}, {}
    ve_bufA_h, ac_expg_h, ve_mt_h, ve_ct_h, ve_g2b_h, wo_pre = {}, {}, {}, {}, {}, {}
    ve_scm, ve_ctb, ve_graw, ve_g2b = {}, {}, {}, {}
    mm_sumsq, ac_rsq, ve_rrec, mm_rsb, ve_ggrp = {}, {}, {}, {}, {}
    ve_oc_slot, mm_wo_slot = {}, {}
    mm_op_last = [None]
    out_tok = [None, None]

    def streams(c):
        if c >= NCH:
            return
        if c >= 2:
            p.wait(S, "MM", mm_y[c - 2])
            p.wait(S, "VE", ve_graw[c - 2])
        p.dma(S, xts1p if c % 2 == 0 else xts2p, d_xtok.ap()[c])
        p.dma(S, sl1 if c % 2 == 0 else sl2, d_sall.ap()[c])
        p.dma(S, zs1 if c % 2 == 0 else zs2,
              bass.AP(d_zsil, c * 16 * 128 * C, [[C, 128], [128 * C, 16], [1, C]]))
        st_tok[c] = p.dclose("sync")

    def relb_pre(c):
        if c >= NCH:
            return
        rl_tok[c] = []
        if c >= 1:
            p.wait(S, "VE", ve_bufA[c - 1])
            p.wait(S, "AC", ac_expg[c - 1])
        p.dma(S, relb[:, 0:2048],
              d_rel.ap()[c][0:2048].unsqueeze(0).unsqueeze(1).to_broadcast([1, 128, 2048]))
        rl_tok[c].append(p.dclose("sync"))
        if c >= 1:
            p.wait(G, "VE", ve_bufA[c - 1])
            p.wait(G, "AC", ac_expg[c - 1])
        p.dma(G, relb[:, 2048:4096],
              d_rel.ap()[c][2048:4096].unsqueeze(0).unsqueeze(1).to_broadcast([1, 128, 2048]))
        rl_tok[c].append(p.dclose("gpsimd"))

    def emitF1(c):
        if c >= NCH:
            return
        for tk in rl_tok[c]:
            p.ewait(V, tk)
        if c >= 1:
            p.wait(V, "AC", ac_L[(c - 1, 1)])
        for hh in range(2):
            hc = hh * 2048
            p.op(V, lambda e, c=c, hh=hh, hc=hc: e.tensor_tensor(
                bufA[:, hc:hc + 2048].rearrange("p (h l) -> p h l", h=16),
                rtok[:, c * NH + hh * 16:c * NH + hh * 16 + 16].unsqueeze(2).to_broadcast([128, 16, C]),
                relb[:, hc:hc + 2048].rearrange("p (h l) -> p h l", h=16), ALU.subtract), inc="VE")
            ve_bufA_h[(c, hh)] = p.sems["VE"].n
        ve_bufA[c] = p.sems["VE"].n
        p.wait(V, "MM", mm_scores[c])
        p.op(V, lambda e: e.tensor_tensor(scm, PSB[:, 0:128], maskle[:, :], ALU.mult), inc="VE")
        ve_scm[c] = p.sems["VE"].n

    def emitAC(c):
        if c >= NCH:
            return
        for tk in rl_tok[c]:
            p.ewait("scalar", tk)
        for hh in range(2):
            hc = hh * 2048
            if c >= 1:
                p.wait("scalar", "VE", ve_ct_h[(c - 1, hh)])
            p.op("scalar", lambda e, hc=hc: e.activation(
                bufB[:, hc:hc + 2048], relb[:, hc:hc + 2048], ACTF.Exp, bias=0.0, scale=1.0), inc="AC")
            ac_expg_h[(c, hh)] = p.sems["AC"].n
            p.wait("scalar", "VE", ve_bufA_h[(c, hh)])
            p.op("scalar", lambda e, hc=hc: e.activation(
                bufA[:, hc:hc + 2048], bufA[:, hc:hc + 2048], ACTF.Relu, bias=0.0, scale=1.0), inc="AC")
            p.op("scalar", lambda e, hc=hc: e.activation(
                bufA[:, hc:hc + 2048], bufA[:, hc:hc + 2048], ACTF.Exp, bias=0.0, scale=-1.0), inc="AC")
            ac_L[(c, hh)] = p.sems["AC"].n
        ac_expg[c] = p.sems["AC"].n

    def emitF2(c):
        if c >= NCH:
            return
        p.ewait(V, st_tok[c])
        if c >= 1:
            p.wait(V, "MM", mm_y[c - 1])
            p.wait("scalar", "MM", mm_y[c - 1])
            p.wait("scalar", "VE", ve_dec[c - 1])
        p.op("scalar", lambda e: e.activation(s_i2b, s_i2, ACTF.Copy, bias=0.0, scale=1.0), inc="AC")
        ac_si2b[c] = p.sems["AC"].n
        xts = xts1p if c % 2 == 0 else xts2p
        p.op(V, lambda e, c=c, xts=xts: e.tensor_tensor(
            xtq.rearrange("p (h q) -> p h q", h=NH),
            xts.rearrange("p (h q) -> p h q", h=NH),
            dtok[:, c * NH:(c + 1) * NH].unsqueeze(2).to_broadcast([128, NH, HD]), ALU.mult), inc="VE")
        if c + 1 < NCH:
            p.wait(V, "AC", ac_si2b[c])
            p.op(V, lambda e, c=c: e.tensor_tensor(
                s_i2.rearrange("p (h q) -> p h q", h=NH),
                s_i2.rearrange("p (h q) -> p h q", h=NH),
                crow_all[:, c * 32:(c + 1) * 32].unsqueeze(2).to_broadcast([128, NH, HD]), ALU.mult), inc="VE")
        ve_dec[c] = p.sems["VE"].n

    def emitMT(c):
        for hh in range(2):
            hc = hh * 2048
            p.wait(V, "AC", ac_expg_h[(c, hh)])
            p.op(V, lambda e, c=c, hc=hc: e.tensor_tensor(
                ct_b[:, hc:hc + 2048].rearrange("p (h l) -> p h l", h=16),
                Cf[:, c * C:(c + 1) * C].unsqueeze(1).to_broadcast([128, 16, C]),
                bufB[:, hc:hc + 2048].rearrange("p (h l) -> p h l", h=16), ALU.mult), inc="VE")
            ve_ct_h[(c, hh)] = p.sems["VE"].n
            p.wait(V, "AC", ac_L[(c, hh)])
            p.op(V, lambda e, hc=hc: e.tensor_tensor(
                mt_b[:, hc:hc + 2048].rearrange("p (h l) -> p h l", h=16),
                bufA[:, hc:hc + 2048].rearrange("p (h l) -> p h l", h=16),
                scm.unsqueeze(1).to_broadcast([128, 16, C]), ALU.mult), inc="VE")
            ve_mt_h[(c, hh)] = p.sems["VE"].n
        ve_ctb[c] = p.sems["VE"].n

    def emitScores(c):
        if c >= NCH:
            return
        if c >= 1:
            p.wait("tensor", "VE", ve_scm[c - 1])
        else:
            p.wait("tensor", "VE")
            p.wait("tensor", "AC")
        p.op("tensor", lambda e, c=c: e.matmul(
            PSB[:, 0:128], Bf[:, c * C:(c + 1) * C], Cf[:, c * C:(c + 1) * C],
            start=True, stop=True), inc="MM")
        mm_scores[c] = p.sems["MM"].n

    def emitY(c):
        slc = sl1 if c % 2 == 0 else sl2
        xts = xts1p if c % 2 == 0 else xts2p
        p.wait("tensor", "VE", ve_mt_h[(c, 0)])
        p.wait("tensor", "AC", ac_si2b[c])
        for t in range(16):
            if t == 8:
                p.wait("tensor", "VE", ve_mt_h[(c, 1)])
            bank = t // 4
            col0 = bank * 512 + (t % 4) * 128
            for hh in range(2):
                h = 2 * t + hh
                pvh = PSA[hh * 64:hh * 64 + 64, col0:col0 + 128]
                stt = (t % 4 == 0)
                sp = (t % 4 == 3)
                tp = (0, hh * 64)
                p.op("tensor", lambda e, h=h, pvh=pvh, stt=stt, tp=tp: e.matmul(
                    pvh, xtq[:, h * HD:(h + 1) * HD], mt_b[:, h * C:(h + 1) * C],
                    start=stt, stop=False, tile_position=tp, skip_group_check=True), inc=None)
                p.op("tensor", lambda e, h=h, pvh=pvh, slc=slc, tp=tp: e.matmul(
                    pvh, slc[:, h * HD:(h + 1) * HD], ct_b[:, h * C:(h + 1) * C],
                    start=False, stop=False, tile_position=tp, skip_group_check=True), inc=None)
                p.op("tensor", lambda e, h=h, pvh=pvh, tp=tp: e.matmul(
                    pvh, s_i2b[:, h * HD:(h + 1) * HD], ct_b[:, h * C:(h + 1) * C],
                    start=False, stop=False, tile_position=tp, skip_group_check=True), inc=None)
                p.op("tensor", lambda e, h=h, pvh=pvh, xts=xts, sp=sp, tp=tp: e.matmul(
                    pvh, xts[:, h * HD:(h + 1) * HD], dii[:, h * C:(h + 1) * C],
                    start=False, stop=sp, tile_position=tp, skip_group_check=True), inc="MM" if (sp and hh == 1) else None)
        mm_y[c] = p.sems["MM"].n

    def emitG(c):
        zsc = zs1 if c % 2 == 0 else zs2
        p.wait(V, "MM", mm_y[c])
        for bank in range(4):
            p.op(V, lambda e, bank=bank, zsc=zsc: e.tensor_tensor(
                graw[:, bank * 512:(bank + 1) * 512], PSA[:, bank * 512:(bank + 1) * 512],
                zsc[:, bank * 512:(bank + 1) * 512], ALU.mult), inc="VE")
        ve_graw[c] = p.sems["VE"].n
        p.op(V, lambda e: e.tensor_tensor(g2b[:, 0:1024], graw[:, 0:1024], graw[:, 0:1024], ALU.mult), inc="VE")
        ve_g2b_h[(c, 0)] = p.sems["VE"].n
        p.op(V, lambda e: e.tensor_tensor(g2b[:, 1024:2048], graw[:, 1024:2048], graw[:, 1024:2048], ALU.mult), inc="VE")
        ve_g2b[c] = p.sems["VE"].n

    def emitSumsq(c):
        p.wait("tensor", "VE", ve_g2b_h[(c, 0)])
        if c >= 1:
            p.wait("tensor", "AC", ac_rsq[c - 1])
        for tt in range(16):
            if tt == 8:
                p.wait("tensor", "VE", ve_g2b[c])
            p.op("tensor", lambda e, tt=tt: e.matmul(
                PSB[0:1, 672:800], onesb[:, :], g2b[:, tt * 128:(tt + 1) * 128],
                start=(tt == 0), stop=(tt == 15)), inc="MM" if tt == 15 else None)
        mm_sumsq[c] = p.sems["MM"].n
        p.wait("scalar", "MM", mm_sumsq[c])
        p.op("scalar", lambda e: e.activation(
            PSB[0:1, 672:800], PSB[0:1, 672:800], ACTF.Sqrt, bias=c1e5[0:1, 0:1], scale=1.0 / 2048.0), inc="AC")
        ac_rsq[c] = p.sems["AC"].n

    def emitTail(c):
        p.wait(V, "AC", ac_rsq[c])
        p.op(V, lambda e: e.reciprocal(rrow[:, :], PSB[0:1, 672:800]), inc="VE")
        ve_rrec[c] = p.sems["VE"].n
        p.wait("tensor", "VE", ve_rrec[c])
        p.op("tensor", lambda e: e.matmul(PSB[:, 800:928], onesrF[:, :], rrow[:, :], start=True, stop=True), inc="MM")
        mm_rsb[c] = p.sems["MM"].n
        p.wait(V, "MM", mm_rsb[c])
        if mm_op_last[0] is not None:
            p.wait(V, "MM", mm_op_last[0])
        p.op(V, lambda e, c=c: e.tensor_tensor(
            ggrp.rearrange("p (t g l) -> p t g l", t=16, g=4)[:, :, c % 4, :],
            graw.rearrange("p (t l) -> p t l", t=16),
            PSB[:, 800:928].unsqueeze(1).to_broadcast([128, 16, 128]), ALU.mult), inc="VE")
        ve_ggrp[c] = p.sems["VE"].n

    def bounce(c):
        if c >= 4:
            return
        for mi in (2 * c, 2 * c + 1):
            inb = wostg if mi % 2 == 0 else obuf2
            outb = wobf1 if mi % 2 == 0 else wobf2
            p.ewait(V, wi_tok[mi])
            if mi - 2 >= 0:
                p.ewait(V, wb_tok[mi - 2])
            p.op(V, lambda e, inb=inb, outb=outb: e.tensor_tensor(
                outb.rearrange("p (k c2) -> p k c2", k=16),
                inb.rearrange("p (k c2) -> p k c2", k=16),
                nw_s[:, :].unsqueeze(2).to_broadcast([128, 16, 128]), ALU.mult), inc="VE")
            ve_wfold[mi] = p.sems["VE"].n
            p.wait(G, "VE", ve_wfold[mi])
            p.dma(G, d_woutb.ap()[mi], outb)
            wb_tok[mi] = p.dclose("gpsimd")
        for mi in (2 * c + 2, 2 * c + 3):
            if mi < 8:
                p.wait(S, "VE", ve_wfold[mi - 2])
                p.dma(S, wostg if mi % 2 == 0 else obuf2, d_wout.ap()[mi].rearrange("p k c -> p (k c)"))
                wi_tok[mi] = p.dclose("sync")

    def wo_dma_issue(gidx, m, wo_tok, mm_op):
        if m >= 8:
            return
        slot = m % 4
        if gidx == 0:
            p.ewait(S, wb_tok[6])
            p.ewait(S, wb_tok[7])
            p.ewait(S, wb_tok[m])
            if slot >= 2:
                p.wait(S, "VE", ve_wfold[6])
                p.wait(G, "VE", ve_wfold[6])
        if m - 4 in mm_op:
            p.wait(S, "MM", mm_op[m - 4])
        elif slot in mm_wo_slot:
            p.wait(S, "MM", mm_wo_slot[slot])
        wob = WOBS[slot]
        if slot % 2 == 1:
            if m - 4 in mm_op:
                p.wait(G, "MM", mm_op[m - 4])
            elif slot in mm_wo_slot:
                p.wait(G, "MM", mm_wo_slot[slot])
            p.dma(G, wob, d_woutb.ap()[m])
            wo_tok[m] = [p.dclose("gpsimd")]
        else:
            p.dma(S, wob, d_woutb.ap()[m])
            wo_tok[m] = [p.dclose("sync")]

    def emitOutproj(c):
        gidx = c // 4
        mm_op = {}
        if ("tok", gidx) in wo_pre:
            wo_tok = wo_pre.pop(("tok", gidx))
        else:
            wo_tok = {}
            for mq in range(4):
                wo_dma_issue(gidx, mq, wo_tok, mm_op)

        def wo_dma(m):
            wo_dma_issue(gidx, m, wo_tok, mm_op)
        for m in range(8):
            wob = WOBS[m % 4]
            for tk in wo_tok[m]:
                p.ewait("tensor", tk)
            if m == 0:
                p.wait("tensor", "VE", ve_ggrp[c])
            if (m % 2) in ve_oc_slot:
                p.wait("tensor", "VE", ve_oc_slot[m % 2])
            pv = PSB[:, 1024 + (m % 2) * 512:1024 + (m % 2) * 512 + 512]
            for k in range(16):
                p.op("tensor", lambda e, k=k, pv=pv, wob=wob: e.matmul(
                    pv, wob[:, k * 128:(k + 1) * 128], ggrp[:, k * 512:(k + 1) * 512],
                    start=(k == 0), stop=(k == 15)), inc="MM" if k == 15 else None)
            mm_op[m] = p.sems["MM"].n
            mm_wo_slot[m % 4] = mm_op[m]
            wo_dma(m + 4)
            p.wait(V, "MM", mm_op[m])
            if m == 0:
                p.ewait(V, out_tok[1])
            if m == 4:
                p.ewait(V, out_tok[0])
            p.op(V, lambda e, m=m, pv=pv: e.tensor_copy(obuf2[:, (m % 4) * 512:(m % 4) * 512 + 512], pv), inc="VE")
            ve_oc_slot[m % 2] = p.sems["VE"].n
            if m % 4 == 3:
                p.wait(G, "VE", ve_oc_slot[m % 2])
                p.dma(G, bass.AP(d_out, gidx * 512 + (m // 4) * 4 * 128 * T, [[T, 128], [128 * T, 4], [1, 512]]),
                      obuf2.rearrange("p (m l) -> p m l", m=4), accum=True)
                out_tok[m // 4] = p.dclose("gpsimd")
        mm_op_last[0] = mm_op[7]

    WOBS = [wobf1, wobf2, wos3, wos4]
    streams(0)
    streams(1)
    relb_pre(0)
    for mi in range(2):
        p.dma(S, wostg if mi % 2 == 0 else obuf2, d_wout.ap()[mi].rearrange("p k c -> p (k c)"))
        wi_tok[mi] = p.dclose("sync")
    emitScores(0)
    emitF1(0)
    emitAC(0)
    relb_pre(1)
    emitF2(0)
    for c in range(NCH):
        bounce(c)
        emitMT(c)
        emitScores(c + 1)
        emitY(c)
        emitF1(c + 1)
        emitAC(c + 1)
        relb_pre(c + 2)
        emitG(c)
        streams(c + 2)
        emitSumsq(c)
        emitF2(c + 1)
        if c >= 1 and (c - 1) % 4 == 3:
            emitOutproj(c - 1)
        emitTail(c)
        if c % 4 == 2 and c // 4 >= 1:
            wo_tok_pre = {}
            wo_pre[("tok", c // 4)] = wo_tok_pre
            for mq in range(4):
                wo_dma_issue(c // 4, mq, wo_tok_pre, {})
    emitOutproj(NCH - 1)
    p.barrier("GD", "LD", "VE", "MM", "AC", "GP")
    p.dma(G, d_out.ap(), d_u.ap()[:, H:TW], accum=True)
    p.dma_barrier(S)
    for snm in ("VE", "MM", "AC", "GP", "CC"):
        p.wait(S, snm)

    p.build()
    p.es.close()
    return nc


def kernel(**inputs):
    u = np.ascontiguousarray(inputs["L0_attn_u"])
    win = inputs["L0_attn_in_proj_w"]
    w1 = inputs["L0_ffn_w1"]
    wout = inputs["L0_attn_out_proj_w"]
    w1m = np.ascontiguousarray(w1.reshape(KT, 128, 32, 128).transpose(2, 1, 0, 3))
    woutm = np.ascontiguousarray(wout.reshape(16, 128, 8, 128).transpose(2, 1, 0, 3))
    in_maps = []
    for core in range(8):
        b, q = core // 4, core % 4
        t0 = q * T
        halo = np.zeros((H, D), np.float32) if q == 0 else u[b, t0 - H:t0]
        u_r = np.ascontiguousarray(np.concatenate([halo, u[b, t0:t0 + T]], axis=0).T)
        sel = np.zeros((8, 1), np.float32)
        btwT = np.zeros((8, 8), np.float32)
        for r in range(8):
            rb, rq = r // 4, r % 4
            if rb == b and rq < q:
                sel[r, 0] = 1.0
            for j in range(8):
                jb, jq = j // 4, j % 4
                if jb == b and rb == b and rq < jq < q:
                    btwT[j, r] = 1.0
        in_maps.append({
            "u_r": u_r, "win": win, "convw": inputs["L0_attn_conv_w"],
            "convb": inputs["L0_attn_conv_b"], "dtb": inputs["L0_attn_dt_bias"],
            "alog": inputs["L0_attn_A_log"], "dpar": inputs["L0_attn_D_param"],
            "normw": inputs["L0_attn_norm_w"], "w1m": w1m, "b1": inputs["L0_ffn_b1"],
            "w2": inputs["L0_ffn_w2"], "b2": inputs["L0_ffn_b2"], "woutm": woutm,
            "sel": sel, "btwT": btwT,
        })
    nc = build_nc()
    res = run_bass_kernel_spmd(nc, in_maps, core_ids=list(range(8)))
    out = np.zeros((2, 8192, D), np.float32)
    for core in range(8):
        b, q = core // 4, core % 4
        out[b, q * T:(q + 1) * T] = np.ascontiguousarray(res.results[core]["out"]).T
    return out




# version 34 (final)
# speedup vs baseline: 1.0373x; 1.0161x over previous
import numpy as np
from contextlib import ExitStack

import concourse.bass as bass
import concourse.mybir as mybir
from concourse.bass_utils import run_bass_kernel_spmd

F32 = mybir.dt.float32
BF16 = mybir.dt.bfloat16
I32 = mybir.dt.int32
ALU = mybir.AluOpType
ACTF = mybir.ActivationFunctionType

ENGINES = ("sync", "gpsimd", "vector", "scalar", "tensor")

T = 2048
H = 3
TW = T + H
D = 1024
KT = 8
INTER = 2048
NH, HD, DS = 32, 64, 128
NCH, C = 16, 128
FF = 4096


class Sem:
    def __init__(self, h):
        self.h = h
        self.n = 0


class Prog:
    def __init__(self, nc):
        self.nc = nc
        self.streams = {e: [] for e in ENGINES}
        self.es = ExitStack()
        self.sems = {}
        names = ["VE", "AC", "MM", "GP", "CC"]
        names += ["LQ%d" % i for i in range(8)] + ["GQ%d" % i for i in range(8)]
        for name in names:
            self.sems[name] = Sem(self.es.enter_context(nc.semaphore(name)))
        self.lastw = {(e, s): 0 for e in ENGINES for s in self.sems}
        self.qstate = {
            "sync": {"pool": ["LQ%d" % i for i in range(8)], "epoch": 0, "cur": None, "last": None},
            "gpsimd": {"pool": ["GQ%d" % i for i in range(8)], "epoch": 0, "cur": None, "last": None},
        }

    def sb(self, name, shape, dt):
        return self.es.enter_context(self.nc.sbuf_tensor(name, shape, dt))

    def emit(self, eng, fn):
        self.streams[eng].append(fn)

    def wait(self, eng, sname, val=None):
        s = self.sems[sname]
        v = s.n if val is None else val
        if v <= self.lastw[(eng, sname)]:
            return
        self.lastw[(eng, sname)] = v
        self.emit(eng, lambda e, h=s.h, v=v: e.wait_ge(h, v))

    ENG_SEM = {"vector": "VE", "scalar": "AC", "gpsimd": "GP"}

    def op(self, eng, fn, inc=None, n=1):
        own = self.ENG_SEM.get(eng)
        if own is not None:
            self.wait(eng, own)
            inc = own
        if inc is not None:
            s = self.sems[inc]
            s.n += n
            self.emit(eng, lambda e, f=fn, h=s.h, m=n: f(e).then_inc(h, m))
        else:
            self.emit(eng, fn)

    def dma(self, eng, out, in_, accum=False):
        q = "gpsimd" if eng == "gpsimd" else "sync"
        st = self.qstate[q]
        if eng == "gpsimd":
            self.wait(eng, "GP")
        if st["cur"] is None:
            nm = st["pool"][st["epoch"] % len(st["pool"])]
            st["epoch"] += 1
            st["cur"] = nm
            self.wait(eng, nm)
        s = self.sems[st["cur"]]
        s.n += 16
        if accum:
            self.emit(eng, lambda e, o=out, i=in_, h=s.h: e.dma_start(
                out=o, in_=i, accum_op=ALU.add).then_inc(h, 16))
        else:
            self.emit(eng, lambda e, o=out, i=in_, h=s.h: e.dma_start(
                out=o, in_=i).then_inc(h, 16))

    def dclose(self, q):
        st = self.qstate[q]
        if st["cur"] is not None:
            st["last"] = (st["cur"], self.sems[st["cur"]].n)
            st["cur"] = None
        return st["last"]

    def ewait(self, eng, token):
        if token is None:
            return
        nm, tot = token
        self.wait(eng, nm, tot)

    def dwait(self, eng, q):
        self.ewait(eng, self.dclose(q))

    def dma_barrier(self, *engines):
        self.dclose("sync")
        self.dclose("gpsimd")
        engs = engines if engines else ENGINES
        for e in engs:
            for nm in self.qstate["sync"]["pool"] + self.qstate["gpsimd"]["pool"]:
                self.wait(e, nm)

    def barrier(self, *producers):
        prs = list(producers)
        if "LD" in prs or "GD" in prs:
            self.dma_barrier()
            prs = [x for x in prs if x not in ("LD", "GD")]
        for e in ENGINES:
            for pr in prs:
                self.wait(e, pr)

    def build(self):
        with self.nc.allow_non_contiguous_dma("small strided param loads"), self.nc.Block() as block:
            @block.sync
            def _(e):
                for f in self.streams["sync"]:
                    f(e)

            @block.gpsimd
            def _(e):
                for f in self.streams["gpsimd"]:
                    f(e)

            @block.vector
            def _(e):
                for f in self.streams["vector"]:
                    f(e)

            @block.scalar
            def _(e):
                for f in self.streams["scalar"]:
                    f(e)

            @block.tensor
            def _(e):
                for f in self.streams["tensor"]:
                    f(e)


def _carve(arena, dt, lo_bytes, n_elems, parts=128):
    esz = {F32: 4, BF16: 2, I32: 4}[dt]
    assert lo_bytes % esz == 0
    ap = arena[:, :].bitcast(dt)
    lo = lo_bytes // esz
    return ap[0:parts, lo:lo + n_elems]


def build_nc():
    nc = bass.Bass("TRN2", target_bir_lowering=False)
    d_u = nc.dram_tensor("u_r", [D, TW], F32, kind="ExternalInput")
    d_win = nc.dram_tensor("win", [D, 4384], F32, kind="ExternalInput")
    d_convw = nc.dram_tensor("convw", [2304, 4], F32, kind="ExternalInput")
    d_convb = nc.dram_tensor("convb", [2304], F32, kind="ExternalInput")
    d_dtb = nc.dram_tensor("dtb", [NH], F32, kind="ExternalInput")
    d_alog = nc.dram_tensor("alog", [NH], F32, kind="ExternalInput")
    d_dpar = nc.dram_tensor("dpar", [NH], F32, kind="ExternalInput")
    d_normw = nc.dram_tensor("normw", [INTER], F32, kind="ExternalInput")
    d_w1 = nc.dram_tensor("w1m", [32, 128, KT, 128], F32, kind="ExternalInput")
    d_b1 = nc.dram_tensor("b1", [FF], F32, kind="ExternalInput")
    d_w2 = nc.dram_tensor("w2", [FF, D], F32, kind="ExternalInput")
    d_w1b = nc.dram_tensor("w1b_d", [32, 128, 1024], BF16, kind="Internal")
    d_woutb = nc.dram_tensor("woutb_d", [8, 128, 2048], BF16, kind="Internal")
    d_b2 = nc.dram_tensor("b2", [D], F32, kind="ExternalInput")
    d_wout = nc.dram_tensor("woutm", [8, 128, 16, 128], F32, kind="ExternalInput")
    d_sel = nc.dram_tensor("sel", [8, 1], F32, kind="ExternalInput")
    d_btwT = nc.dram_tensor("btwT", [8, 8], F32, kind="ExternalInput")
    d_out = nc.dram_tensor("out", [D, T], F32, kind="ExternalOutput")
    d_zsil = nc.dram_tensor("zsil_d", [NCH, 16, 128, C], BF16, kind="Internal")
    d_xtok = nc.dram_tensor("xtok_d", [NCH, 128, INTER], BF16, kind="Internal")
    d_sall = nc.dram_tensor("sall_d", [NCH, 128, INTER], BF16, kind="Internal")
    d_rel = nc.dram_tensor("rel_d", [NCH, NH * C], F32, kind="Internal")
    d_scb = nc.dram_tensor("scb_d", [8, INTER], F32, kind="Internal")
    d_ccin = nc.dram_tensor("cc_in", [129, INTER], F32, kind="Internal")
    d_ccout = nc.dram_tensor("cc_out", [8 * 129, INTER], F32, kind="Internal", addr_space="Shared")

    p = Prog(nc)
    sb = p.sb
    xn = sb("xn", [128, KT * TW], BF16)
    AR1 = sb("AR1", [128, 16384], F32)
    AR2 = sb("AR2", [128, 18459], F32)
    Bf = sb("Bf", [128, T], BF16)
    Cf = sb("Cf", [128, T], BF16)
    Bt = sb("Bt", [128, NCH * C], BF16)
    dt_s = sb("dt_s", [NH, T], F32)
    ga = sb("ga", [NH, T], F32)
    dtok = sb("dtok", [128, NCH * NH], F32)
    rtok = sb("rtok", [128, NCH * NH], F32)
    identF = sb("identF", [128, 128], F32)
    maskle = sb("maskle", [128, 128], F32)
    onesb = sb("onesb", [128, 1], BF16)
    onesrB = sb("onesrB", [1, 128], BF16)
    onesrF = sb("onesrF", [1, 128], F32)
    convw_s = sb("convw_s", [128, 72], F32)
    convb_s = sb("convb_s", [128, 18], F32)
    dtb_s = sb("dtb_s", [NH, 1], F32)
    acol = sb("acol", [NH, 1], F32)
    b1c = sb("b1c", [128, 32], F32)
    b2c = sb("b2c", [128, 8], F32)
    c155 = sb("c155", [128, 1], F32)
    c1e5 = sb("c1e5", [128, 1], F32)
    nw_s = sb("nw_s", [128, 16], F32)
    sel_s = sb("sel_s", [8, 1], F32)
    btw_s = sb("btw_s", [8, 8], F32)
    zcol = sb("zcol", [NH, 1], F32)
    gaex = sb("gaex", [NH, NCH], F32)
    cdee = sb("cdee", [NH, NCH], F32)
    crow = sb("crow", [1, 32], F32)
    rrow = sb("rrow", [1, 128], F32)
    smallF = sb("smallF", [NH, C], F32)
    smallF2 = sb("smallF2", [NH, C], F32)
    db_s = sb("db_s", [128, NH], F32)
    gls = sb("gls", [8, 32], F32)
    esc = sb("esc", [8, 32], F32)
    PSA = p.es.enter_context(nc.psum_tensor("PSA", [128, 2048], F32))
    PSB = p.es.enter_context(nc.psum_tensor("PSB", [128, 2048], F32))

    S = "sync"
    G = "gpsimd"

    p.dma(S, convw_s[:, :].rearrange("p (ct k) -> p ct k", ct=18),
          bass.AP(d_convw, 0, [[4, 128], [512, 18], [1, 4]]))
    p.dma(S, convb_s[:, :], bass.AP(d_convb, 0, [[1, 128], [128, 18]]))
    p.dma(S, dtb_s[:, :], d_dtb.ap().unsqueeze(1))
    p.dma(S, acol[:, :], d_alog.ap().unsqueeze(1))
    p.dma(S, b1c[:, :], bass.AP(d_b1, 0, [[1, 128], [128, 32]]))
    p.dma(S, b2c[:, :], bass.AP(d_b2, 0, [[1, 128], [128, 8]]))
    p.dma(S, nw_s[:, :], bass.AP(d_normw, 0, [[1, 128], [128, 16]]))
    p.dma(S, sel_s[:, :], d_sel.ap())
    p.dma(S, btw_s[:, :], d_btwT.ap())
    p.dma(S, db_s[:, :], d_dpar.ap().unsqueeze(0).unsqueeze(1).to_broadcast([1, 128, NH]))
    iot_p = _carve(AR2, F32, 0, 128)
    iot_f = _carve(AR2, F32, 512, 128)
    p.op(G, lambda e: e.iota(iot_p, [[0, 128]], channel_multiplier=1,
                             allow_small_or_imprecise_dtypes=True), inc="GP")
    p.op(G, lambda e: e.iota(iot_f, [[1, 128]], channel_multiplier=0,
                             allow_small_or_imprecise_dtypes=True), inc="GP")
    p.wait("vector", "GP")
    p.dwait("vector", "sync")
    V = "vector"
    p.op(V, lambda e: e.tensor_tensor(identF[:, :], iot_p, iot_f, ALU.is_equal), inc="VE")
    p.op(V, lambda e: e.tensor_tensor(maskle[:, :], iot_p, iot_f, ALU.is_le), inc="VE")
    p.op(V, lambda e: e.tensor_scalar(onesb[:, :], identF[:, 0:1], 0.0, 1.0, ALU.mult, ALU.add), inc="VE")
    p.op(V, lambda e: e.tensor_scalar(onesrB[:, :], identF[0:1, :], 0.0, 1.0, ALU.mult, ALU.add), inc="VE")
    p.op(V, lambda e: e.tensor_scalar(onesrF[:, :], identF[0:1, :], 0.0, 1.0, ALU.mult, ALU.add), inc="VE")
    p.op(V, lambda e: e.tensor_scalar(zcol[:, :], dtb_s[:, :], 0.0, None, ALU.mult), inc="VE")
    p.op(V, lambda e: e.tensor_scalar(b1c[:, :], b1c[:, :], 256.0, None, ALU.mult), inc="VE")
    p.op(V, lambda e: e.tensor_scalar(b2c[:, :], b2c[:, :], 256.0, None, ALU.mult), inc="VE")
    p.op(V, lambda e: e.tensor_scalar(c155[:, :], identF[:, 0:1], 0.0, -0.5, ALU.mult, ALU.add), inc="VE")
    p.op(V, lambda e: e.tensor_scalar(c1e5[:, :], identF[:, 0:1], 0.0, 1e-5, ALU.mult, ALU.add), inc="VE")
    p.ewait("scalar", p.qstate["sync"]["last"])
    p.op("scalar", lambda e: e.activation(acol[:, :], acol[:, :], ACTF.Exp, bias=0.0, scale=1.0), inc="AC")
    p.wait(V, "AC")
    p.op(V, lambda e: e.tensor_scalar(acol[:, :], acol[:, :], -1.0, None, ALU.mult), inc="VE")

    u_t = [_carve(AR1, F32, k * TW * 4, TW) for k in range(7)]
    u_t.append(_carve(AR2, F32, 1024 * 4, TW))
    ub_t = [_carve(AR2, BF16, 16384, TW * KT)[:, k * TW:(k + 1) * TW] for k in range(KT)]
    u2b = _carve(AR2, BF16, 49200, TW)
    xsc = _carve(AR2, F32, 53600, 512)
    mrowS = _carve(AR2, F32, 55648, 512, parts=1)
    rrowS = _carve(AR2, F32, 57696, 512, parts=1)
    for k in range(KT):
        p.dma(S, u_t[k], d_u.ap()[k * 128:(k + 1) * 128, :])
    p.barrier("LD")
    for k in range(KT):
        p.op(V, lambda e, k=k: e.tensor_copy(ub_t[k], u_t[k]), inc="VE")
    ncs = [(0, 512), (512, 512), (1024, 512), (1536, 512), (2048, 3)]
    mrowS2 = _carve(AR2, F32, 59744, 512, parts=1)
    rrowS2 = _carve(AR2, F32, 61792, 512, parts=1)
    ve_ub = p.sems["VE"].n
    ac_sq, mm_stats, ve_var, ac_sqrt, ve_recip, mm_bcast, ve_xn = {}, {}, {}, {}, {}, {}, {}
    u2bB = _carve(AR2, BF16, 63840, 4096)

    def emit_squares(j):
        if j >= len(ncs):
            return
        c0, w = ncs[j]
        if j >= 1:
            p.wait("scalar", "MM", mm_stats[j - 1])
        for k in range(KT):
            p.op("scalar", lambda e, k=k, c0=c0, w=w: e.activation(
                u2bB[:, k * w:(k + 1) * w], u_t[k][:, c0:c0 + w], ACTF.Square, bias=0.0, scale=1.0), inc="AC")
        ac_sq[j] = p.sems["AC"].n

    emit_squares(0)
    for j, (c0, w) in enumerate(ncs):
        sb0 = (j % 2) * 1024
        pa0 = (j % 2) * 1024
        mro = mrowS if j % 2 == 0 else mrowS2
        rro = rrowS if j % 2 == 0 else rrowS2
        p.wait("tensor", "AC", ac_sq[j])
        p.wait("tensor", "VE", ve_ub)
        if j - 2 >= 0:
            p.wait("tensor", "VE", ve_recip[j - 2])
        for k in range(KT):
            p.op("tensor", lambda e, k=k, c0=c0, w=w, sb0=sb0: e.matmul(
                PSB[0:1, sb0:sb0 + w], onesb[:, :], ub_t[k][:, c0:c0 + w],
                start=(k == 0), stop=(k == KT - 1)), inc="MM")
            p.op("tensor", lambda e, k=k, w=w, sb0=sb0: e.matmul(
                PSB[0:1, sb0 + 512:sb0 + 512 + w], onesb[:, :], u2bB[:, k * w:(k + 1) * w],
                start=(k == 0), stop=(k == KT - 1)), inc="MM")
        mm_stats[j] = p.sems["MM"].n
        emit_squares(j + 1)
        p.wait(V, "MM", mm_stats[j])
        p.op(V, lambda e, w=w, sb0=sb0, mro=mro: e.tensor_scalar(
            mro[:, 0:w], PSB[0:1, sb0:sb0 + w], 1.0 / 1024.0, None, ALU.mult), inc="VE")
        p.op(V, lambda e, w=w, mro=mro, rro=rro: e.tensor_tensor(rro[:, 0:w], mro[:, 0:w], mro[:, 0:w], ALU.mult), inc="VE")
        p.op(V, lambda e, w=w, sb0=sb0, rro=rro: e.scalar_tensor_tensor(
            PSB[0:1, sb0 + 512:sb0 + 512 + w], PSB[0:1, sb0 + 512:sb0 + 512 + w], 1.0 / 1024.0,
            rro[:, 0:w], ALU.mult, ALU.subtract), inc="VE")
        p.op(V, lambda e, w=w, sb0=sb0: e.tensor_scalar(
            PSB[0:1, sb0 + 512:sb0 + 512 + w], PSB[0:1, sb0 + 512:sb0 + 512 + w], 1e-5, None, ALU.add), inc="VE")
        ve_var[j] = p.sems["VE"].n
        p.wait("scalar", "VE", ve_var[j])
        p.op("scalar", lambda e, w=w, sb0=sb0: e.activation(
            PSB[0:1, sb0 + 512:sb0 + 512 + w], PSB[0:1, sb0 + 512:sb0 + 512 + w], ACTF.Sqrt, bias=0.0, scale=1.0), inc="AC")
        ac_sqrt[j] = p.sems["AC"].n
        p.wait(V, "AC", ac_sqrt[j])
        p.op(V, lambda e, w=w, sb0=sb0, rro=rro: e.reciprocal(rro[:, 0:w], PSB[0:1, sb0 + 512:sb0 + 512 + w]), inc="VE")
        p.op(V, lambda e, w=w, mro=mro, rro=rro: e.tensor_tensor(mro[:, 0:w], mro[:, 0:w], rro[:, 0:w], ALU.mult), inc="VE")
        ve_recip[j] = p.sems["VE"].n
        p.wait("tensor", "VE", ve_recip[j])
        if j - 2 >= 0:
            p.wait("tensor", "VE", ve_xn[j - 2])
        p.op("tensor", lambda e, w=w, pa0=pa0, mro=mro: e.matmul(
            PSA[:, pa0:pa0 + w], onesrF[:, :], mro[:, 0:w], start=True, stop=True), inc=None)
        p.op("tensor", lambda e, w=w, pa0=pa0, rro=rro: e.matmul(
            PSA[:, pa0 + 512:pa0 + 512 + w], onesrF[:, :], rro[:, 0:w], start=True, stop=True), inc="MM")
        mm_bcast[j] = p.sems["MM"].n
        p.wait(V, "MM", mm_bcast[j])
        for k in range(KT):
            p.op(V, lambda e, k=k, c0=c0, w=w, pa0=pa0: e.tensor_tensor(
                xsc[:, 0:w], u_t[k][:, c0:c0 + w], PSA[:, pa0 + 512:pa0 + 512 + w], ALU.mult), inc="VE")
            p.op(V, lambda e, k=k, c0=c0, w=w, pa0=pa0: e.tensor_tensor(
                xn[:, k * TW + c0:k * TW + c0 + w], xsc[:, 0:w], PSA[:, pa0:pa0 + w], ALU.subtract), inc="VE")
        ve_xn[j] = p.sems["VE"].n
    p.barrier("VE")

    xbc = _carve(AR2, BF16, 0, 18 * TW)
    winbc = _carve(AR1, BF16, 0, KT * 2336)
    wst1 = _carve(AR1, F32, 37376, 2336)
    wst2 = _carve(AR1, F32, 46720, 2336)
    wtok, ve_wc = {}, {}

    def win_dma(k):
        if k >= KT:
            return
        if k - 2 >= 0:
            p.wait(S, "VE", ve_wc[k - 2])
        st = wst1 if k % 2 == 0 else wst2
        toks = []
        for h in range(2):
            p.dma(S, st[:, h * 1168:(h + 1) * 1168],
                  d_win.ap()[k * 128:(k + 1) * 128, 2048 + h * 1168:2048 + (h + 1) * 1168])
            toks.append(p.dclose("sync"))
        wtok[k] = toks

    p.wait(S, "VE")
    win_dma(0)
    win_dma(1)
    for k in range(KT):
        st = wst1 if k % 2 == 0 else wst2
        for tk_ in wtok[k]:
            p.ewait(V, tk_)
        p.op(V, lambda e, k=k, st=st: e.tensor_copy(winbc[:, k * 2336:(k + 1) * 2336], st), inc="VE")
        ve_wc[k] = p.sems["VE"].n
        win_dma(k + 2)
    p.wait("tensor", "VE", ve_wc[KT - 1])
    ve_slot = {}
    for mt in range(18):
        for j, (c0, w) in enumerate(ncs):
            sl = j % 4
            pv = PSA[:, sl * 512:sl * 512 + w]
            if sl in ve_slot:
                p.wait("tensor", "VE", ve_slot[sl])
            for k in range(KT):
                p.op("tensor", lambda e, k=k, mt=mt, c0=c0, w=w, pv=pv: e.matmul(
                    pv, winbc[:, k * 2336 + mt * 128:k * 2336 + mt * 128 + 128],
                    xn[:, k * TW + c0:k * TW + c0 + w],
                    start=(k == 0), stop=(k == KT - 1)), inc="MM" if k == KT - 1 else None)
            p.wait(V, "MM", p.sems["MM"].n)
            p.op(V, lambda e, mt=mt, c0=c0, w=w, pv=pv: e.tensor_copy(
                xbc[:, mt * TW + c0:mt * TW + c0 + w], pv), inc="VE")
            ve_slot[sl] = p.sems["VE"].n
    dtraw = PSB[0:32, 0:2048]
    for j, (c0, w) in enumerate([(3, 512), (515, 512), (1027, 512), (1539, 512)]):
        sl = j % 2
        pv = PSA[0:32, sl * 512:sl * 512 + w]
        if sl in ve_slot:
            p.wait("tensor", "VE", ve_slot[sl])
        for k in range(KT):
            p.op("tensor", lambda e, k=k, c0=c0, w=w, pv=pv: e.matmul(
                pv, winbc[:, k * 2336 + 2304:k * 2336 + 2336],
                xn[:, k * TW + c0:k * TW + c0 + w],
                start=(k == 0), stop=(k == KT - 1)), inc="MM" if k == KT - 1 else None)
        p.wait(V, "MM", p.sems["MM"].n)
        p.op(V, lambda e, j=j, c0=c0, w=w, pv=pv: e.tensor_copy(dtraw[:, c0 - 3:c0 - 3 + w], pv), inc="VE")
        ve_slot[sl] = p.sems["VE"].n
    p.barrier("VE", "MM")
    ab_t = PSA[0:32, 0:2048]
    p.op(V, lambda e: e.tensor_scalar(dt_s[:, :], dtraw, dtb_s[:, :], 0.0, ALU.add, ALU.max), inc="VE")
    p.wait("scalar", "VE")
    p.op("scalar", lambda e: e.activation(ab_t, dtraw, ACTF.Abs, bias=dtb_s[:, :], scale=1.0), inc="AC")
    p.op("scalar", lambda e: e.activation(ab_t, ab_t, ACTF.Exp, bias=0.0, scale=-1.0), inc="AC")
    p.wait(V, "AC")
    p.op(V, lambda e: e.tensor_scalar(ab_t, ab_t, 1.0, None, ALU.add), inc="VE")
    p.wait("scalar", "VE")
    p.op("scalar", lambda e: e.activation(ab_t, ab_t, ACTF.Ln, bias=0.0, scale=1.0), inc="AC")
    p.wait(V, "AC")
    p.op(V, lambda e: e.tensor_tensor(dt_s[:, :], dt_s[:, :], ab_t, ALU.add), inc="VE")
    p.op(V, lambda e: e.tensor_scalar(ab_t, dt_s[:, :], acol[:, :], None, ALU.mult), inc="VE")
    p.op(V, lambda e: e.tensor_tensor_scan(ga[:, :], ab_t, dt_s[:, :], 0.0, ALU.add, ALU.bypass), inc="VE")
    p.op(V, lambda e: e.tensor_copy(gaex[:, :], bass.AP(ga, 127, [[T, NH], [128, NCH]])), inc="VE")
    p.op(V, lambda e: e.memset(smallF[:, 0:NCH], 0.0), inc="VE")
    p.op(V, lambda e: e.tensor_copy(smallF[:, 1:NCH], gaex[:, 0:NCH - 1]), inc="VE")
    p.op(V, lambda e: e.tensor_tensor(cdee[:, :], gaex[:, :], smallF[:, 0:NCH], ALU.subtract), inc="VE")
    p.wait("scalar", "VE")
    p.op("scalar", lambda e: e.activation(cdee[:, :], cdee[:, :], ACTF.Exp, bias=0.0, scale=1.0), inc="AC")
    p.barrier("AC", "VE")

    zbuf1 = _carve(AR1, BF16, 51200, 2048)
    zbuf2 = _carve(AR1, BF16, 47104, 2048)
    zncs = [(3, 512), (515, 512), (1027, 512), (1539, 512)]
    winz1 = _carve(AR1, BF16, 0, KT * 1024)
    winz2 = _carve(AR1, BF16, 24576, KT * 1024)
    zst1 = _carve(AR1, F32, 16384, 1024)
    zst2 = _carve(AR1, F32, 20480, 1024)
    ztok, ve_zc = {}, {}

    def z_dma(kk):
        if kk >= 2 * KT:
            return
        if kk - 2 >= 0:
            p.wait(S, "VE", ve_zc[kk - 2])
        p.dma(S, zst1 if kk % 2 == 0 else zst2,
              d_win.ap()[(kk % 8) * 128:(kk % 8) * 128 + 128, (kk // 8) * 1024:(kk // 8) * 1024 + 1024])
        ztok[kk] = p.dclose("sync")

    p.wait(S, "MM")
    p.wait(S, "VE")
    z_dma(0)
    z_dma(1)
    for kk in range(2 * KT):
        st = zst1 if kk % 2 == 0 else zst2
        wz = winz1 if kk < 8 else winz2
        p.ewait(V, ztok[kk])
        if kk == 0:
            p.wait(V, "MM")
        p.op(V, lambda e, kk=kk, st=st, wz=wz: e.tensor_copy(wz[:, (kk % 8) * 1024:(kk % 8) * 1024 + 1024], st), inc="VE")
        ve_zc[kk] = p.sems["VE"].n
        z_dma(kk + 2)
    ac_zslot = {}
    zs_tok = {}
    for half in range(2):
        wz = winz1 if half == 0 else winz2
        p.wait("tensor", "VE", ve_zc[half * 8 + 7])
        for mi in range(8):
            mt = half * 8 + mi
            zb = zbuf1 if mt % 2 == 0 else zbuf2
            if mt - 2 in zs_tok:
                p.ewait("scalar", zs_tok[mt - 2])
            for j, (c0, w) in enumerate(zncs):
                sl = j % 4
                pv = PSA[:, sl * 512:sl * 512 + w]
                if sl in ac_zslot:
                    p.wait("tensor", "AC", ac_zslot[sl])
                for k in range(KT):
                    p.op("tensor", lambda e, k=k, mi=mi, c0=c0, w=w, pv=pv, wz=wz: e.matmul(
                        pv, wz[:, k * 1024 + mi * 128:k * 1024 + mi * 128 + 128],
                        xn[:, k * TW + c0:k * TW + c0 + w],
                        start=(k == 0), stop=(k == KT - 1)), inc="MM" if k == KT - 1 else None)
                p.wait("scalar", "MM", p.sems["MM"].n)
                p.op("scalar", lambda e, j=j, w=w, pv=pv, zb=zb: e.activation(
                    zb[:, j * 512:j * 512 + w], pv, ACTF.Silu, bias=0.0, scale=1.0), inc="AC")
                ac_zslot[sl] = p.sems["AC"].n
            p.wait(G, "AC", ac_zslot[3])
            p.dma(G, bass.AP(d_zsil, mt * 128 * C, [[C, 128], [16 * 128 * C, NCH], [1, C]]),
                  zb[:, :].rearrange("p (c l) -> p c l", c=NCH))
            zs_tok[mt] = p.dclose("gpsimd")
    p.barrier("GD", "MM", "AC", "VE")

    diag = _carve(AR1, BF16, 0, 72 * 128)
    xtbuf1 = _carve(AR1, BF16, 18432, 2048)
    xtbuf2 = _carve(AR1, BF16, 36352, 2048)
    cbRf = _carve(AR1, F32, 22528, 2304, parts=1)
    cbR = _carve(AR1, BF16, 31744, 2304, parts=1)
    p.dma(S, cbRf, d_convb.ap().unsqueeze(0))
    cb_tok = p.dclose("sync")
    p.ewait(V, cb_tok)
    p.wait(V, "MM")
    p.op(V, lambda e: e.tensor_copy(cbR, cbRf), inc="VE")
    for ct in range(18):
        for tk in range(4):
            p.op(V, lambda e, ct=ct, tk=tk: e.tensor_scalar(
                diag[:, (ct * 4 + tk) * 128:(ct * 4 + tk) * 128 + 128],
                identF[:, :], convw_s[:, ct * 4 + tk:ct * 4 + tk + 1], None, ALU.mult), inc="VE")
    ve_diag = p.sems["VE"].n
    p.wait("tensor", "VE", ve_diag)
    ac_cgrp = {}
    xt_dtok = {}
    ac_B = None
    for c in range(NCH):
        xtb = xtbuf1 if c % 2 == 0 else xtbuf2
        if c - 2 in xt_dtok:
            p.ewait("scalar", xt_dtok[c - 2])
        for grp in range(4):
            if grp in ac_cgrp:
                p.wait("tensor", "AC", ac_cgrp[grp])
            for j in range(4):
                ct = grp * 4 + j
                pvt = PSA[:, grp * 512 + j * 128:grp * 512 + j * 128 + 128]
                for tk in range(4):
                    p.op("tensor", lambda e, c=c, ct=ct, tk=tk, j=j, pvt=pvt: e.matmul(
                        pvt, xbc[:, ct * TW + c * 128 + tk:ct * TW + c * 128 + tk + 128],
                        diag[:, (ct * 4 + tk) * 128:(ct * 4 + tk) * 128 + 128],
                        start=(tk == 0 and j == 0), stop=False), inc=None)
                p.op("tensor", lambda e, ct=ct, pvt=pvt, j=j: e.matmul(
                    pvt, onesrB[:, :], cbR[:, ct * 128:ct * 128 + 128],
                    start=False, stop=(j == 3)), inc="MM" if j == 3 else None)
            p.wait("scalar", "MM", p.sems["MM"].n)
            p.op("scalar", lambda e, grp=grp, xtb=xtb: e.activation(
                xtb[:, grp * 512:grp * 512 + 512], PSA[:, grp * 512:grp * 512 + 512],
                ACTF.Silu, bias=0.0, scale=1.0), inc="AC")
            ac_cgrp[grp] = p.sems["AC"].n
        p.wait(G, "AC", ac_cgrp[3])
        p.dma(G, d_xtok.ap()[c], xtb[:, :])
        xt_dtok[c] = p.dclose("gpsimd")
        pvb = PSB[:, 0:128]
        if ac_B is not None:
            p.wait("tensor", "AC", ac_B)
        for tk in range(4):
            p.op("tensor", lambda e, c=c, tk=tk, pvb=pvb: e.matmul(
                pvb, xbc[:, 16 * TW + c * 128 + tk:16 * TW + c * 128 + tk + 128],
                diag[:, (64 + tk) * 128:(64 + tk) * 128 + 128],
                start=(tk == 0), stop=False), inc=None)
        p.op("tensor", lambda e, pvb=pvb: e.matmul(
            pvb, onesrB[:, :], cbR[:, 2048:2176], start=False, stop=True), inc="MM")
        p.wait("scalar", "MM", p.sems["MM"].n)
        p.op("scalar", lambda e, c=c, pvb=pvb: e.activation(
            Bt[:, c * C:(c + 1) * C], pvb, ACTF.Silu, bias=0.0, scale=1.0), inc="AC")
        ac_B = p.sems["AC"].n
    ac_f = None
    for ct in (16, 17):
        for j, (c0, w) in enumerate(zncs):
            sl = j % 2
            pv = PSB[:, 512 + sl * 512:512 + sl * 512 + w]
            if (16, sl) in ac_cgrp:
                p.wait("tensor", "AC", ac_cgrp[(16, sl)])
            for tk in range(4):
                p.op("tensor", lambda e, ct=ct, tk=tk, c0=c0, w=w, pv=pv: e.matmul(
                    pv, diag[:, (ct * 4 + tk) * 128:(ct * 4 + tk) * 128 + 128],
                    xbc[:, ct * TW + c0 - 3 + tk:ct * TW + c0 - 3 + tk + w],
                    start=(tk == 0), stop=(tk == 3)), inc="MM" if tk == 3 else None)
            p.wait("scalar", "MM", p.sems["MM"].n)
            dst = Bf if ct == 16 else Cf
            p.op("scalar", lambda e, ct=ct, c0=c0, w=w, pv=pv, dst=dst: e.activation(
                dst[:, c0 - 3:c0 - 3 + w], pv, ACTF.Silu, bias=convb_s[:, ct:ct + 1], scale=1.0), inc="AC")
            ac_cgrp[(16, sl)] = p.sems["AC"].n
    p.barrier("AC", "MM", "VE", "GD")

    xbc_mm, xbc_ve, xbc_ac = p.sems["MM"].n, p.sems["VE"].n, p.sems["AC"].n
    s_run = _carve(AR1, F32, 0, 2048)
    crow_all = _carve(AR2, F32, 70144, 512)
    rowAll = _carve(AR2, F32, 65536, 512, parts=1)
    p.wait("tensor", "VE")
    p.wait("tensor", "AC")
    for cc in range(16):
        p.op("tensor", lambda e, cc=cc: e.transpose(
            PSB[0:1, cc * 32:(cc + 1) * 32], cdee[:, cc:cc + 1], identF[0:32, 0:32]),
            inc="MM" if cc == 15 else None)
    p.wait(V, "MM")
    p.op(V, lambda e: e.tensor_copy(rowAll, PSB[0:1, 0:512]), inc="VE")
    p.wait("tensor", "VE")
    for cc in range(16):
        p.op("tensor", lambda e, cc=cc: e.matmul(
            PSB[:, cc * 32:(cc + 1) * 32], onesrF[:, :], rowAll[:, cc * 32:(cc + 1) * 32],
            start=(cc == 0), stop=(cc == 15)), inc="MM" if cc == 15 else None)
    p.wait(V, "MM")
    p.op(V, lambda e: e.tensor_copy(crow_all, PSB[:, 0:512]), inc="VE")
    p.wait("scalar", "MM")
    xts1 = _carve(AR1, BF16, 8192, 2048)
    xts2 = _carve(AR1, BF16, 12288, 2048)
    xtw = _carve(AR1, BF16, 16384, 2048)
    ssc = _carve(AR1, BF16, 20480, 2048)
    w2bf = _carve(AR1, BF16, 0, 32768)
    w2pst = [_carve(AR2, F32, i * 4096, 1024) for i in range(4)]
    w1pst = [_carve(AR2, F32, 16384 + i * 4096, 1024) for i in range(4)]
    w1pob = [_carve(AR2, BF16, 32768 + i * 2048, 1024) for i in range(4)]
    tok_w2, tok_w1i, tok_w1o = {}, {}, {}
    ve_w2c, ve_w1c = {}, {}
    p.wait(S, "MM", xbc_mm)
    p.wait(S, "VE", xbc_ve)
    p.wait(S, "AC", xbc_ac)
    p.wait("scalar", "MM", xbc_mm)
    p.ewait("scalar", zs_tok[14])
    p.ewait("scalar", zs_tok[15])
    p.ewait("scalar", xt_dtok[14])
    p.ewait("scalar", xt_dtok[15])

    def w1_in(j):
        if j > 31:
            return
        if j - 4 >= 0:
            p.wait(S, "AC", ve_w1c[j - 4])
        p.dma(S, w1pst[j % 4], d_w1.ap()[j].rearrange("p k c -> p (k c)"))
        tok_w1i[j] = p.dclose("sync")

    def w1_cast_out(j):
        if j < 0 or j > 31:
            return
        p.ewait("scalar", tok_w1i[j])
        if j - 4 >= 0:
            p.ewait("scalar", tok_w1o[j - 4])
        p.op("scalar", lambda e, j=j: e.activation(
            w1pob[j % 4], w1pst[j % 4], ACTF.Copy, bias=0.0, scale=1.0), inc="AC")
        ve_w1c[j] = p.sems["AC"].n
        p.wait(G, "AC", ve_w1c[j])
        p.dma(G, d_w1b.ap()[j], w1pob[j % 4])
        tok_w1o[j] = p.dclose("gpsimd")

    def w2_in(i):
        if i > 31:
            return
        if i - 4 >= 12:
            p.wait(S, "AC", ve_w2c[i - 4])
        p.dma(S, w2pst[(i - 12) % 4], d_w2.ap()[i * 128:(i + 1) * 128, :])
        tok_w2[i] = p.dclose("sync")

    def w2_cast(i):
        if i < 12 or i > 31:
            return
        p.ewait("scalar", tok_w2[i])
        p.op("scalar", lambda e, i=i: e.activation(
            w2bf[:, i * 1024:(i + 1) * 1024], w2pst[(i - 12) % 4], ACTF.Copy, bias=0.0, scale=1.0), inc="AC")
        ve_w2c[i] = p.sems["AC"].n

    p.op(V, lambda e: e.memset(s_run, 0.0), inc="VE")
    p.dclose("sync")
    p.dma(S, xts1, d_xtok.ap()[0])
    xt_tok = [p.dclose("sync")]
    smallFB = _carve(AR2, F32, 45056, 128, parts=32)
    smallF2B = _carve(AR2, F32, 45568, 128, parts=32)
    xtwB = _carve(AR2, BF16, 46080, 2048)
    ve_xtw, mm_tp, ve_sF, ac_sF, mm_st, ve_sr = {}, {}, {}, {}, {}, {}
    rel_tok, sall_tok = {}, {}

    def d_front(c):
        if c >= NCH:
            return
        sF = smallF if c % 2 == 0 else smallFB
        sF2 = smallF2 if c % 2 == 0 else smallF2B
        if c >= 2:
            p.ewait("scalar", rel_tok[c - 2])
            p.wait("scalar", "MM", mm_tp[c - 2])
        p.op("scalar", lambda e, c=c, sF=sF: e.activation(
            sF[:, :], ga[:, c * C:(c + 1) * C], ACTF.Exp, bias=gaex[:, c:c + 1], scale=-1.0), inc="AC")
        ac_sF[c] = p.sems["AC"].n
        p.wait(V, "AC", ac_sF[c])
        if c >= 2:
            p.wait(V, "MM", mm_tp[c - 2])
        p.op(V, lambda e, c=c, sF=sF, sF2=sF2: e.tensor_tensor(
            sF2[:, :], dt_s[:, c * C:(c + 1) * C], sF[:, :], ALU.mult), inc="VE")
        if c == 0:
            p.op(V, lambda e, c=c, sF=sF: e.tensor_scalar(
                sF[:, :], ga[:, c * C:(c + 1) * C], zcol[:, :], None, ALU.subtract), inc="VE")
        else:
            p.op(V, lambda e, c=c, sF=sF: e.tensor_scalar(
                sF[:, :], ga[:, c * C:(c + 1) * C], gaex[:, c - 1:c], None, ALU.subtract), inc="VE")
        ve_sF[c] = p.sems["VE"].n

    def d_trans(c):
        if c >= NCH:
            return
        sF = smallF if c % 2 == 0 else smallFB
        sF2 = smallF2 if c % 2 == 0 else smallF2B
        t0 = 512 if c % 2 == 0 else 608
        p.wait("tensor", "VE", ve_sF[c])
        if c >= 2:
            p.wait("tensor", "VE", ve_xtw[c - 2])
        p.op("tensor", lambda e, t0=t0, sF2=sF2: e.transpose(PSB[:, t0:t0 + 32], sF2[:, :], identF[0:32, 0:32]), inc=None)
        p.op("tensor", lambda e, t0=t0, sF=sF: e.transpose(PSB[:, t0 + 32:t0 + 64], sF[:, :], identF[0:32, 0:32]), inc=None)
        p.op("tensor", lambda e, c=c, t0=t0: e.transpose(PSB[:, t0 + 64:t0 + 96], dt_s[:, c * C:(c + 1) * C], identF[0:32, 0:32]), inc="MM")
        mm_tp[c] = p.sems["MM"].n

    def d_copies(c):
        if c >= NCH:
            return
        sF = smallF if c % 2 == 0 else smallFB
        xtw_c = xtw if c % 2 == 0 else xtwB
        xts = xts1 if c % 2 == 0 else xts2
        t0 = 512 if c % 2 == 0 else 608
        p.ewait(V, xt_tok[c])
        p.wait(V, "MM", mm_tp[c])
        if c >= 2:
            p.wait(V, "MM", mm_st[c - 2])
        p.op(V, lambda e, c=c, t0=t0: e.tensor_copy(rtok[:, c * NH:(c + 1) * NH], PSB[:, t0 + 32:t0 + 64]), inc="VE")
        p.op(V, lambda e, c=c, t0=t0: e.tensor_copy(dtok[:, c * NH:(c + 1) * NH], PSB[:, t0 + 64:t0 + 96]), inc="VE")
        p.op(V, lambda e, xts=xts, xtw_c=xtw_c, t0=t0: e.tensor_tensor(
            xtw_c.rearrange("p (h q) -> p h q", h=NH),
            xts.rearrange("p (h q) -> p h q", h=NH),
            PSB[:, t0:t0 + 32].unsqueeze(2).to_broadcast([128, NH, HD]), ALU.mult), inc="VE")
        ve_xtw[c] = p.sems["VE"].n
        p.wait(G, "VE", ve_xtw[c])
        p.dma(G, d_rel.ap()[c], sF[:, :])
        rel_tok[c] = p.dclose("gpsimd")

    def d_states(c):
        xtw_c = xtw if c % 2 == 0 else xtwB
        p.wait("tensor", "VE", ve_xtw[c])
        if c >= 1:
            p.wait("tensor", "VE", ve_sr[c - 1])
        for j in range(4):
            p.op("tensor", lambda e, c=c, j=j, xtw_c=xtw_c: e.matmul(
                PSA[:, j * 512:(j + 1) * 512], Bt[:, c * C:(c + 1) * C], xtw_c[:, j * 512:(j + 1) * 512],
                start=True, stop=True), inc="MM" if j == 3 else None)
        mm_st[c] = p.sems["MM"].n

    def d_tail(c):
        if c - 1 in sall_tok:
            p.ewait(V, sall_tok[c - 1])
        p.op(V, lambda e: e.tensor_copy(ssc, s_run), inc="VE")
        p.wait(G, "VE")
        p.dma(G, d_sall.ap()[c], ssc)
        sall_tok[c] = p.dclose("gpsimd")
        p.wait(V, "MM", mm_st[c])
        p.op(V, lambda e, c=c: e.tensor_tensor(
            s_run.rearrange("p (h q) -> p h q", h=NH),
            s_run.rearrange("p (h q) -> p h q", h=NH),
            crow_all[:, c * 32:(c + 1) * 32].unsqueeze(2).to_broadcast([128, NH, HD]), ALU.mult), inc="VE")
        p.op(V, lambda e: e.tensor_tensor(s_run, s_run, PSA[:, :], ALU.add), inc="VE")
        ve_sr[c] = p.sems["VE"].n

    d_front(0)
    d_trans(0)
    d_copies(0)
    for c in range(NCH):
        if c + 1 < NCH:
            if c >= 1:
                p.wait(S, "VE", ve_xtw[c - 1])
            p.dma(S, xts2 if c % 2 == 0 else xts1, d_xtok.ap()[c + 1])
            xt_tok.append(p.dclose("sync"))
        d_front(c + 1)
        d_states(c)
        d_trans(c + 1)
        d_copies(c + 1)
        d_tail(c)
        w1_in(2 * c)
        w1_in(2 * c + 1)
        w1_cast_out(2 * c - 2)
        w1_cast_out(2 * c - 1)
        w2_in(12 + 2 * c)
        w2_in(13 + 2 * c)
        w2_cast(10 + 2 * c)
        w2_cast(11 + 2 * c)
    w1_cast_out(30)
    w1_cast_out(31)
    p.barrier("VE")
    zrow = _carve(AR1, F32, 8192, 2016, parts=1)
    p.op(V, lambda e: e.memset(zrow, 0.0), inc="VE")
    p.barrier("VE")
    p.dma(G, d_ccin.ap()[0:128, :], s_run)
    p.dma(G, d_ccin.ap()[128:129, 0:32], ga[:, T - 1:T])
    p.dma(G, d_ccin.ap()[128:129, 32:2048], zrow)
    p.dwait(G, "gpsimd")
    p.op(G, lambda e: e.collective_compute(
        "AllGather", ALU.bypass, replica_groups=[list(range(8))],
        ins=[d_ccin.ap().opt()], outs=[d_ccout.ap().opt()]), inc="CC")

    ve_p1d, ac_p1d, mm_p1d = p.sems["VE"].n, p.sems["AC"].n, p.sems["MM"].n
    w2h = [_carve(AR2, F32, 40960 + i * 4096, 1024) for i in range(4)]
    p.wait(S, "MM", mm_p1d)
    p.ewait(S, rel_tok[15])
    p.dma_barrier("scalar")
    p.wait("scalar", "MM", mm_p1d)
    tok_w2h = {}
    for j in range(14):
        if j < 12:
            if j >= 4:
                p.wait(S, "AC", ve_w2c[j - 4])
            p.dma(S, w2h[j % 4], d_w2.ap()[j * 128:(j + 1) * 128, :])
            tok_w2h[j] = p.dclose("sync")
        if j >= 2:
            i = j - 2
            p.ewait("scalar", tok_w2h[i])
            p.op("scalar", lambda e, i=i: e.activation(
                w2bf[:, i * 1024:(i + 1) * 1024], w2h[i % 4], ACTF.Copy, bias=0.0, scale=1.0), inc="AC")
            ve_w2c[i] = p.sems["AC"].n
    ve_prepass = p.sems["AC"].n

    NPB = 5
    stgp = ([_carve(AR2, BF16, 8192 + i * 4096, 2048) for i in range(3)] +
            [_carve(AR2, BF16, 61440 + i * 4096, 2048) for i in range(2)])
    qbuf = [_carve(AR2, I32, 20480, 2048), _carve(AR2, I32, 0, 2048)]
    scA = _carve(AR2, F32, 28672, 2048)
    scB = _carve(AR2, F32, 36864, 2048)
    scC = _carve(AR2, F32, 45056, 2048)
    hgq = [_carve(AR2, BF16, 53248, 2048), _carve(AR2, BF16, 57344, 2048)]
    tok_stg, mm_after, evac_ac = {}, {}, {}
    ve_V1, ve_xhi, ve_y = {}, {}, {}
    ac_A1, ac_A3, ac_A5 = {}, {}, {}
    mm_w2_after, ve_obuf, tok_out = {}, {}, {}

    def issue_stage(gm):
        if gm >= 256 or gm % 2 != 0:
            return
        pg = gm // 2
        p.wait(S, "AC", ve_prepass)
        if pg - NPB >= 0:
            p.wait(S, "MM", mm_after[2 * (pg - NPB) + 1])
        if gm < 32:
            p.ewait(S, tok_w1o[gm])
            p.ewait(S, tok_w1o[gm + 1])
        m0 = gm % 32
        p.dma(S, stgp[pg % NPB].rearrange("p (m c) -> p m c", m=2),
              bass.AP(d_w1b, m0 * 131072, [[1024, 128], [131072, 2], [1, 1024]]))
        tok_stg[pg] = p.dclose("sync")

    def emit_chain1(ms, h, q, qo, hg):
        c0, c1 = h * 1024, (h + 1) * 1024
        p.op("scalar", lambda e, q=q, qo=qo, c0=c0, c1=c1: e.activation(
            qo[:, c0:c1], q[:, c0:c1], ACTF.Identity, bias=c155[:, 0:1], scale=1.0 / 64.0), inc="AC")
        ac_A1[(ms, h)] = p.sems["AC"].n
        p.wait(V, "AC", ac_A1[(ms, h)])
        p.op(V, lambda e, qo=qo, c0=c0, c1=c1: e.tensor_copy(scA[:, c0:c1], qo[:, c0:c1]), inc="VE")
        ve_V1[(ms, h)] = p.sems["VE"].n
        p.op(V, lambda e, q=q, c0=c0, c1=c1: e.scalar_tensor_tensor(
            scB[:, c0:c1], q[:, c0:c1], 1.0 / 64.0, scA[:, c0:c1], ALU.mult, ALU.subtract), inc="VE")
        if ms % 4 == 0 and ms >= 4:
            p.ewait(V, tok_out[ms // 4 - 1][h])
        p.op(V, lambda e, c0=c0, c1=c1: e.tensor_scalar(scC[:, c0:c1], scA[:, c0:c1], 0.25, None, ALU.mult), inc="VE")
        p.op(V, lambda e, c0=c0, c1=c1: e.tensor_scalar(scA[:, c0:c1], scC[:, c0:c1], 0.25, None, ALU.add), inc="VE")
        ve_xhi[(ms, h)] = p.sems["VE"].n

    def emit_chain2(ms, h):
        c0, c1 = h * 1024, (h + 1) * 1024
        if ms % 4 == 0 and ms >= 4:
            p.ewait("scalar", tok_out[ms // 4 - 1][h])
        p.wait("scalar", "VE", ve_xhi[(ms, h)])
        p.op("scalar", lambda e, c0=c0, c1=c1: e.activation(
            scC[:, c0:c1], scC[:, c0:c1], ACTF.Gelu, bias=0.0, scale=1.0), inc="AC")
        p.op("scalar", lambda e, c0=c0, c1=c1: e.activation(
            scA[:, c0:c1], scA[:, c0:c1], ACTF.Gelu, bias=0.0, scale=1.0), inc="AC")
        ac_A3[(ms, h)] = p.sems["AC"].n
        p.wait(V, "AC", ac_A3[(ms, h)])
        p.op(V, lambda e, c0=c0, c1=c1: e.tensor_tensor(scA[:, c0:c1], scA[:, c0:c1], scC[:, c0:c1], ALU.subtract), inc="VE")
        p.op(V, lambda e, c0=c0, c1=c1: e.tensor_tensor(scB[:, c0:c1], scB[:, c0:c1], scA[:, c0:c1], ALU.mult), inc="VE")
        p.op(V, lambda e, c0=c0, c1=c1: e.tensor_tensor(scC[:, c0:c1], scC[:, c0:c1], scB[:, c0:c1], ALU.add), inc="VE")
        ve_y[(ms, h)] = p.sems["VE"].n

    def emit_chain3(ms, h, q, hg):
        c0, c1 = h * 1024, (h + 1) * 1024
        p.wait("scalar", "VE", ve_y[(ms, h)])
        p.op("scalar", lambda e, q=q, c0=c0, c1=c1: e.activation(
            q[:, c0:c1], scC[:, c0:c1], ACTF.Copy, bias=0.0, scale=256.0), inc="AC")
        if h == 0 and ms - 2 >= 0:
            p.wait("scalar", "MM", mm_w2_after[ms - 2])
        p.op("scalar", lambda e, q=q, hg=hg, c0=c0, c1=c1: e.activation(
            hg[:, c0:c1], q[:, c0:c1], ACTF.Copy, bias=0.0, scale=1.0), inc="AC")
        ac_A5[(ms, h)] = p.sems["AC"].n

    def emit_w2_half(pend_w2, half):
        ms, hg = pend_w2
        msub = ms % 4
        if half == 0:
            p.wait("tensor", "AC", ve_w2c[msub * 8 + 7])
        p.wait("tensor", "AC", ac_A5[(ms, half)])
        for m8w in range(4 * half, 4 * half + 4):
            mw = msub * 8 + m8w
            for m2 in range(8):
                stt = (msub == 0 and m8w == 0 and m2 % 2 == 0)
                sp = (msub == 3 and m8w == 7 and m2 % 2 == 1)
                last = (m8w == 7 and m2 == 7)
                p.op("tensor", lambda e, mw=mw, m2=m2, hg=hg, m8w=m8w, stt=stt, sp=sp: e.matmul(
                    PSA[:, m2 * 256:(m2 + 1) * 256],
                    w2bf[:, mw * 1024 + m2 * 128:mw * 1024 + m2 * 128 + 128],
                    hg[:, m8w * 256:(m8w + 1) * 256],
                    start=stt, stop=sp), inc="MM" if last else None)
        if half == 1:
            mm_w2_after[ms] = p.sems["MM"].n

    p.wait("tensor", "VE", ve_p1d)
    p.wait("tensor", "AC", ac_p1d)
    for gm0 in range(0, 8, 2):
        issue_stage(gm0)
    pend_w2 = None
    pend = None
    for j in (28, 29, 30, 31):
        p.ewait(V, tok_w1o[j])
    for g in range(8):
        gc0 = 3 + g * 256
        for msub in range(4):
            ms = g * 4 + msub
            par = msub % 2
            q, qo, hg = qbuf[par], qbuf[1 - par], hgq[ms % 2]
            for m8 in range(8):
                m = msub * 8 + m8
                gm = g * 32 + m
                issue_stage(gm + 8)
                bank = m8 % 4
                p.ewait("tensor", tok_stg[gm // 2])
                if bank in evac_ac:
                    p.wait("tensor", "AC", evac_ac[bank])
                pv = PSB[:, bank * 512:bank * 512 + 256]
                sg = stgp[(gm // 2) % NPB]
                sc0 = (gm % 2) * 1024
                for k in range(KT):
                    p.op("tensor", lambda e, k=k, sg=sg, pv=pv, gc0=gc0, sc0=sc0: e.matmul(
                        pv, sg[:, sc0 + k * 128:sc0 + (k + 1) * 128], xn[:, k * TW + gc0:k * TW + gc0 + 256],
                        start=(k == 0), stop=(k == KT - 1)), inc="MM" if k == KT - 1 else None)
                mm_after[gm] = p.sems["MM"].n
                if m8 in (0, 4):
                    h = m8 // 4
                    if msub == 0:
                        if g > 0:
                            p.wait("scalar", "VE", ve_obuf[g - 1])
                    else:
                        p.wait("scalar", "VE", ve_V1[(ms - 1, h)])
                p.wait("scalar", "MM", mm_after[gm])
                p.op("scalar", lambda e, m=m, m8=m8, pv=pv, q=q: e.activation(
                    q[:, m8 * 256:m8 * 256 + 256], pv, ACTF.Identity, bias=b1c[:, m:m + 1], scale=256.0), inc="AC")
                evac_ac[bank] = p.sems["AC"].n
                if m8 == 1 and pend is not None:
                    emit_chain2(pend[0], 0)
                if m8 == 2 and pend is not None:
                    emit_chain2(pend[0], 1)
                if m8 == 3 and pend is not None:
                    emit_chain3(pend[0], 0, pend[1], pend[3])
                if m8 == 4:
                    emit_chain1(ms, 0, q, qo, hg)
                if m8 == 5 and pend is not None:
                    emit_chain3(pend[0], 1, pend[1], pend[3])
                    pend = None
            emit_chain1(ms, 1, q, qo, hg)
            if pend_w2 is not None:
                emit_w2_half(pend_w2, 0)
                emit_w2_half(pend_w2, 1)
                pend_w2 = None
            if msub == 3:
                emit_chain2(ms, 0)
                emit_chain2(ms, 1)
                emit_chain3(ms, 0, q, hg)
                emit_chain3(ms, 1, q, hg)
            else:
                pend = (ms, q, qo, hg)
            pend_w2 = (ms, hg)
        emit_w2_half(pend_w2, 0)
        emit_w2_half(pend_w2, 1)
        pend_w2 = None
        p.wait("scalar", "MM", mm_w2_after[g * 4 + 3])
        tok_out[g] = []
        for hf in range(2):
            for m2 in range(4 * hf, 4 * hf + 4):
                p.op("scalar", lambda e, m2=m2: e.activation(
                    qbuf[0][:, m2 * 256:m2 * 256 + 256], PSA[:, m2 * 256:(m2 + 1) * 256],
                    ACTF.Identity, bias=b2c[:, m2:m2 + 1], scale=1.0), inc="AC")
            p.wait(V, "AC", p.sems["AC"].n)
            p.op(V, lambda e, hf=hf: e.tensor_scalar(
                scC[:, hf * 1024:(hf + 1) * 1024], qbuf[0][:, hf * 1024:(hf + 1) * 1024],
                1.0 / 256.0, None, ALU.mult), inc="VE")
            p.wait(G, "VE", p.sems["VE"].n)
            p.dma(G, bass.AP(d_out, g * 256 + hf * 4 * 128 * T, [[T, 128], [128 * T, 4], [1, 256]]),
                  scC[:, hf * 1024:(hf + 1) * 1024].rearrange("p (m l) -> p m l", m=4))
            tok_out[g].append(p.dclose("gpsimd"))
        ve_obuf[g] = p.sems["VE"].n
    p.barrier("GD", "MM", "AC", "VE", "GP")


    p.barrier("CC", "GD", "LD", "VE", "MM", "AC", "GP")
    scb8 = _carve(AR2, F32, 37376, 2048, parts=8)
    srs = _carve(AR1, F32, 0, 2048)
    scbc = _carve(AR1, F32, 8192, 2048)
    tmp2 = _carve(AR1, F32, 16384, 2048)
    s_i2 = _carve(AR2, F32, 8704, 2048)
    s_i2b = _carve(AR2, BF16, 16896, 2048)
    dii = _carve(AR2, BF16, 512, NH * C)
    p.op(V, lambda e: e.tensor_tensor(
        dii.rearrange("p (h c) -> p h c", h=NH),
        identF[:, :].unsqueeze(1).to_broadcast([128, NH, 128]),
        db_s[:, :].unsqueeze(2).to_broadcast([128, NH, 128]), ALU.mult), inc="VE")
    p.dma(S, gls[:, :], bass.AP(d_ccout, 128 * INTER, [[129 * INTER, 8], [1, 32]]))
    p.barrier("LD")
    p.ewait("tensor", p.qstate["sync"]["last"])
    p.op("tensor", lambda e: e.matmul(PSB[0:8, 0:32], btw_s[:, :], gls[:, :], start=True, stop=True), inc="MM")
    p.wait("scalar", "MM")
    p.op("scalar", lambda e: e.activation(esc[:, :], PSB[0:8, 0:32], ACTF.Exp, bias=0.0, scale=1.0), inc="AC")
    p.wait(V, "AC")
    p.op(V, lambda e: e.tensor_scalar(esc[:, :], esc[:, :], sel_s[:, :], None, ALU.mult), inc="VE")
    p.op(V, lambda e: e.tensor_copy(
        scb8.rearrange("r (h q) -> r h q", h=NH),
        esc[:, :].unsqueeze(2).to_broadcast([8, NH, HD])), inc="VE")
    p.wait(G, "VE")
    p.dma(G, d_scb.ap(), scb8)
    p.dwait(S, "gpsimd")
    p.op(V, lambda e: e.memset(s_i2, 0.0), inc="VE")
    srs2 = _carve(AR1, F32, 24576, 2048)
    scbc2 = _carve(AR1, F32, 32768, 2048)
    ri_tok = {}
    ve_rr = {}
    for r in range(9):
        if r < 8:
            sb_s = srs if r % 2 == 0 else srs2
            sb_c = scbc if r % 2 == 0 else scbc2
            if r >= 2:
                p.wait(S, "VE", ve_rr[r - 2])
            p.dma(S, sb_s, d_ccout.ap()[129 * r:129 * r + 128, :])
            p.dma(S, sb_c, d_scb.ap()[r].unsqueeze(0).unsqueeze(1).to_broadcast([1, 128, INTER]))
            ri_tok[r] = p.dclose("sync")
        if r >= 1:
            rr = r - 1
            sb_s = srs if rr % 2 == 0 else srs2
            sb_c = scbc if rr % 2 == 0 else scbc2
            p.ewait(V, ri_tok[rr])
            p.op(V, lambda e, sb_s=sb_s, sb_c=sb_c: e.tensor_tensor(tmp2, sb_s, sb_c, ALU.mult), inc="VE")
            p.op(V, lambda e: e.tensor_tensor(s_i2, s_i2, tmp2, ALU.add), inc="VE")
            ve_rr[rr] = p.sems["VE"].n

    xts1p = _carve(AR1, BF16, 0, 2048)
    xts2p = _carve(AR1, BF16, 4096, 2048)
    sl1 = _carve(AR1, BF16, 8192, 2048)
    sl2 = _carve(AR1, BF16, 12288, 2048)
    zs1 = _carve(AR1, BF16, 16384, 2048)
    zs2 = _carve(AR1, BF16, 20480, 2048)
    relb = _carve(AR1, F32, 24576, 4096)
    bufA = _carve(AR1, F32, 40960, 4096)
    wobf1 = _carve(AR1, BF16, 57344, 2048)
    wobf2 = _carve(AR1, BF16, 61440, 2048)
    bufB = _carve(AR2, F32, 20992, 4096)
    mt_b = _carve(AR2, BF16, 37376, 4096)
    ct_b = _carve(AR2, BF16, 45568, 4096)
    xtq = _carve(AR2, BF16, 53760, 2048)
    graw = _carve(AR2, F32, 57856, 2048)
    g2b = _carve(AR2, BF16, 66048, 2048)
    scm = _carve(AR2, F32, 0, 128)
    ggrp = _carve(xn, BF16, 0, 8192)
    wostg = _carve(xn, F32, 16384, 2048)
    wos3 = _carve(xn, BF16, 16384, 2048)
    wos4 = _carve(xn, BF16, 20480, 2048)
    obuf2 = _carve(xn, F32, 24576, 2048)

    p.barrier("VE", "GD", "LD")
    p.dclose("sync")
    p.dclose("gpsimd")
    st_tok, rl_tok = {}, {}
    wi_tok, wb_tok, ve_wfold = {}, {}, {}
    mm_scores, mm_y = {}, {}
    ac_expg, ve_bufA, ac_bufB, ac_L, ac_si2b, ac_g2b, ve_dec = {}, {}, {}, {}, {}, {}, {}
    ve_bufA_h, ac_expg_h, ve_mt_h, ve_ct_h, ve_g2b_h, mm_y_bank, wo_pre = {}, {}, {}, {}, {}, {}, {}
    ve_scm, ve_ctb, ve_graw, ve_g2b = {}, {}, {}, {}
    mm_sumsq, ac_rsq, ve_rrec, mm_rsb, ve_ggrp = {}, {}, {}, {}, {}
    ve_oc_slot, mm_wo_slot = {}, {}
    mm_op_last = [None]
    out_tok = [None, None]

    def streams(c):
        if c >= NCH:
            return
        if c >= 2:
            p.wait(S, "MM", mm_y[c - 2])
            p.wait(S, "VE", ve_graw[c - 2])
        p.dma(S, xts1p if c % 2 == 0 else xts2p, d_xtok.ap()[c])
        p.dma(S, sl1 if c % 2 == 0 else sl2, d_sall.ap()[c])
        p.dma(S, zs1 if c % 2 == 0 else zs2,
              bass.AP(d_zsil, c * 16 * 128 * C, [[C, 128], [128 * C, 16], [1, C]]))
        st_tok[c] = p.dclose("sync")

    def relb_pre(c):
        if c >= NCH:
            return
        rl_tok[c] = []
        if c >= 1:
            p.wait(S, "VE", ve_bufA[c - 1])
            p.wait(S, "AC", ac_expg[c - 1])
        p.dma(S, relb[:, 0:2048],
              d_rel.ap()[c][0:2048].unsqueeze(0).unsqueeze(1).to_broadcast([1, 128, 2048]))
        rl_tok[c].append(p.dclose("sync"))
        if c >= 1:
            p.wait(G, "VE", ve_bufA[c - 1])
            p.wait(G, "AC", ac_expg[c - 1])
        p.dma(G, relb[:, 2048:4096],
              d_rel.ap()[c][2048:4096].unsqueeze(0).unsqueeze(1).to_broadcast([1, 128, 2048]))
        rl_tok[c].append(p.dclose("gpsimd"))

    def emitF1(c):
        if c >= NCH:
            return
        if c >= 1:
            p.wait(V, "AC", ac_L[(c - 1, 1)])
        for hh in range(2):
            hc = hh * 2048
            p.ewait(V, rl_tok[c][hh])
            p.op(V, lambda e, c=c, hh=hh, hc=hc: e.tensor_tensor(
                bufA[:, hc:hc + 2048].rearrange("p (h l) -> p h l", h=16),
                rtok[:, c * NH + hh * 16:c * NH + hh * 16 + 16].unsqueeze(2).to_broadcast([128, 16, C]),
                relb[:, hc:hc + 2048].rearrange("p (h l) -> p h l", h=16), ALU.subtract), inc="VE")
            ve_bufA_h[(c, hh)] = p.sems["VE"].n
        ve_bufA[c] = p.sems["VE"].n
        p.wait(V, "MM", mm_scores[c])
        p.op(V, lambda e: e.tensor_tensor(scm, PSB[:, 0:128], maskle[:, :], ALU.mult), inc="VE")
        ve_scm[c] = p.sems["VE"].n

    def emitAC(c):
        if c >= NCH:
            return
        for hh in range(2):
            hc = hh * 2048
            p.ewait("scalar", rl_tok[c][hh])
            if c >= 1:
                p.wait("scalar", "VE", ve_ct_h[(c - 1, hh)])
            p.op("scalar", lambda e, hc=hc: e.activation(
                bufB[:, hc:hc + 2048], relb[:, hc:hc + 2048], ACTF.Exp, bias=0.0, scale=1.0), inc="AC")
            ac_expg_h[(c, hh)] = p.sems["AC"].n
            p.wait("scalar", "VE", ve_bufA_h[(c, hh)])
            p.op("scalar", lambda e, hc=hc: e.activation(
                bufA[:, hc:hc + 2048], bufA[:, hc:hc + 2048], ACTF.Relu, bias=0.0, scale=1.0), inc="AC")
            p.op("scalar", lambda e, hc=hc: e.activation(
                bufA[:, hc:hc + 2048], bufA[:, hc:hc + 2048], ACTF.Exp, bias=0.0, scale=-1.0), inc="AC")
            ac_L[(c, hh)] = p.sems["AC"].n
        ac_expg[c] = p.sems["AC"].n

    def emitF2(c):
        if c >= NCH:
            return
        p.ewait(V, st_tok[c])
        if c >= 1:
            p.wait(V, "MM", mm_y[c - 1])
            p.wait("scalar", "MM", mm_y[c - 1])
            p.wait("scalar", "VE", ve_dec[c - 1])
        p.op("scalar", lambda e: e.activation(s_i2b, s_i2, ACTF.Copy, bias=0.0, scale=1.0), inc="AC")
        ac_si2b[c] = p.sems["AC"].n
        xts = xts1p if c % 2 == 0 else xts2p
        p.op(V, lambda e, c=c, xts=xts: e.tensor_tensor(
            xtq.rearrange("p (h q) -> p h q", h=NH),
            xts.rearrange("p (h q) -> p h q", h=NH),
            dtok[:, c * NH:(c + 1) * NH].unsqueeze(2).to_broadcast([128, NH, HD]), ALU.mult), inc="VE")
        if c + 1 < NCH:
            p.wait(V, "AC", ac_si2b[c])
            p.op(V, lambda e, c=c: e.tensor_tensor(
                s_i2.rearrange("p (h q) -> p h q", h=NH),
                s_i2.rearrange("p (h q) -> p h q", h=NH),
                crow_all[:, c * 32:(c + 1) * 32].unsqueeze(2).to_broadcast([128, NH, HD]), ALU.mult), inc="VE")
        ve_dec[c] = p.sems["VE"].n

    def emitMT(c):
        for hh in range(2):
            hc = hh * 2048
            p.wait(V, "AC", ac_expg_h[(c, hh)])
            p.op(V, lambda e, c=c, hc=hc: e.tensor_tensor(
                ct_b[:, hc:hc + 2048].rearrange("p (h l) -> p h l", h=16),
                Cf[:, c * C:(c + 1) * C].unsqueeze(1).to_broadcast([128, 16, C]),
                bufB[:, hc:hc + 2048].rearrange("p (h l) -> p h l", h=16), ALU.mult), inc="VE")
            ve_ct_h[(c, hh)] = p.sems["VE"].n
            p.wait(V, "AC", ac_L[(c, hh)])
            p.op(V, lambda e, hc=hc: e.tensor_tensor(
                mt_b[:, hc:hc + 2048].rearrange("p (h l) -> p h l", h=16),
                bufA[:, hc:hc + 2048].rearrange("p (h l) -> p h l", h=16),
                scm.unsqueeze(1).to_broadcast([128, 16, C]), ALU.mult), inc="VE")
            ve_mt_h[(c, hh)] = p.sems["VE"].n
        ve_ctb[c] = p.sems["VE"].n

    def emitScores(c):
        if c >= NCH:
            return
        if c >= 1:
            p.wait("tensor", "VE", ve_scm[c - 1])
        else:
            p.wait("tensor", "VE")
            p.wait("tensor", "AC")
        p.op("tensor", lambda e, c=c: e.matmul(
            PSB[:, 0:128], Bf[:, c * C:(c + 1) * C], Cf[:, c * C:(c + 1) * C],
            start=True, stop=True), inc="MM")
        mm_scores[c] = p.sems["MM"].n

    def emitY(c):
        slc = sl1 if c % 2 == 0 else sl2
        xts = xts1p if c % 2 == 0 else xts2p
        p.wait("tensor", "VE", ve_mt_h[(c, 0)])
        p.wait("tensor", "AC", ac_si2b[c])
        for t in range(16):
            if t == 8:
                p.wait("tensor", "VE", ve_mt_h[(c, 1)])
            bank = t // 4
            col0 = bank * 512 + (t % 4) * 128
            for hh in range(2):
                h = 2 * t + hh
                pvh = PSA[hh * 64:hh * 64 + 64, col0:col0 + 128]
                stt = (t % 4 == 0)
                sp = (t % 4 == 3)
                tp = (0, hh * 64)
                p.op("tensor", lambda e, h=h, pvh=pvh, stt=stt, tp=tp: e.matmul(
                    pvh, xtq[:, h * HD:(h + 1) * HD], mt_b[:, h * C:(h + 1) * C],
                    start=stt, stop=False, tile_position=tp, skip_group_check=True), inc=None)
                p.op("tensor", lambda e, h=h, pvh=pvh, slc=slc, tp=tp: e.matmul(
                    pvh, slc[:, h * HD:(h + 1) * HD], ct_b[:, h * C:(h + 1) * C],
                    start=False, stop=False, tile_position=tp, skip_group_check=True), inc=None)
                p.op("tensor", lambda e, h=h, pvh=pvh, tp=tp: e.matmul(
                    pvh, s_i2b[:, h * HD:(h + 1) * HD], ct_b[:, h * C:(h + 1) * C],
                    start=False, stop=False, tile_position=tp, skip_group_check=True), inc=None)
                p.op("tensor", lambda e, h=h, pvh=pvh, xts=xts, sp=sp, tp=tp: e.matmul(
                    pvh, xts[:, h * HD:(h + 1) * HD], dii[:, h * C:(h + 1) * C],
                    start=False, stop=sp, tile_position=tp, skip_group_check=True), inc="MM" if (sp and hh == 1) else None)
            if t % 4 == 3:
                mm_y_bank[(c, t // 4)] = p.sems["MM"].n
        mm_y[c] = p.sems["MM"].n

    def emitG(c):
        zsc = zs1 if c % 2 == 0 else zs2
        for bank in range(4):
            p.wait(V, "MM", mm_y_bank[(c, bank)])
            p.op(V, lambda e, bank=bank, zsc=zsc: e.tensor_tensor(
                graw[:, bank * 512:(bank + 1) * 512], PSA[:, bank * 512:(bank + 1) * 512],
                zsc[:, bank * 512:(bank + 1) * 512], ALU.mult), inc="VE")
        ve_graw[c] = p.sems["VE"].n
        p.op(V, lambda e: e.tensor_tensor(g2b[:, 0:1024], graw[:, 0:1024], graw[:, 0:1024], ALU.mult), inc="VE")
        ve_g2b_h[(c, 0)] = p.sems["VE"].n
        p.op(V, lambda e: e.tensor_tensor(g2b[:, 1024:2048], graw[:, 1024:2048], graw[:, 1024:2048], ALU.mult), inc="VE")
        ve_g2b[c] = p.sems["VE"].n

    def emitSumsq(c):
        p.wait("tensor", "VE", ve_g2b_h[(c, 0)])
        if c >= 1:
            p.wait("tensor", "AC", ac_rsq[c - 1])
        for tt in range(16):
            if tt == 8:
                p.wait("tensor", "VE", ve_g2b[c])
            p.op("tensor", lambda e, tt=tt: e.matmul(
                PSB[0:1, 672:800], onesb[:, :], g2b[:, tt * 128:(tt + 1) * 128],
                start=(tt == 0), stop=(tt == 15)), inc="MM" if tt == 15 else None)
        mm_sumsq[c] = p.sems["MM"].n
        p.wait("scalar", "MM", mm_sumsq[c])
        p.op("scalar", lambda e: e.activation(
            PSB[0:1, 672:800], PSB[0:1, 672:800], ACTF.Sqrt, bias=c1e5[0:1, 0:1], scale=1.0 / 2048.0), inc="AC")
        ac_rsq[c] = p.sems["AC"].n

    def emitTail(c):
        p.wait(V, "AC", ac_rsq[c])
        p.op(V, lambda e: e.reciprocal(rrow[:, :], PSB[0:1, 672:800]), inc="VE")
        ve_rrec[c] = p.sems["VE"].n
        p.wait("tensor", "VE", ve_rrec[c])
        p.op("tensor", lambda e: e.matmul(PSB[:, 800:928], onesrF[:, :], rrow[:, :], start=True, stop=True), inc="MM")
        mm_rsb[c] = p.sems["MM"].n
        p.wait(V, "MM", mm_rsb[c])
        if mm_op_last[0] is not None:
            p.wait(V, "MM", mm_op_last[0])
        p.op(V, lambda e, c=c: e.tensor_tensor(
            ggrp.rearrange("p (t g l) -> p t g l", t=16, g=4)[:, :, c % 4, :],
            graw.rearrange("p (t l) -> p t l", t=16),
            PSB[:, 800:928].unsqueeze(1).to_broadcast([128, 16, 128]), ALU.mult), inc="VE")
        ve_ggrp[c] = p.sems["VE"].n

    def bounce(c):
        if c >= 4:
            return
        for mi in (2 * c, 2 * c + 1):
            inb = wostg if mi % 2 == 0 else obuf2
            outb = wobf1 if mi % 2 == 0 else wobf2
            p.ewait(V, wi_tok[mi])
            if mi - 2 >= 0:
                p.ewait(V, wb_tok[mi - 2])
            p.op(V, lambda e, inb=inb, outb=outb: e.tensor_tensor(
                outb.rearrange("p (k c2) -> p k c2", k=16),
                inb.rearrange("p (k c2) -> p k c2", k=16),
                nw_s[:, :].unsqueeze(2).to_broadcast([128, 16, 128]), ALU.mult), inc="VE")
            ve_wfold[mi] = p.sems["VE"].n
            p.wait(G, "VE", ve_wfold[mi])
            p.dma(G, d_woutb.ap()[mi], outb)
            wb_tok[mi] = p.dclose("gpsimd")
        for mi in (2 * c + 2, 2 * c + 3):
            if mi < 8:
                p.wait(S, "VE", ve_wfold[mi - 2])
                p.dma(S, wostg if mi % 2 == 0 else obuf2, d_wout.ap()[mi].rearrange("p k c -> p (k c)"))
                wi_tok[mi] = p.dclose("sync")

    def wo_dma_issue(gidx, m, wo_tok, mm_op):
        if m >= 8:
            return
        slot = m % 4
        if gidx == 0:
            p.ewait(S, wb_tok[6])
            p.ewait(S, wb_tok[7])
            p.ewait(S, wb_tok[m])
            if slot >= 2:
                p.wait(S, "VE", ve_wfold[6])
                p.wait(G, "VE", ve_wfold[6])
        if m - 4 in mm_op:
            p.wait(S, "MM", mm_op[m - 4])
        elif slot in mm_wo_slot:
            p.wait(S, "MM", mm_wo_slot[slot])
        wob = WOBS[slot]
        if slot % 2 == 1:
            if m - 4 in mm_op:
                p.wait(G, "MM", mm_op[m - 4])
            elif slot in mm_wo_slot:
                p.wait(G, "MM", mm_wo_slot[slot])
            p.dma(G, wob, d_woutb.ap()[m])
            wo_tok[m] = [p.dclose("gpsimd")]
        else:
            p.dma(S, wob, d_woutb.ap()[m])
            wo_tok[m] = [p.dclose("sync")]

    def emitOutproj(c):
        gidx = c // 4
        mm_op = {}
        if ("tok", gidx) in wo_pre:
            wo_tok = wo_pre.pop(("tok", gidx))
        else:
            wo_tok = {}
            for mq in range(4):
                wo_dma_issue(gidx, mq, wo_tok, mm_op)

        def wo_dma(m):
            wo_dma_issue(gidx, m, wo_tok, mm_op)
        for m in range(8):
            wob = WOBS[m % 4]
            for tk in wo_tok[m]:
                p.ewait("tensor", tk)
            if m == 0:
                p.wait("tensor", "VE", ve_ggrp[c])
            if (m % 2) in ve_oc_slot:
                p.wait("tensor", "VE", ve_oc_slot[m % 2])
            pv = PSB[:, 1024 + (m % 2) * 512:1024 + (m % 2) * 512 + 512]
            for k in range(16):
                p.op("tensor", lambda e, k=k, pv=pv, wob=wob: e.matmul(
                    pv, wob[:, k * 128:(k + 1) * 128], ggrp[:, k * 512:(k + 1) * 512],
                    start=(k == 0), stop=(k == 15)), inc="MM" if k == 15 else None)
            mm_op[m] = p.sems["MM"].n
            mm_wo_slot[m % 4] = mm_op[m]
            wo_dma(m + 4)
            p.wait(V, "MM", mm_op[m])
            if m == 0:
                p.ewait(V, out_tok[1])
            if m == 4:
                p.ewait(V, out_tok[0])
            p.op(V, lambda e, m=m, pv=pv: e.tensor_copy(obuf2[:, (m % 4) * 512:(m % 4) * 512 + 512], pv), inc="VE")
            ve_oc_slot[m % 2] = p.sems["VE"].n
            if m % 4 == 3:
                p.wait(G, "VE", ve_oc_slot[m % 2])
                p.dma(G, bass.AP(d_out, gidx * 512 + (m // 4) * 4 * 128 * T, [[T, 128], [128 * T, 4], [1, 512]]),
                      obuf2.rearrange("p (m l) -> p m l", m=4), accum=True)
                out_tok[m // 4] = p.dclose("gpsimd")
        mm_op_last[0] = mm_op[7]

    WOBS = [wobf1, wobf2, wos3, wos4]
    streams(0)
    streams(1)
    relb_pre(0)
    for mi in range(2):
        p.dma(S, wostg if mi % 2 == 0 else obuf2, d_wout.ap()[mi].rearrange("p k c -> p (k c)"))
        wi_tok[mi] = p.dclose("sync")
    emitScores(0)
    emitF1(0)
    emitAC(0)
    relb_pre(1)
    emitF2(0)
    for c in range(NCH):
        bounce(c)
        emitMT(c)
        emitScores(c + 1)
        emitY(c)
        emitF1(c + 1)
        emitAC(c + 1)
        relb_pre(c + 2)
        emitG(c)
        streams(c + 2)
        emitSumsq(c)
        emitF2(c + 1)
        if c >= 1 and (c - 1) % 4 == 3:
            emitOutproj(c - 1)
        emitTail(c)
        if c % 4 == 2 and c // 4 >= 1:
            wo_tok_pre = {}
            wo_pre[("tok", c // 4)] = wo_tok_pre
            for mq in range(4):
                wo_dma_issue(c // 4, mq, wo_tok_pre, {})
    emitOutproj(NCH - 1)
    p.barrier("GD", "LD", "VE", "MM", "AC", "GP")
    p.dma(G, d_out.ap(), d_u.ap()[:, H:TW], accum=True)
    p.dma_barrier(S)
    for snm in ("VE", "MM", "AC", "GP", "CC"):
        p.wait(S, snm)

    p.build()
    p.es.close()
    return nc


def kernel(**inputs):
    u = np.ascontiguousarray(inputs["L0_attn_u"])
    win = inputs["L0_attn_in_proj_w"]
    w1 = inputs["L0_ffn_w1"]
    wout = inputs["L0_attn_out_proj_w"]
    w1m = np.ascontiguousarray(w1.reshape(KT, 128, 32, 128).transpose(2, 1, 0, 3))
    woutm = np.ascontiguousarray(wout.reshape(16, 128, 8, 128).transpose(2, 1, 0, 3))
    in_maps = []
    for core in range(8):
        b, q = core // 4, core % 4
        t0 = q * T
        halo = np.zeros((H, D), np.float32) if q == 0 else u[b, t0 - H:t0]
        u_r = np.ascontiguousarray(np.concatenate([halo, u[b, t0:t0 + T]], axis=0).T)
        sel = np.zeros((8, 1), np.float32)
        btwT = np.zeros((8, 8), np.float32)
        for r in range(8):
            rb, rq = r // 4, r % 4
            if rb == b and rq < q:
                sel[r, 0] = 1.0
            for j in range(8):
                jb, jq = j // 4, j % 4
                if jb == b and rb == b and rq < jq < q:
                    btwT[j, r] = 1.0
        in_maps.append({
            "u_r": u_r, "win": win, "convw": inputs["L0_attn_conv_w"],
            "convb": inputs["L0_attn_conv_b"], "dtb": inputs["L0_attn_dt_bias"],
            "alog": inputs["L0_attn_A_log"], "dpar": inputs["L0_attn_D_param"],
            "normw": inputs["L0_attn_norm_w"], "w1m": w1m, "b1": inputs["L0_ffn_b1"],
            "w2": inputs["L0_ffn_w2"], "b2": inputs["L0_ffn_b2"], "woutm": woutm,
            "sel": sel, "btwT": btwT,
        })
    nc = build_nc()
    res = run_bass_kernel_spmd(nc, in_maps, core_ids=list(range(8)))
    out = np.zeros((2, 8192, D), np.float32)
    for core in range(8):
        b, q = core // 4, core % 4
        out[b, q * T:(q + 1) * T] = np.ascontiguousarray(res.results[core]["out"]).T
    return out




# version 35
# speedup vs baseline: 1.0463x; 1.0087x over previous
import numpy as np
from contextlib import ExitStack

import concourse.bass as bass
import concourse.mybir as mybir
from concourse.bass_utils import run_bass_kernel_spmd

F32 = mybir.dt.float32
BF16 = mybir.dt.bfloat16
I32 = mybir.dt.int32
ALU = mybir.AluOpType
ACTF = mybir.ActivationFunctionType

ENGINES = ("sync", "gpsimd", "vector", "scalar", "tensor")

T = 2048
H = 3
TW = T + H
D = 1024
KT = 8
INTER = 2048
NH, HD, DS = 32, 64, 128
NCH, C = 16, 128
FF = 4096


class Sem:
    def __init__(self, h):
        self.h = h
        self.n = 0


class Prog:
    def __init__(self, nc):
        self.nc = nc
        self.streams = {e: [] for e in ENGINES}
        self.es = ExitStack()
        self.sems = {}
        names = ["VE", "AC", "MM", "GP", "CC"]
        names += ["LQ%d" % i for i in range(8)] + ["GQ%d" % i for i in range(8)]
        for name in names:
            self.sems[name] = Sem(self.es.enter_context(nc.semaphore(name)))
        self.lastw = {(e, s): 0 for e in ENGINES for s in self.sems}
        self.qstate = {
            "sync": {"pool": ["LQ%d" % i for i in range(8)], "epoch": 0, "cur": None, "last": None},
            "gpsimd": {"pool": ["GQ%d" % i for i in range(8)], "epoch": 0, "cur": None, "last": None},
        }

    def sb(self, name, shape, dt):
        return self.es.enter_context(self.nc.sbuf_tensor(name, shape, dt))

    def emit(self, eng, fn):
        self.streams[eng].append(fn)

    def wait(self, eng, sname, val=None):
        s = self.sems[sname]
        v = s.n if val is None else val
        if v <= self.lastw[(eng, sname)]:
            return
        self.lastw[(eng, sname)] = v
        self.emit(eng, lambda e, h=s.h, v=v: e.wait_ge(h, v))

    ENG_SEM = {"vector": "VE", "scalar": "AC", "gpsimd": "GP"}

    def op(self, eng, fn, inc=None, n=1):
        own = self.ENG_SEM.get(eng)
        if own is not None:
            self.wait(eng, own)
            inc = own
        if inc is not None:
            s = self.sems[inc]
            s.n += n
            self.emit(eng, lambda e, f=fn, h=s.h, m=n: f(e).then_inc(h, m))
        else:
            self.emit(eng, fn)

    def dma(self, eng, out, in_, accum=False):
        q = "gpsimd" if eng == "gpsimd" else "sync"
        st = self.qstate[q]
        if eng == "gpsimd":
            self.wait(eng, "GP")
        if st["cur"] is None:
            nm = st["pool"][st["epoch"] % len(st["pool"])]
            st["epoch"] += 1
            st["cur"] = nm
            self.wait(eng, nm)
        s = self.sems[st["cur"]]
        s.n += 16
        if accum:
            self.emit(eng, lambda e, o=out, i=in_, h=s.h: e.dma_start(
                out=o, in_=i, accum_op=ALU.add).then_inc(h, 16))
        else:
            self.emit(eng, lambda e, o=out, i=in_, h=s.h: e.dma_start(
                out=o, in_=i).then_inc(h, 16))

    def dclose(self, q):
        st = self.qstate[q]
        if st["cur"] is not None:
            st["last"] = (st["cur"], self.sems[st["cur"]].n)
            st["cur"] = None
        return st["last"]

    def ewait(self, eng, token):
        if token is None:
            return
        nm, tot = token
        self.wait(eng, nm, tot)

    def dwait(self, eng, q):
        self.ewait(eng, self.dclose(q))

    def dma_barrier(self, *engines):
        self.dclose("sync")
        self.dclose("gpsimd")
        engs = engines if engines else ENGINES
        for e in engs:
            for nm in self.qstate["sync"]["pool"] + self.qstate["gpsimd"]["pool"]:
                self.wait(e, nm)

    def barrier(self, *producers):
        prs = list(producers)
        if "LD" in prs or "GD" in prs:
            self.dma_barrier()
            prs = [x for x in prs if x not in ("LD", "GD")]
        for e in ENGINES:
            for pr in prs:
                self.wait(e, pr)

    def build(self):
        with self.nc.allow_non_contiguous_dma("small strided param loads"), self.nc.Block() as block:
            @block.sync
            def _(e):
                for f in self.streams["sync"]:
                    f(e)

            @block.gpsimd
            def _(e):
                for f in self.streams["gpsimd"]:
                    f(e)

            @block.vector
            def _(e):
                for f in self.streams["vector"]:
                    f(e)

            @block.scalar
            def _(e):
                for f in self.streams["scalar"]:
                    f(e)

            @block.tensor
            def _(e):
                for f in self.streams["tensor"]:
                    f(e)


def _carve(arena, dt, lo_bytes, n_elems, parts=128):
    esz = {F32: 4, BF16: 2, I32: 4}[dt]
    assert lo_bytes % esz == 0
    ap = arena[:, :].bitcast(dt)
    lo = lo_bytes // esz
    return ap[0:parts, lo:lo + n_elems]


def build_nc():
    nc = bass.Bass("TRN2", target_bir_lowering=False)
    d_u = nc.dram_tensor("u_r", [D, TW], F32, kind="ExternalInput")
    d_win = nc.dram_tensor("win", [D, 4384], F32, kind="ExternalInput")
    d_convw = nc.dram_tensor("convw", [2304, 4], F32, kind="ExternalInput")
    d_convb = nc.dram_tensor("convb", [2304], F32, kind="ExternalInput")
    d_dtb = nc.dram_tensor("dtb", [NH], F32, kind="ExternalInput")
    d_alog = nc.dram_tensor("alog", [NH], F32, kind="ExternalInput")
    d_dpar = nc.dram_tensor("dpar", [NH], F32, kind="ExternalInput")
    d_normw = nc.dram_tensor("normw", [INTER], F32, kind="ExternalInput")
    d_w1 = nc.dram_tensor("w1m", [32, 128, KT, 128], F32, kind="ExternalInput")
    d_b1 = nc.dram_tensor("b1", [FF], F32, kind="ExternalInput")
    d_w2 = nc.dram_tensor("w2", [FF, D], F32, kind="ExternalInput")
    d_w1b = nc.dram_tensor("w1b_d", [32, 128, 1024], BF16, kind="Internal")
    d_woutb = nc.dram_tensor("woutb_d", [8, 128, 2048], BF16, kind="Internal")
    d_b2 = nc.dram_tensor("b2", [D], F32, kind="ExternalInput")
    d_wout = nc.dram_tensor("woutm", [8, 128, 16, 128], F32, kind="ExternalInput")
    d_sel = nc.dram_tensor("sel", [8, 1], F32, kind="ExternalInput")
    d_btwT = nc.dram_tensor("btwT", [8, 8], F32, kind="ExternalInput")
    d_out = nc.dram_tensor("out", [D, T], F32, kind="ExternalOutput")
    d_zsil = nc.dram_tensor("zsil_d", [NCH, 16, 128, C], BF16, kind="Internal")
    d_xtok = nc.dram_tensor("xtok_d", [NCH, 128, INTER], BF16, kind="Internal")
    d_sall = nc.dram_tensor("sall_d", [NCH, 128, INTER], BF16, kind="Internal")
    d_rel = nc.dram_tensor("rel_d", [NCH, NH * C], F32, kind="Internal")
    d_scb = nc.dram_tensor("scb_d", [8, INTER], F32, kind="Internal")
    d_ccin = nc.dram_tensor("cc_in", [129, INTER], F32, kind="Internal")
    d_ccout = nc.dram_tensor("cc_out", [8 * 129, INTER], F32, kind="Internal", addr_space="Shared")

    p = Prog(nc)
    sb = p.sb
    xn = sb("xn", [128, KT * TW], BF16)
    AR1 = sb("AR1", [128, 16384], F32)
    AR2 = sb("AR2", [128, 18459], F32)
    Bf = sb("Bf", [128, T], BF16)
    Cf = sb("Cf", [128, T], BF16)
    Bt = sb("Bt", [128, NCH * C], BF16)
    dt_s = sb("dt_s", [NH, T], F32)
    ga = sb("ga", [NH, T], F32)
    dtok = sb("dtok", [128, NCH * NH], F32)
    rtok = sb("rtok", [128, NCH * NH], F32)
    identF = sb("identF", [128, 128], F32)
    maskle = sb("maskle", [128, 128], F32)
    onesb = sb("onesb", [128, 1], BF16)
    onesrB = sb("onesrB", [1, 128], BF16)
    onesrF = sb("onesrF", [1, 128], F32)
    convw_s = sb("convw_s", [128, 72], F32)
    convb_s = sb("convb_s", [128, 18], F32)
    dtb_s = sb("dtb_s", [NH, 1], F32)
    acol = sb("acol", [NH, 1], F32)
    b1c = sb("b1c", [128, 32], F32)
    b2c = sb("b2c", [128, 8], F32)
    c155 = sb("c155", [128, 1], F32)
    c1e5 = sb("c1e5", [128, 1], F32)
    nw_s = sb("nw_s", [128, 16], F32)
    sel_s = sb("sel_s", [8, 1], F32)
    btw_s = sb("btw_s", [8, 8], F32)
    zcol = sb("zcol", [NH, 1], F32)
    gaex = sb("gaex", [NH, NCH], F32)
    cdee = sb("cdee", [NH, NCH], F32)
    crow = sb("crow", [1, 32], F32)
    rrow = sb("rrow", [1, 128], F32)
    smallF = sb("smallF", [NH, C], F32)
    smallF2 = sb("smallF2", [NH, C], F32)
    db_s = sb("db_s", [128, NH], F32)
    gls = sb("gls", [8, 32], F32)
    esc = sb("esc", [8, 32], F32)
    PSA = p.es.enter_context(nc.psum_tensor("PSA", [128, 2048], F32))
    PSB = p.es.enter_context(nc.psum_tensor("PSB", [128, 2048], F32))

    S = "sync"
    G = "gpsimd"

    p.dma(S, convw_s[:, :].rearrange("p (ct k) -> p ct k", ct=18),
          bass.AP(d_convw, 0, [[4, 128], [512, 18], [1, 4]]))
    p.dma(S, convb_s[:, :], bass.AP(d_convb, 0, [[1, 128], [128, 18]]))
    p.dma(S, dtb_s[:, :], d_dtb.ap().unsqueeze(1))
    p.dma(S, acol[:, :], d_alog.ap().unsqueeze(1))
    p.dma(S, b1c[:, :], bass.AP(d_b1, 0, [[1, 128], [128, 32]]))
    p.dma(S, b2c[:, :], bass.AP(d_b2, 0, [[1, 128], [128, 8]]))
    p.dma(S, nw_s[:, :], bass.AP(d_normw, 0, [[1, 128], [128, 16]]))
    p.dma(S, sel_s[:, :], d_sel.ap())
    p.dma(S, btw_s[:, :], d_btwT.ap())
    p.dma(S, db_s[:, :], d_dpar.ap().unsqueeze(0).unsqueeze(1).to_broadcast([1, 128, NH]))
    iot_p = _carve(AR2, F32, 0, 128)
    iot_f = _carve(AR2, F32, 512, 128)
    p.op(G, lambda e: e.iota(iot_p, [[0, 128]], channel_multiplier=1,
                             allow_small_or_imprecise_dtypes=True), inc="GP")
    p.op(G, lambda e: e.iota(iot_f, [[1, 128]], channel_multiplier=0,
                             allow_small_or_imprecise_dtypes=True), inc="GP")
    p.wait("vector", "GP")
    p.dwait("vector", "sync")
    V = "vector"
    p.op(V, lambda e: e.tensor_tensor(identF[:, :], iot_p, iot_f, ALU.is_equal), inc="VE")
    p.op(V, lambda e: e.tensor_tensor(maskle[:, :], iot_p, iot_f, ALU.is_le), inc="VE")
    p.op(V, lambda e: e.tensor_scalar(onesb[:, :], identF[:, 0:1], 0.0, 1.0, ALU.mult, ALU.add), inc="VE")
    p.op(V, lambda e: e.tensor_scalar(onesrB[:, :], identF[0:1, :], 0.0, 1.0, ALU.mult, ALU.add), inc="VE")
    p.op(V, lambda e: e.tensor_scalar(onesrF[:, :], identF[0:1, :], 0.0, 1.0, ALU.mult, ALU.add), inc="VE")
    p.op(V, lambda e: e.tensor_scalar(zcol[:, :], dtb_s[:, :], 0.0, None, ALU.mult), inc="VE")
    p.op(V, lambda e: e.tensor_scalar(b1c[:, :], b1c[:, :], 256.0, None, ALU.mult), inc="VE")
    p.op(V, lambda e: e.tensor_scalar(b2c[:, :], b2c[:, :], 256.0, None, ALU.mult), inc="VE")
    p.op(V, lambda e: e.tensor_scalar(c155[:, :], identF[:, 0:1], 0.0, -0.5, ALU.mult, ALU.add), inc="VE")
    p.op(V, lambda e: e.tensor_scalar(c1e5[:, :], identF[:, 0:1], 0.0, 1e-5, ALU.mult, ALU.add), inc="VE")
    p.ewait("scalar", p.qstate["sync"]["last"])
    p.op("scalar", lambda e: e.activation(acol[:, :], acol[:, :], ACTF.Exp, bias=0.0, scale=1.0), inc="AC")
    p.wait(V, "AC")
    p.op(V, lambda e: e.tensor_scalar(acol[:, :], acol[:, :], -1.0, None, ALU.mult), inc="VE")

    u_t = [_carve(AR1, F32, k * TW * 4, TW) for k in range(7)]
    u_t.append(_carve(AR2, F32, 1024 * 4, TW))
    ub_t = [_carve(AR2, BF16, 16384, TW * KT)[:, k * TW:(k + 1) * TW] for k in range(KT)]
    u2b = _carve(AR2, BF16, 49200, TW)
    xsc = _carve(AR2, F32, 53600, 512)
    mrowS = _carve(AR2, F32, 55648, 512, parts=1)
    rrowS = _carve(AR2, F32, 57696, 512, parts=1)
    for k in range(KT):
        p.dma(S, u_t[k], d_u.ap()[k * 128:(k + 1) * 128, :])
    p.barrier("LD")
    for k in range(KT):
        p.op(V, lambda e, k=k: e.tensor_copy(ub_t[k], u_t[k]), inc="VE")
    ncs = [(0, 512), (512, 512), (1024, 512), (1536, 512), (2048, 3)]
    mrowS2 = _carve(AR2, F32, 59744, 512, parts=1)
    rrowS2 = _carve(AR2, F32, 61792, 512, parts=1)
    ve_ub = p.sems["VE"].n
    ac_sq, mm_stats, ve_var, ac_sqrt, ve_recip, mm_bcast, ve_xn = {}, {}, {}, {}, {}, {}, {}
    u2bB = _carve(AR2, BF16, 63840, 4096)

    def emit_squares(j):
        if j >= len(ncs):
            return
        c0, w = ncs[j]
        if j >= 1:
            p.wait("scalar", "MM", mm_stats[j - 1])
        for k in range(KT):
            p.op("scalar", lambda e, k=k, c0=c0, w=w: e.activation(
                u2bB[:, k * w:(k + 1) * w], u_t[k][:, c0:c0 + w], ACTF.Square, bias=0.0, scale=1.0), inc="AC")
        ac_sq[j] = p.sems["AC"].n

    emit_squares(0)
    for j, (c0, w) in enumerate(ncs):
        sb0 = (j % 2) * 1024
        pa0 = (j % 2) * 1024
        mro = mrowS if j % 2 == 0 else mrowS2
        rro = rrowS if j % 2 == 0 else rrowS2
        p.wait("tensor", "AC", ac_sq[j])
        p.wait("tensor", "VE", ve_ub)
        if j - 2 >= 0:
            p.wait("tensor", "VE", ve_recip[j - 2])
        for k in range(KT):
            p.op("tensor", lambda e, k=k, c0=c0, w=w, sb0=sb0: e.matmul(
                PSB[0:1, sb0:sb0 + w], onesb[:, :], ub_t[k][:, c0:c0 + w],
                start=(k == 0), stop=(k == KT - 1)), inc="MM")
            p.op("tensor", lambda e, k=k, w=w, sb0=sb0: e.matmul(
                PSB[0:1, sb0 + 512:sb0 + 512 + w], onesb[:, :], u2bB[:, k * w:(k + 1) * w],
                start=(k == 0), stop=(k == KT - 1)), inc="MM")
        mm_stats[j] = p.sems["MM"].n
        emit_squares(j + 1)
        p.wait(V, "MM", mm_stats[j])
        p.op(V, lambda e, w=w, sb0=sb0, mro=mro: e.tensor_scalar(
            mro[:, 0:w], PSB[0:1, sb0:sb0 + w], 1.0 / 1024.0, None, ALU.mult), inc="VE")
        p.op(V, lambda e, w=w, mro=mro, rro=rro: e.tensor_tensor(rro[:, 0:w], mro[:, 0:w], mro[:, 0:w], ALU.mult), inc="VE")
        p.op(V, lambda e, w=w, sb0=sb0, rro=rro: e.scalar_tensor_tensor(
            PSB[0:1, sb0 + 512:sb0 + 512 + w], PSB[0:1, sb0 + 512:sb0 + 512 + w], 1.0 / 1024.0,
            rro[:, 0:w], ALU.mult, ALU.subtract), inc="VE")
        p.op(V, lambda e, w=w, sb0=sb0: e.tensor_scalar(
            PSB[0:1, sb0 + 512:sb0 + 512 + w], PSB[0:1, sb0 + 512:sb0 + 512 + w], 1e-5, None, ALU.add), inc="VE")
        ve_var[j] = p.sems["VE"].n
        p.wait("scalar", "VE", ve_var[j])
        p.op("scalar", lambda e, w=w, sb0=sb0: e.activation(
            PSB[0:1, sb0 + 512:sb0 + 512 + w], PSB[0:1, sb0 + 512:sb0 + 512 + w], ACTF.Sqrt, bias=0.0, scale=1.0), inc="AC")
        ac_sqrt[j] = p.sems["AC"].n
        p.wait(V, "AC", ac_sqrt[j])
        p.op(V, lambda e, w=w, sb0=sb0, rro=rro: e.reciprocal(rro[:, 0:w], PSB[0:1, sb0 + 512:sb0 + 512 + w]), inc="VE")
        p.op(V, lambda e, w=w, mro=mro, rro=rro: e.tensor_tensor(mro[:, 0:w], mro[:, 0:w], rro[:, 0:w], ALU.mult), inc="VE")
        ve_recip[j] = p.sems["VE"].n
        p.wait("tensor", "VE", ve_recip[j])
        if j - 2 >= 0:
            p.wait("tensor", "VE", ve_xn[j - 2])
        p.op("tensor", lambda e, w=w, pa0=pa0, mro=mro: e.matmul(
            PSA[:, pa0:pa0 + w], onesrF[:, :], mro[:, 0:w], start=True, stop=True), inc=None)
        p.op("tensor", lambda e, w=w, pa0=pa0, rro=rro: e.matmul(
            PSA[:, pa0 + 512:pa0 + 512 + w], onesrF[:, :], rro[:, 0:w], start=True, stop=True), inc="MM")
        mm_bcast[j] = p.sems["MM"].n
        p.wait(V, "MM", mm_bcast[j])
        for k in range(KT):
            p.op(V, lambda e, k=k, c0=c0, w=w, pa0=pa0: e.tensor_tensor(
                xsc[:, 0:w], u_t[k][:, c0:c0 + w], PSA[:, pa0 + 512:pa0 + 512 + w], ALU.mult), inc="VE")
            p.op(V, lambda e, k=k, c0=c0, w=w, pa0=pa0: e.tensor_tensor(
                xn[:, k * TW + c0:k * TW + c0 + w], xsc[:, 0:w], PSA[:, pa0:pa0 + w], ALU.subtract), inc="VE")
        ve_xn[j] = p.sems["VE"].n
    p.barrier("VE")

    xbc = _carve(AR2, BF16, 0, 18 * TW)
    winbc = _carve(AR1, BF16, 0, KT * 2336)
    wst1 = _carve(AR1, F32, 37376, 2336)
    wst2 = _carve(AR1, F32, 46720, 2336)
    wtok, ve_wc = {}, {}

    def win_dma(k):
        if k >= KT:
            return
        if k - 2 >= 0:
            p.wait(S, "VE", ve_wc[k - 2])
        st = wst1 if k % 2 == 0 else wst2
        toks = []
        for h in range(2):
            p.dma(S, st[:, h * 1168:(h + 1) * 1168],
                  d_win.ap()[k * 128:(k + 1) * 128, 2048 + h * 1168:2048 + (h + 1) * 1168])
            toks.append(p.dclose("sync"))
        wtok[k] = toks

    p.wait(S, "VE")
    win_dma(0)
    win_dma(1)
    for k in range(KT):
        st = wst1 if k % 2 == 0 else wst2
        for tk_ in wtok[k]:
            p.ewait(V, tk_)
        p.op(V, lambda e, k=k, st=st: e.tensor_copy(winbc[:, k * 2336:(k + 1) * 2336], st), inc="VE")
        ve_wc[k] = p.sems["VE"].n
        win_dma(k + 2)
    p.wait("tensor", "VE", ve_wc[KT - 1])
    ve_slot = {}
    for mt in range(18):
        for j, (c0, w) in enumerate(ncs):
            sl = j % 4
            pv = PSA[:, sl * 512:sl * 512 + w]
            if sl in ve_slot:
                p.wait("tensor", "VE", ve_slot[sl])
            for k in range(KT):
                p.op("tensor", lambda e, k=k, mt=mt, c0=c0, w=w, pv=pv: e.matmul(
                    pv, winbc[:, k * 2336 + mt * 128:k * 2336 + mt * 128 + 128],
                    xn[:, k * TW + c0:k * TW + c0 + w],
                    start=(k == 0), stop=(k == KT - 1)), inc="MM" if k == KT - 1 else None)
            p.wait(V, "MM", p.sems["MM"].n)
            p.op(V, lambda e, mt=mt, c0=c0, w=w, pv=pv: e.tensor_copy(
                xbc[:, mt * TW + c0:mt * TW + c0 + w], pv), inc="VE")
            ve_slot[sl] = p.sems["VE"].n
    dtraw = PSB[0:32, 0:2048]
    for j, (c0, w) in enumerate([(3, 512), (515, 512), (1027, 512), (1539, 512)]):
        sl = j % 2
        pv = PSA[0:32, sl * 512:sl * 512 + w]
        if sl in ve_slot:
            p.wait("tensor", "VE", ve_slot[sl])
        for k in range(KT):
            p.op("tensor", lambda e, k=k, c0=c0, w=w, pv=pv: e.matmul(
                pv, winbc[:, k * 2336 + 2304:k * 2336 + 2336],
                xn[:, k * TW + c0:k * TW + c0 + w],
                start=(k == 0), stop=(k == KT - 1)), inc="MM" if k == KT - 1 else None)
        p.wait(V, "MM", p.sems["MM"].n)
        p.op(V, lambda e, j=j, c0=c0, w=w, pv=pv: e.tensor_copy(dtraw[:, c0 - 3:c0 - 3 + w], pv), inc="VE")
        ve_slot[sl] = p.sems["VE"].n
    p.barrier("VE", "MM")
    ab_t = PSA[0:32, 0:2048]
    p.op(V, lambda e: e.tensor_scalar(dt_s[:, :], dtraw, dtb_s[:, :], 0.0, ALU.add, ALU.max), inc="VE")
    p.wait("scalar", "VE")
    p.op("scalar", lambda e: e.activation(ab_t, dtraw, ACTF.Abs, bias=dtb_s[:, :], scale=1.0), inc="AC")
    p.op("scalar", lambda e: e.activation(ab_t, ab_t, ACTF.Exp, bias=0.0, scale=-1.0), inc="AC")
    p.wait(V, "AC")
    p.op(V, lambda e: e.tensor_scalar(ab_t, ab_t, 1.0, None, ALU.add), inc="VE")
    p.wait("scalar", "VE")
    p.op("scalar", lambda e: e.activation(ab_t, ab_t, ACTF.Ln, bias=0.0, scale=1.0), inc="AC")
    p.wait(V, "AC")
    p.op(V, lambda e: e.tensor_tensor(dt_s[:, :], dt_s[:, :], ab_t, ALU.add), inc="VE")
    p.op(V, lambda e: e.tensor_scalar(ab_t, dt_s[:, :], acol[:, :], None, ALU.mult), inc="VE")
    p.op(V, lambda e: e.tensor_tensor_scan(ga[:, :], ab_t, dt_s[:, :], 0.0, ALU.add, ALU.bypass), inc="VE")
    p.op(V, lambda e: e.tensor_copy(gaex[:, :], bass.AP(ga, 127, [[T, NH], [128, NCH]])), inc="VE")
    p.op(V, lambda e: e.memset(smallF[:, 0:NCH], 0.0), inc="VE")
    p.op(V, lambda e: e.tensor_copy(smallF[:, 1:NCH], gaex[:, 0:NCH - 1]), inc="VE")
    p.op(V, lambda e: e.tensor_tensor(cdee[:, :], gaex[:, :], smallF[:, 0:NCH], ALU.subtract), inc="VE")
    p.wait("scalar", "VE")
    p.op("scalar", lambda e: e.activation(cdee[:, :], cdee[:, :], ACTF.Exp, bias=0.0, scale=1.0), inc="AC")
    p.barrier("AC", "VE")

    zbuf1 = _carve(AR1, BF16, 51200, 2048)
    zbuf2 = _carve(AR1, BF16, 47104, 2048)
    zncs = [(3, 512), (515, 512), (1027, 512), (1539, 512)]
    winz1 = _carve(AR1, BF16, 0, KT * 1024)
    winz2 = _carve(AR1, BF16, 24576, KT * 1024)
    zst1 = _carve(AR1, F32, 16384, 1024)
    zst2 = _carve(AR1, F32, 20480, 1024)
    ztok, ve_zc = {}, {}

    def z_dma(kk):
        if kk >= 2 * KT:
            return
        if kk - 2 >= 0:
            p.wait(S, "VE", ve_zc[kk - 2])
        p.dma(S, zst1 if kk % 2 == 0 else zst2,
              d_win.ap()[(kk % 8) * 128:(kk % 8) * 128 + 128, (kk // 8) * 1024:(kk // 8) * 1024 + 1024])
        ztok[kk] = p.dclose("sync")

    p.wait(S, "MM")
    p.wait(S, "VE")
    z_dma(0)
    z_dma(1)
    for kk in range(2 * KT):
        st = zst1 if kk % 2 == 0 else zst2
        wz = winz1 if kk < 8 else winz2
        p.ewait(V, ztok[kk])
        if kk == 0:
            p.wait(V, "MM")
        p.op(V, lambda e, kk=kk, st=st, wz=wz: e.tensor_copy(wz[:, (kk % 8) * 1024:(kk % 8) * 1024 + 1024], st), inc="VE")
        ve_zc[kk] = p.sems["VE"].n
        z_dma(kk + 2)
    ac_zslot = {}
    zs_tok = {}
    for half in range(2):
        wz = winz1 if half == 0 else winz2
        p.wait("tensor", "VE", ve_zc[half * 8 + 7])
        for mi in range(8):
            mt = half * 8 + mi
            zb = zbuf1 if mt % 2 == 0 else zbuf2
            if mt - 2 in zs_tok:
                p.ewait("scalar", zs_tok[mt - 2])
            for j, (c0, w) in enumerate(zncs):
                sl = j % 4
                pv = PSA[:, sl * 512:sl * 512 + w]
                if sl in ac_zslot:
                    p.wait("tensor", "AC", ac_zslot[sl])
                for k in range(KT):
                    p.op("tensor", lambda e, k=k, mi=mi, c0=c0, w=w, pv=pv, wz=wz: e.matmul(
                        pv, wz[:, k * 1024 + mi * 128:k * 1024 + mi * 128 + 128],
                        xn[:, k * TW + c0:k * TW + c0 + w],
                        start=(k == 0), stop=(k == KT - 1)), inc="MM" if k == KT - 1 else None)
                p.wait("scalar", "MM", p.sems["MM"].n)
                p.op("scalar", lambda e, j=j, w=w, pv=pv, zb=zb: e.activation(
                    zb[:, j * 512:j * 512 + w], pv, ACTF.Silu, bias=0.0, scale=1.0), inc="AC")
                ac_zslot[sl] = p.sems["AC"].n
            p.wait(G, "AC", ac_zslot[3])
            p.dma(G, bass.AP(d_zsil, mt * 128 * C, [[C, 128], [16 * 128 * C, NCH], [1, C]]),
                  zb[:, :].rearrange("p (c l) -> p c l", c=NCH))
            zs_tok[mt] = p.dclose("gpsimd")
    p.barrier("GD", "MM", "AC", "VE")

    diag = _carve(AR1, BF16, 0, 72 * 128)
    xtbuf1 = _carve(AR1, BF16, 18432, 2048)
    xtbuf2 = _carve(AR1, BF16, 36352, 2048)
    cbRf = _carve(AR1, F32, 22528, 2304, parts=1)
    cbR = _carve(AR1, BF16, 31744, 2304, parts=1)
    p.dma(S, cbRf, d_convb.ap().unsqueeze(0))
    cb_tok = p.dclose("sync")
    p.ewait(V, cb_tok)
    p.wait(V, "MM")
    p.op(V, lambda e: e.tensor_copy(cbR, cbRf), inc="VE")
    for ct in range(18):
        for tk in range(4):
            p.op(V, lambda e, ct=ct, tk=tk: e.tensor_scalar(
                diag[:, (ct * 4 + tk) * 128:(ct * 4 + tk) * 128 + 128],
                identF[:, :], convw_s[:, ct * 4 + tk:ct * 4 + tk + 1], None, ALU.mult), inc="VE")
    ve_diag = p.sems["VE"].n
    p.wait("tensor", "VE", ve_diag)
    ac_cgrp = {}
    xt_dtok = {}
    ac_B = None
    for c in range(NCH):
        xtb = xtbuf1 if c % 2 == 0 else xtbuf2
        if c - 2 in xt_dtok:
            p.ewait("scalar", xt_dtok[c - 2])
        for grp in range(4):
            if grp in ac_cgrp:
                p.wait("tensor", "AC", ac_cgrp[grp])
            for j in range(4):
                ct = grp * 4 + j
                pvt = PSA[:, grp * 512 + j * 128:grp * 512 + j * 128 + 128]
                for tk in range(4):
                    p.op("tensor", lambda e, c=c, ct=ct, tk=tk, j=j, pvt=pvt: e.matmul(
                        pvt, xbc[:, ct * TW + c * 128 + tk:ct * TW + c * 128 + tk + 128],
                        diag[:, (ct * 4 + tk) * 128:(ct * 4 + tk) * 128 + 128],
                        start=(tk == 0 and j == 0), stop=False), inc=None)
                p.op("tensor", lambda e, ct=ct, pvt=pvt, j=j: e.matmul(
                    pvt, onesrB[:, :], cbR[:, ct * 128:ct * 128 + 128],
                    start=False, stop=(j == 3)), inc="MM" if j == 3 else None)
            p.wait("scalar", "MM", p.sems["MM"].n)
            p.op("scalar", lambda e, grp=grp, xtb=xtb: e.activation(
                xtb[:, grp * 512:grp * 512 + 512], PSA[:, grp * 512:grp * 512 + 512],
                ACTF.Silu, bias=0.0, scale=1.0), inc="AC")
            ac_cgrp[grp] = p.sems["AC"].n
        p.wait(G, "AC", ac_cgrp[3])
        p.dma(G, d_xtok.ap()[c], xtb[:, :])
        xt_dtok[c] = p.dclose("gpsimd")
        pvb = PSB[:, 0:128]
        if ac_B is not None:
            p.wait("tensor", "AC", ac_B)
        for tk in range(4):
            p.op("tensor", lambda e, c=c, tk=tk, pvb=pvb: e.matmul(
                pvb, xbc[:, 16 * TW + c * 128 + tk:16 * TW + c * 128 + tk + 128],
                diag[:, (64 + tk) * 128:(64 + tk) * 128 + 128],
                start=(tk == 0), stop=False), inc=None)
        p.op("tensor", lambda e, pvb=pvb: e.matmul(
            pvb, onesrB[:, :], cbR[:, 2048:2176], start=False, stop=True), inc="MM")
        p.wait("scalar", "MM", p.sems["MM"].n)
        p.op("scalar", lambda e, c=c, pvb=pvb: e.activation(
            Bt[:, c * C:(c + 1) * C], pvb, ACTF.Silu, bias=0.0, scale=1.0), inc="AC")
        ac_B = p.sems["AC"].n
    ac_f = None
    for ct in (16, 17):
        for j, (c0, w) in enumerate(zncs):
            sl = j % 2
            pv = PSB[:, 512 + sl * 512:512 + sl * 512 + w]
            if (16, sl) in ac_cgrp:
                p.wait("tensor", "AC", ac_cgrp[(16, sl)])
            for tk in range(4):
                p.op("tensor", lambda e, ct=ct, tk=tk, c0=c0, w=w, pv=pv: e.matmul(
                    pv, diag[:, (ct * 4 + tk) * 128:(ct * 4 + tk) * 128 + 128],
                    xbc[:, ct * TW + c0 - 3 + tk:ct * TW + c0 - 3 + tk + w],
                    start=(tk == 0), stop=(tk == 3)), inc="MM" if tk == 3 else None)
            p.wait("scalar", "MM", p.sems["MM"].n)
            dst = Bf if ct == 16 else Cf
            p.op("scalar", lambda e, ct=ct, c0=c0, w=w, pv=pv, dst=dst: e.activation(
                dst[:, c0 - 3:c0 - 3 + w], pv, ACTF.Silu, bias=convb_s[:, ct:ct + 1], scale=1.0), inc="AC")
            ac_cgrp[(16, sl)] = p.sems["AC"].n
    p.barrier("AC", "MM", "VE", "GD")

    xbc_mm, xbc_ve, xbc_ac = p.sems["MM"].n, p.sems["VE"].n, p.sems["AC"].n
    s_run = _carve(AR1, F32, 0, 2048)
    crow_all = _carve(AR2, F32, 70144, 512)
    rowAll = _carve(AR2, F32, 65536, 512, parts=1)
    p.wait("tensor", "VE")
    p.wait("tensor", "AC")
    for cc in range(16):
        p.op("tensor", lambda e, cc=cc: e.transpose(
            PSB[0:1, cc * 32:(cc + 1) * 32], cdee[:, cc:cc + 1], identF[0:32, 0:32]),
            inc="MM" if cc == 15 else None)
    p.wait(V, "MM")
    p.op(V, lambda e: e.tensor_copy(rowAll, PSB[0:1, 0:512]), inc="VE")
    p.wait("tensor", "VE")
    for cc in range(16):
        p.op("tensor", lambda e, cc=cc: e.matmul(
            PSB[:, cc * 32:(cc + 1) * 32], onesrF[:, :], rowAll[:, cc * 32:(cc + 1) * 32],
            start=(cc == 0), stop=(cc == 15)), inc="MM" if cc == 15 else None)
    p.wait(V, "MM")
    p.op(V, lambda e: e.tensor_copy(crow_all, PSB[:, 0:512]), inc="VE")
    p.wait("scalar", "MM")
    xts1 = _carve(AR1, BF16, 8192, 2048)
    xts2 = _carve(AR1, BF16, 12288, 2048)
    xtw = _carve(AR1, BF16, 16384, 2048)
    ssc = _carve(AR1, BF16, 20480, 2048)
    w2bf = _carve(AR1, BF16, 0, 32768)
    w2pst = [_carve(AR2, F32, i * 4096, 1024) for i in range(4)]
    w1pst = [_carve(AR2, F32, 16384 + i * 4096, 1024) for i in range(4)]
    w1pob = [_carve(AR2, BF16, 32768 + i * 2048, 1024) for i in range(4)]
    tok_w2, tok_w1i, tok_w1o = {}, {}, {}
    ve_w2c, ve_w1c = {}, {}
    p.wait(S, "MM", xbc_mm)
    p.wait(S, "VE", xbc_ve)
    p.wait(S, "AC", xbc_ac)
    p.wait("scalar", "MM", xbc_mm)
    p.ewait("scalar", zs_tok[14])
    p.ewait("scalar", zs_tok[15])
    p.ewait("scalar", xt_dtok[14])
    p.ewait("scalar", xt_dtok[15])

    def w1_in(j):
        if j > 31:
            return
        if j - 4 >= 0:
            p.wait(S, "AC", ve_w1c[j - 4])
        p.dma(S, w1pst[j % 4], d_w1.ap()[j].rearrange("p k c -> p (k c)"))
        tok_w1i[j] = p.dclose("sync")

    def w1_cast_out(j):
        if j < 0 or j > 31:
            return
        p.ewait("scalar", tok_w1i[j])
        if j - 4 >= 0:
            p.ewait("scalar", tok_w1o[j - 4])
        p.op("scalar", lambda e, j=j: e.activation(
            w1pob[j % 4], w1pst[j % 4], ACTF.Copy, bias=0.0, scale=1.0), inc="AC")
        ve_w1c[j] = p.sems["AC"].n
        p.wait(G, "AC", ve_w1c[j])
        p.dma(G, d_w1b.ap()[j], w1pob[j % 4])
        tok_w1o[j] = p.dclose("gpsimd")

    def w2_in(i):
        if i > 31:
            return
        if i - 4 >= 12:
            p.wait(S, "AC", ve_w2c[i - 4])
        p.dma(S, w2pst[(i - 12) % 4], d_w2.ap()[i * 128:(i + 1) * 128, :])
        tok_w2[i] = p.dclose("sync")

    def w2_cast(i):
        if i < 12 or i > 31:
            return
        p.ewait("scalar", tok_w2[i])
        p.op("scalar", lambda e, i=i: e.activation(
            w2bf[:, i * 1024:(i + 1) * 1024], w2pst[(i - 12) % 4], ACTF.Copy, bias=0.0, scale=1.0), inc="AC")
        ve_w2c[i] = p.sems["AC"].n

    p.op(V, lambda e: e.memset(s_run, 0.0), inc="VE")
    p.dclose("sync")
    p.dma(S, xts1, d_xtok.ap()[0])
    xt_tok = [p.dclose("sync")]
    smallFB = _carve(AR2, F32, 45056, 128, parts=32)
    smallF2B = _carve(AR2, F32, 45568, 128, parts=32)
    xtwB = _carve(AR2, BF16, 46080, 2048)
    ve_xtw, mm_tp, ve_sF, ac_sF, mm_st, ve_sr = {}, {}, {}, {}, {}, {}
    rel_tok, sall_tok = {}, {}

    def d_front(c):
        if c >= NCH:
            return
        sF = smallF if c % 2 == 0 else smallFB
        sF2 = smallF2 if c % 2 == 0 else smallF2B
        if c >= 2:
            p.ewait("scalar", rel_tok[c - 2])
            p.wait("scalar", "MM", mm_tp[c - 2])
        p.op("scalar", lambda e, c=c, sF=sF: e.activation(
            sF[:, :], ga[:, c * C:(c + 1) * C], ACTF.Exp, bias=gaex[:, c:c + 1], scale=-1.0), inc="AC")
        ac_sF[c] = p.sems["AC"].n
        p.wait(V, "AC", ac_sF[c])
        if c >= 2:
            p.wait(V, "MM", mm_tp[c - 2])
        p.op(V, lambda e, c=c, sF=sF, sF2=sF2: e.tensor_tensor(
            sF2[:, :], dt_s[:, c * C:(c + 1) * C], sF[:, :], ALU.mult), inc="VE")
        if c == 0:
            p.op(V, lambda e, c=c, sF=sF: e.tensor_scalar(
                sF[:, :], ga[:, c * C:(c + 1) * C], zcol[:, :], None, ALU.subtract), inc="VE")
        else:
            p.op(V, lambda e, c=c, sF=sF: e.tensor_scalar(
                sF[:, :], ga[:, c * C:(c + 1) * C], gaex[:, c - 1:c], None, ALU.subtract), inc="VE")
        ve_sF[c] = p.sems["VE"].n

    def d_trans(c):
        if c >= NCH:
            return
        sF = smallF if c % 2 == 0 else smallFB
        sF2 = smallF2 if c % 2 == 0 else smallF2B
        t0 = 512 if c % 2 == 0 else 608
        p.wait("tensor", "VE", ve_sF[c])
        if c >= 2:
            p.wait("tensor", "VE", ve_xtw[c - 2])
        p.op("tensor", lambda e, t0=t0, sF2=sF2: e.transpose(PSB[:, t0:t0 + 32], sF2[:, :], identF[0:32, 0:32]), inc=None)
        p.op("tensor", lambda e, t0=t0, sF=sF: e.transpose(PSB[:, t0 + 32:t0 + 64], sF[:, :], identF[0:32, 0:32]), inc=None)
        p.op("tensor", lambda e, c=c, t0=t0: e.transpose(PSB[:, t0 + 64:t0 + 96], dt_s[:, c * C:(c + 1) * C], identF[0:32, 0:32]), inc="MM")
        mm_tp[c] = p.sems["MM"].n

    def d_copies(c):
        if c >= NCH:
            return
        sF = smallF if c % 2 == 0 else smallFB
        xtw_c = xtw if c % 2 == 0 else xtwB
        xts = xts1 if c % 2 == 0 else xts2
        t0 = 512 if c % 2 == 0 else 608
        p.ewait(V, xt_tok[c])
        p.wait(V, "MM", mm_tp[c])
        if c >= 2:
            p.wait(V, "MM", mm_st[c - 2])
        p.op(V, lambda e, c=c, t0=t0: e.tensor_copy(rtok[:, c * NH:(c + 1) * NH], PSB[:, t0 + 32:t0 + 64]), inc="VE")
        p.op(V, lambda e, c=c, t0=t0: e.tensor_copy(dtok[:, c * NH:(c + 1) * NH], PSB[:, t0 + 64:t0 + 96]), inc="VE")
        p.op(V, lambda e, xts=xts, xtw_c=xtw_c, t0=t0: e.tensor_tensor(
            xtw_c.rearrange("p (h q) -> p h q", h=NH),
            xts.rearrange("p (h q) -> p h q", h=NH),
            PSB[:, t0:t0 + 32].unsqueeze(2).to_broadcast([128, NH, HD]), ALU.mult), inc="VE")
        ve_xtw[c] = p.sems["VE"].n
        p.wait(G, "VE", ve_xtw[c])
        p.dma(G, d_rel.ap()[c], sF[:, :])
        rel_tok[c] = p.dclose("gpsimd")

    def d_states(c):
        xtw_c = xtw if c % 2 == 0 else xtwB
        p.wait("tensor", "VE", ve_xtw[c])
        if c >= 1:
            p.wait("tensor", "VE", ve_sr[c - 1])
        for j in range(4):
            p.op("tensor", lambda e, c=c, j=j, xtw_c=xtw_c: e.matmul(
                PSA[:, j * 512:(j + 1) * 512], Bt[:, c * C:(c + 1) * C], xtw_c[:, j * 512:(j + 1) * 512],
                start=True, stop=True), inc="MM" if j == 3 else None)
        mm_st[c] = p.sems["MM"].n

    def d_tail(c):
        if c - 1 in sall_tok:
            p.ewait(V, sall_tok[c - 1])
        p.op(V, lambda e: e.tensor_copy(ssc, s_run), inc="VE")
        p.wait(G, "VE")
        p.dma(G, d_sall.ap()[c], ssc)
        sall_tok[c] = p.dclose("gpsimd")
        p.wait(V, "MM", mm_st[c])
        p.op(V, lambda e, c=c: e.tensor_tensor(
            s_run.rearrange("p (h q) -> p h q", h=NH),
            s_run.rearrange("p (h q) -> p h q", h=NH),
            crow_all[:, c * 32:(c + 1) * 32].unsqueeze(2).to_broadcast([128, NH, HD]), ALU.mult), inc="VE")
        p.op(V, lambda e: e.tensor_tensor(s_run, s_run, PSA[:, :], ALU.add), inc="VE")
        ve_sr[c] = p.sems["VE"].n

    d_front(0)
    d_trans(0)
    d_copies(0)
    for c in range(NCH):
        if c + 1 < NCH:
            if c >= 1:
                p.wait(S, "VE", ve_xtw[c - 1])
            p.dma(S, xts2 if c % 2 == 0 else xts1, d_xtok.ap()[c + 1])
            xt_tok.append(p.dclose("sync"))
        d_front(c + 1)
        d_states(c)
        d_trans(c + 1)
        d_copies(c + 1)
        d_tail(c)
        w1_in(2 * c)
        w1_in(2 * c + 1)
        w1_cast_out(2 * c - 2)
        w1_cast_out(2 * c - 1)
        w2_in(12 + 2 * c)
        w2_in(13 + 2 * c)
        w2_cast(10 + 2 * c)
        w2_cast(11 + 2 * c)
    w1_cast_out(30)
    w1_cast_out(31)
    p.barrier("VE")
    zrow = _carve(AR1, F32, 8192, 2016, parts=1)
    p.op(V, lambda e: e.memset(zrow, 0.0), inc="VE")
    p.barrier("VE")
    p.dma(G, d_ccin.ap()[0:128, :], s_run)
    p.dma(G, d_ccin.ap()[128:129, 0:32], ga[:, T - 1:T])
    p.dma(G, d_ccin.ap()[128:129, 32:2048], zrow)
    p.dwait(G, "gpsimd")
    p.op(G, lambda e: e.collective_compute(
        "AllGather", ALU.bypass, replica_groups=[list(range(8))],
        ins=[d_ccin.ap().opt()], outs=[d_ccout.ap().opt()]), inc="CC")

    ve_p1d, ac_p1d, mm_p1d = p.sems["VE"].n, p.sems["AC"].n, p.sems["MM"].n
    w2h = [_carve(AR2, F32, 40960 + i * 4096, 1024) for i in range(4)]
    p.wait(S, "MM", mm_p1d)
    p.ewait(S, rel_tok[15])
    p.dma_barrier("scalar")
    p.wait("scalar", "MM", mm_p1d)
    tok_w2h = {}
    for j in range(14):
        if j < 12:
            if j >= 4:
                p.wait(S, "AC", ve_w2c[j - 4])
            p.dma(S, w2h[j % 4], d_w2.ap()[j * 128:(j + 1) * 128, :])
            tok_w2h[j] = p.dclose("sync")
        if j >= 2:
            i = j - 2
            p.ewait("scalar", tok_w2h[i])
            p.op("scalar", lambda e, i=i: e.activation(
                w2bf[:, i * 1024:(i + 1) * 1024], w2h[i % 4], ACTF.Copy, bias=0.0, scale=1.0), inc="AC")
            ve_w2c[i] = p.sems["AC"].n
    ve_prepass = p.sems["AC"].n

    NPB = 5
    stgp = ([_carve(AR2, BF16, 8192 + i * 4096, 2048) for i in range(3)] +
            [_carve(AR2, BF16, 61440 + i * 4096, 2048) for i in range(2)])
    qbuf = [_carve(AR2, I32, 20480, 2048), _carve(AR2, I32, 0, 2048)]
    scA = _carve(AR2, F32, 28672, 2048)
    scB = _carve(AR2, F32, 36864, 2048)
    scC = _carve(AR2, F32, 45056, 2048)
    hgq = [_carve(AR2, BF16, 53248, 2048), _carve(AR2, BF16, 57344, 2048)]
    tok_stg, mm_after, evac_ac = {}, {}, {}
    ve_V1, ve_xhi, ve_y = {}, {}, {}
    ac_A1, ac_A3, ac_A5 = {}, {}, {}
    mm_w2_after, ve_obuf, tok_out = {}, {}, {}

    def issue_stage(gm):
        if gm >= 256 or gm % 2 != 0:
            return
        pg = gm // 2
        p.wait(S, "AC", ve_prepass)
        if pg - NPB >= 0:
            p.wait(S, "MM", mm_after[2 * (pg - NPB) + 1])
        if gm < 32:
            p.ewait(S, tok_w1o[gm])
            p.ewait(S, tok_w1o[gm + 1])
        m0 = gm % 32
        p.dma(S, stgp[pg % NPB].rearrange("p (m c) -> p m c", m=2),
              bass.AP(d_w1b, m0 * 131072, [[1024, 128], [131072, 2], [1, 1024]]))
        tok_stg[pg] = p.dclose("sync")

    def emit_chain1(ms, h, q, qo, hg):
        c0, c1 = h * 1024, (h + 1) * 1024
        p.op("scalar", lambda e, q=q, qo=qo, c0=c0, c1=c1: e.activation(
            qo[:, c0:c1], q[:, c0:c1], ACTF.Identity, bias=c155[:, 0:1], scale=1.0 / 64.0), inc="AC")
        ac_A1[(ms, h)] = p.sems["AC"].n
        p.wait(V, "AC", ac_A1[(ms, h)])
        p.op(V, lambda e, qo=qo, c0=c0, c1=c1: e.tensor_copy(scA[:, c0:c1], qo[:, c0:c1]), inc="VE")
        ve_V1[(ms, h)] = p.sems["VE"].n
        p.op(V, lambda e, q=q, c0=c0, c1=c1: e.scalar_tensor_tensor(
            scB[:, c0:c1], q[:, c0:c1], 1.0 / 64.0, scA[:, c0:c1], ALU.mult, ALU.subtract), inc="VE")
        if ms % 4 == 0 and ms >= 4:
            p.ewait(V, tok_out[ms // 4 - 1][h])
        p.op(V, lambda e, c0=c0, c1=c1: e.tensor_scalar(scC[:, c0:c1], scA[:, c0:c1], 0.25, None, ALU.mult), inc="VE")
        p.op(V, lambda e, c0=c0, c1=c1: e.tensor_scalar(scA[:, c0:c1], scC[:, c0:c1], 0.25, None, ALU.add), inc="VE")
        ve_xhi[(ms, h)] = p.sems["VE"].n

    def emit_chain2(ms, h):
        c0, c1 = h * 1024, (h + 1) * 1024
        if ms % 4 == 0 and ms >= 4:
            p.ewait("scalar", tok_out[ms // 4 - 1][h])
        p.wait("scalar", "VE", ve_xhi[(ms, h)])
        p.op("scalar", lambda e, c0=c0, c1=c1: e.activation(
            scC[:, c0:c1], scC[:, c0:c1], ACTF.Gelu, bias=0.0, scale=1.0), inc="AC")
        p.op("scalar", lambda e, c0=c0, c1=c1: e.activation(
            scA[:, c0:c1], scA[:, c0:c1], ACTF.Gelu, bias=0.0, scale=1.0), inc="AC")
        ac_A3[(ms, h)] = p.sems["AC"].n
        p.wait(V, "AC", ac_A3[(ms, h)])
        p.op(V, lambda e, c0=c0, c1=c1: e.tensor_tensor(scA[:, c0:c1], scA[:, c0:c1], scC[:, c0:c1], ALU.subtract), inc="VE")
        p.op(V, lambda e, c0=c0, c1=c1: e.tensor_tensor(scB[:, c0:c1], scB[:, c0:c1], scA[:, c0:c1], ALU.mult), inc="VE")
        p.op(V, lambda e, c0=c0, c1=c1: e.tensor_tensor(scC[:, c0:c1], scC[:, c0:c1], scB[:, c0:c1], ALU.add), inc="VE")
        ve_y[(ms, h)] = p.sems["VE"].n

    def emit_chain3(ms, h, q, hg):
        c0, c1 = h * 1024, (h + 1) * 1024
        p.wait("scalar", "VE", ve_y[(ms, h)])
        p.op("scalar", lambda e, q=q, c0=c0, c1=c1: e.activation(
            q[:, c0:c1], scC[:, c0:c1], ACTF.Copy, bias=0.0, scale=256.0), inc="AC")
        if h == 0 and ms - 2 >= 0:
            p.wait("scalar", "MM", mm_w2_after[ms - 2])
        p.op("scalar", lambda e, q=q, hg=hg, c0=c0, c1=c1: e.activation(
            hg[:, c0:c1], q[:, c0:c1], ACTF.Copy, bias=0.0, scale=1.0), inc="AC")
        ac_A5[(ms, h)] = p.sems["AC"].n

    def emit_w2_half(pend_w2, half):
        ms, hg = pend_w2
        msub = ms % 4
        if half == 0:
            p.wait("tensor", "AC", ve_w2c[msub * 8 + 7])
        p.wait("tensor", "AC", ac_A5[(ms, half)])
        for m8w in range(4 * half, 4 * half + 4):
            mw = msub * 8 + m8w
            for m2 in range(8):
                stt = (msub == 0 and m8w == 0 and m2 % 2 == 0)
                sp = (msub == 3 and m8w == 7 and m2 % 2 == 1)
                last = (m8w == 7 and m2 == 7)
                p.op("tensor", lambda e, mw=mw, m2=m2, hg=hg, m8w=m8w, stt=stt, sp=sp: e.matmul(
                    PSA[:, m2 * 256:(m2 + 1) * 256],
                    w2bf[:, mw * 1024 + m2 * 128:mw * 1024 + m2 * 128 + 128],
                    hg[:, m8w * 256:(m8w + 1) * 256],
                    start=stt, stop=sp), inc="MM" if last else None)
        if half == 1:
            mm_w2_after[ms] = p.sems["MM"].n

    p.wait("tensor", "VE", ve_p1d)
    p.wait("tensor", "AC", ac_p1d)
    for gm0 in range(0, 8, 2):
        issue_stage(gm0)
    pend_w2 = None
    pend = None
    for j in (28, 29, 30, 31):
        p.ewait(V, tok_w1o[j])
    for g in range(8):
        gc0 = 3 + g * 256
        for msub in range(4):
            ms = g * 4 + msub
            par = msub % 2
            q, qo, hg = qbuf[par], qbuf[1 - par], hgq[ms % 2]
            for m8 in range(8):
                m = msub * 8 + m8
                gm = g * 32 + m
                issue_stage(gm + 8)
                bank = m8 % 4
                p.ewait("tensor", tok_stg[gm // 2])
                if bank in evac_ac:
                    p.wait("tensor", "AC", evac_ac[bank])
                pv = PSB[:, bank * 512:bank * 512 + 256]
                sg = stgp[(gm // 2) % NPB]
                sc0 = (gm % 2) * 1024
                for k in range(KT):
                    p.op("tensor", lambda e, k=k, sg=sg, pv=pv, gc0=gc0, sc0=sc0: e.matmul(
                        pv, sg[:, sc0 + k * 128:sc0 + (k + 1) * 128], xn[:, k * TW + gc0:k * TW + gc0 + 256],
                        start=(k == 0), stop=(k == KT - 1)), inc="MM" if k == KT - 1 else None)
                mm_after[gm] = p.sems["MM"].n
                if m8 in (0, 4):
                    h = m8 // 4
                    if msub == 0:
                        if g > 0:
                            p.wait("scalar", "VE", ve_obuf[g - 1])
                    else:
                        p.wait("scalar", "VE", ve_V1[(ms - 1, h)])
                p.wait("scalar", "MM", mm_after[gm])
                p.op("scalar", lambda e, m=m, m8=m8, pv=pv, q=q: e.activation(
                    q[:, m8 * 256:m8 * 256 + 256], pv, ACTF.Identity, bias=b1c[:, m:m + 1], scale=256.0), inc="AC")
                evac_ac[bank] = p.sems["AC"].n
                if m8 == 1 and pend is not None:
                    emit_chain2(pend[0], 0)
                if m8 == 2 and pend is not None:
                    emit_chain2(pend[0], 1)
                if m8 == 3 and pend is not None:
                    emit_chain3(pend[0], 0, pend[1], pend[3])
                if m8 == 4:
                    emit_chain1(ms, 0, q, qo, hg)
                if m8 == 5 and pend is not None:
                    emit_chain3(pend[0], 1, pend[1], pend[3])
                    pend = None
            emit_chain1(ms, 1, q, qo, hg)
            if pend_w2 is not None:
                emit_w2_half(pend_w2, 0)
                emit_w2_half(pend_w2, 1)
                pend_w2 = None
            if msub == 3:
                emit_chain2(ms, 0)
                emit_chain2(ms, 1)
                emit_chain3(ms, 0, q, hg)
                emit_chain3(ms, 1, q, hg)
            else:
                pend = (ms, q, qo, hg)
            pend_w2 = (ms, hg)
        emit_w2_half(pend_w2, 0)
        emit_w2_half(pend_w2, 1)
        pend_w2 = None
        p.wait("scalar", "MM", mm_w2_after[g * 4 + 3])
        tok_out[g] = []
        for hf in range(2):
            for m2 in range(4 * hf, 4 * hf + 4):
                p.op("scalar", lambda e, m2=m2: e.activation(
                    qbuf[0][:, m2 * 256:m2 * 256 + 256], PSA[:, m2 * 256:(m2 + 1) * 256],
                    ACTF.Identity, bias=b2c[:, m2:m2 + 1], scale=1.0), inc="AC")
            p.wait(V, "AC", p.sems["AC"].n)
            p.op(V, lambda e, hf=hf: e.tensor_scalar(
                scC[:, hf * 1024:(hf + 1) * 1024], qbuf[0][:, hf * 1024:(hf + 1) * 1024],
                1.0 / 256.0, None, ALU.mult), inc="VE")
            p.wait(G, "VE", p.sems["VE"].n)
            p.dma(G, bass.AP(d_out, g * 256 + hf * 4 * 128 * T, [[T, 128], [128 * T, 4], [1, 256]]),
                  scC[:, hf * 1024:(hf + 1) * 1024].rearrange("p (m l) -> p m l", m=4))
            tok_out[g].append(p.dclose("gpsimd"))
        ve_obuf[g] = p.sems["VE"].n
    p.barrier("GD", "MM", "AC", "VE", "GP")


    p.barrier("CC", "GD", "LD", "VE", "MM", "AC", "GP")
    scb8 = _carve(AR2, F32, 37376, 2048, parts=8)
    srs = _carve(AR1, F32, 0, 2048)
    scbc = _carve(AR1, F32, 8192, 2048)
    tmp2 = _carve(AR1, F32, 16384, 2048)
    s_i2 = _carve(AR2, F32, 8704, 2048)
    s_i2b = _carve(AR2, BF16, 16896, 2048)
    dii = _carve(AR2, BF16, 512, NH * C)
    p.op(V, lambda e: e.tensor_tensor(
        dii.rearrange("p (h c) -> p h c", h=NH),
        identF[:, :].unsqueeze(1).to_broadcast([128, NH, 128]),
        db_s[:, :].unsqueeze(2).to_broadcast([128, NH, 128]), ALU.mult), inc="VE")
    p.dma(S, gls[:, :], bass.AP(d_ccout, 128 * INTER, [[129 * INTER, 8], [1, 32]]))
    p.barrier("LD")
    p.ewait("tensor", p.qstate["sync"]["last"])
    p.op("tensor", lambda e: e.matmul(PSB[0:8, 0:32], btw_s[:, :], gls[:, :], start=True, stop=True), inc="MM")
    p.wait("scalar", "MM")
    p.op("scalar", lambda e: e.activation(esc[:, :], PSB[0:8, 0:32], ACTF.Exp, bias=0.0, scale=1.0), inc="AC")
    p.wait(V, "AC")
    p.op(V, lambda e: e.tensor_scalar(esc[:, :], esc[:, :], sel_s[:, :], None, ALU.mult), inc="VE")
    p.op(V, lambda e: e.tensor_copy(
        scb8.rearrange("r (h q) -> r h q", h=NH),
        esc[:, :].unsqueeze(2).to_broadcast([8, NH, HD])), inc="VE")
    p.wait(G, "VE")
    p.dma(G, d_scb.ap(), scb8)
    p.dwait(S, "gpsimd")
    p.op(V, lambda e: e.memset(s_i2, 0.0), inc="VE")
    srs2 = _carve(AR1, F32, 24576, 2048)
    scbc2 = _carve(AR1, F32, 32768, 2048)
    ri_tok = {}
    ve_rr = {}
    for r in range(9):
        if r < 8:
            sb_s = srs if r % 2 == 0 else srs2
            sb_c = scbc if r % 2 == 0 else scbc2
            if r >= 2:
                p.wait(S, "VE", ve_rr[r - 2])
            p.dma(S, sb_s, d_ccout.ap()[129 * r:129 * r + 128, :])
            p.dma(S, sb_c, d_scb.ap()[r].unsqueeze(0).unsqueeze(1).to_broadcast([1, 128, INTER]))
            ri_tok[r] = p.dclose("sync")
        if r >= 1:
            rr = r - 1
            sb_s = srs if rr % 2 == 0 else srs2
            sb_c = scbc if rr % 2 == 0 else scbc2
            p.ewait(V, ri_tok[rr])
            p.op(V, lambda e, sb_s=sb_s, sb_c=sb_c: e.tensor_tensor(tmp2, sb_s, sb_c, ALU.mult), inc="VE")
            p.op(V, lambda e: e.tensor_tensor(s_i2, s_i2, tmp2, ALU.add), inc="VE")
            ve_rr[rr] = p.sems["VE"].n

    xts1p = _carve(AR1, BF16, 0, 2048)
    xts2p = _carve(AR1, BF16, 4096, 2048)
    sl1 = _carve(AR1, BF16, 8192, 2048)
    sl2 = _carve(AR1, BF16, 12288, 2048)
    zs1 = _carve(AR1, BF16, 16384, 2048)
    zs2 = _carve(AR1, BF16, 20480, 2048)
    relb = _carve(AR1, F32, 24576, 4096)
    bufA = _carve(AR1, F32, 40960, 4096)
    wobf1 = _carve(AR1, BF16, 57344, 2048)
    wobf2 = _carve(AR1, BF16, 61440, 2048)
    bufB = _carve(AR2, F32, 20992, 4096)
    mt_b = _carve(AR2, BF16, 37376, 4096)
    ct_b = _carve(AR2, BF16, 45568, 4096)
    xtq = _carve(AR2, BF16, 53760, 2048)
    graw = _carve(AR2, F32, 57856, 2048)
    g2b = _carve(AR2, BF16, 66048, 2048)
    scm = _carve(AR2, F32, 0, 128)
    ggrp = _carve(xn, BF16, 0, 8192)
    wostg = _carve(xn, F32, 16384, 2048)
    wos3 = _carve(xn, BF16, 16384, 2048)
    wos4 = _carve(xn, BF16, 20480, 2048)
    obuf2 = _carve(xn, F32, 24576, 2048)

    p.barrier("VE", "GD", "LD")
    p.dclose("sync")
    p.dclose("gpsimd")
    st_tok, rl_tok = {}, {}
    wi_tok, wb_tok, ve_wfold = {}, {}, {}
    mm_scores, mm_y = {}, {}
    ac_expg, ve_bufA, ac_bufB, ac_L, ac_si2b, ac_g2b, ve_dec = {}, {}, {}, {}, {}, {}, {}
    ve_bufA_h, ac_expg_h, ve_mt_h, ve_ct_h, ve_g2b_h, mm_y_bank, wo_pre = {}, {}, {}, {}, {}, {}, {}
    ve_scm, ve_ctb, ve_graw, ve_g2b = {}, {}, {}, {}
    mm_sumsq, ac_rsq, ve_rrec, mm_rsb, ve_ggrp = {}, {}, {}, {}, {}
    ve_oc_slot, mm_wo_slot = {}, {}
    mm_op_last = [None]
    out_tok = [None, None]

    def streams(c):
        if c >= NCH:
            return
        if c >= 2:
            p.wait(S, "MM", mm_y[c - 2])
            p.wait(S, "VE", ve_graw[c - 2])
        p.dma(S, xts1p if c % 2 == 0 else xts2p, d_xtok.ap()[c])
        p.dma(S, sl1 if c % 2 == 0 else sl2, d_sall.ap()[c])
        p.dma(S, zs1 if c % 2 == 0 else zs2,
              bass.AP(d_zsil, c * 16 * 128 * C, [[C, 128], [128 * C, 16], [1, C]]))
        st_tok[c] = p.dclose("sync")

    def relb_pre(c):
        if c >= NCH:
            return
        rl_tok[c] = []
        if c >= 1:
            p.wait(S, "VE", ve_bufA_h[(c - 1, 0)])
            p.wait(S, "AC", ac_expg_h[(c - 1, 0)])
        p.dma(S, relb[:, 0:2048],
              d_rel.ap()[c][0:2048].unsqueeze(0).unsqueeze(1).to_broadcast([1, 128, 2048]))
        rl_tok[c].append(p.dclose("sync"))
        if c >= 1:
            p.wait(G, "VE", ve_bufA_h[(c - 1, 1)])
            p.wait(G, "AC", ac_expg_h[(c - 1, 1)])
        p.dma(G, relb[:, 2048:4096],
              d_rel.ap()[c][2048:4096].unsqueeze(0).unsqueeze(1).to_broadcast([1, 128, 2048]))
        rl_tok[c].append(p.dclose("gpsimd"))

    def emitF1(c):
        if c >= NCH:
            return
        if c >= 1:
            p.wait(V, "AC", ac_L[(c - 1, 1)])
        for hh in range(2):
            hc = hh * 2048
            p.ewait(V, rl_tok[c][hh])
            p.op(V, lambda e, c=c, hh=hh, hc=hc: e.tensor_tensor(
                bufA[:, hc:hc + 2048].rearrange("p (h l) -> p h l", h=16),
                rtok[:, c * NH + hh * 16:c * NH + hh * 16 + 16].unsqueeze(2).to_broadcast([128, 16, C]),
                relb[:, hc:hc + 2048].rearrange("p (h l) -> p h l", h=16), ALU.subtract), inc="VE")
            ve_bufA_h[(c, hh)] = p.sems["VE"].n
        ve_bufA[c] = p.sems["VE"].n
        p.wait(V, "MM", mm_scores[c])
        p.op(V, lambda e: e.tensor_tensor(scm, PSB[:, 0:128], maskle[:, :], ALU.mult), inc="VE")
        ve_scm[c] = p.sems["VE"].n

    def emitAC(c):
        if c >= NCH:
            return
        for hh in range(2):
            hc = hh * 2048
            p.ewait("scalar", rl_tok[c][hh])
            if c >= 1:
                p.wait("scalar", "VE", ve_ct_h[(c - 1, hh)])
            p.op("scalar", lambda e, hc=hc: e.activation(
                bufB[:, hc:hc + 2048], relb[:, hc:hc + 2048], ACTF.Exp, bias=0.0, scale=1.0), inc="AC")
            ac_expg_h[(c, hh)] = p.sems["AC"].n
            p.wait("scalar", "VE", ve_bufA_h[(c, hh)])
            p.op("scalar", lambda e, hc=hc: e.activation(
                bufA[:, hc:hc + 2048], bufA[:, hc:hc + 2048], ACTF.Relu, bias=0.0, scale=1.0), inc="AC")
            p.op("scalar", lambda e, hc=hc: e.activation(
                bufA[:, hc:hc + 2048], bufA[:, hc:hc + 2048], ACTF.Exp, bias=0.0, scale=-1.0), inc="AC")
            ac_L[(c, hh)] = p.sems["AC"].n
        ac_expg[c] = p.sems["AC"].n

    def emitF2(c):
        if c >= NCH:
            return
        p.ewait(V, st_tok[c])
        if c >= 1:
            p.wait(V, "MM", mm_y[c - 1])
            p.wait("scalar", "MM", mm_y[c - 1])
            p.wait("scalar", "VE", ve_dec[c - 1])
        p.op("scalar", lambda e: e.activation(s_i2b, s_i2, ACTF.Copy, bias=0.0, scale=1.0), inc="AC")
        ac_si2b[c] = p.sems["AC"].n
        xts = xts1p if c % 2 == 0 else xts2p
        p.op(V, lambda e, c=c, xts=xts: e.tensor_tensor(
            xtq.rearrange("p (h q) -> p h q", h=NH),
            xts.rearrange("p (h q) -> p h q", h=NH),
            dtok[:, c * NH:(c + 1) * NH].unsqueeze(2).to_broadcast([128, NH, HD]), ALU.mult), inc="VE")
        if c + 1 < NCH:
            p.wait(V, "AC", ac_si2b[c])
            p.op(V, lambda e, c=c: e.tensor_tensor(
                s_i2.rearrange("p (h q) -> p h q", h=NH),
                s_i2.rearrange("p (h q) -> p h q", h=NH),
                crow_all[:, c * 32:(c + 1) * 32].unsqueeze(2).to_broadcast([128, NH, HD]), ALU.mult), inc="VE")
        ve_dec[c] = p.sems["VE"].n

    def emitMT(c):
        for hh in range(2):
            hc = hh * 2048
            p.wait(V, "AC", ac_expg_h[(c, hh)])
            p.op(V, lambda e, c=c, hc=hc: e.tensor_tensor(
                ct_b[:, hc:hc + 2048].rearrange("p (h l) -> p h l", h=16),
                Cf[:, c * C:(c + 1) * C].unsqueeze(1).to_broadcast([128, 16, C]),
                bufB[:, hc:hc + 2048].rearrange("p (h l) -> p h l", h=16), ALU.mult), inc="VE")
            ve_ct_h[(c, hh)] = p.sems["VE"].n
            p.wait(V, "AC", ac_L[(c, hh)])
            p.op(V, lambda e, hc=hc: e.tensor_tensor(
                mt_b[:, hc:hc + 2048].rearrange("p (h l) -> p h l", h=16),
                bufA[:, hc:hc + 2048].rearrange("p (h l) -> p h l", h=16),
                scm.unsqueeze(1).to_broadcast([128, 16, C]), ALU.mult), inc="VE")
            ve_mt_h[(c, hh)] = p.sems["VE"].n
        ve_ctb[c] = p.sems["VE"].n

    def emitScores(c):
        if c >= NCH:
            return
        if c >= 1:
            p.wait("tensor", "VE", ve_scm[c - 1])
        else:
            p.wait("tensor", "VE")
            p.wait("tensor", "AC")
        p.op("tensor", lambda e, c=c: e.matmul(
            PSB[:, 0:128], Bf[:, c * C:(c + 1) * C], Cf[:, c * C:(c + 1) * C],
            start=True, stop=True), inc="MM")
        mm_scores[c] = p.sems["MM"].n

    def emitY(c):
        slc = sl1 if c % 2 == 0 else sl2
        xts = xts1p if c % 2 == 0 else xts2p
        p.wait("tensor", "VE", ve_mt_h[(c, 0)])
        p.wait("tensor", "AC", ac_si2b[c])
        for t in range(16):
            if t == 8:
                p.wait("tensor", "VE", ve_mt_h[(c, 1)])
            bank = t // 4
            col0 = bank * 512 + (t % 4) * 128
            for hh in range(2):
                h = 2 * t + hh
                pvh = PSA[hh * 64:hh * 64 + 64, col0:col0 + 128]
                stt = (t % 4 == 0)
                sp = (t % 4 == 3)
                tp = (0, hh * 64)
                p.op("tensor", lambda e, h=h, pvh=pvh, stt=stt, tp=tp: e.matmul(
                    pvh, xtq[:, h * HD:(h + 1) * HD], mt_b[:, h * C:(h + 1) * C],
                    start=stt, stop=False, tile_position=tp, skip_group_check=True), inc=None)
                p.op("tensor", lambda e, h=h, pvh=pvh, slc=slc, tp=tp: e.matmul(
                    pvh, slc[:, h * HD:(h + 1) * HD], ct_b[:, h * C:(h + 1) * C],
                    start=False, stop=False, tile_position=tp, skip_group_check=True), inc=None)
                p.op("tensor", lambda e, h=h, pvh=pvh, tp=tp: e.matmul(
                    pvh, s_i2b[:, h * HD:(h + 1) * HD], ct_b[:, h * C:(h + 1) * C],
                    start=False, stop=False, tile_position=tp, skip_group_check=True), inc=None)
                p.op("tensor", lambda e, h=h, pvh=pvh, xts=xts, sp=sp, tp=tp: e.matmul(
                    pvh, xts[:, h * HD:(h + 1) * HD], dii[:, h * C:(h + 1) * C],
                    start=False, stop=sp, tile_position=tp, skip_group_check=True), inc="MM" if (sp and hh == 1) else None)
            if t % 4 == 3:
                mm_y_bank[(c, t // 4)] = p.sems["MM"].n
        mm_y[c] = p.sems["MM"].n

    def emitG(c):
        zsc = zs1 if c % 2 == 0 else zs2
        for bank in range(4):
            p.wait(V, "MM", mm_y_bank[(c, bank)])
            p.op(V, lambda e, bank=bank, zsc=zsc: e.tensor_tensor(
                graw[:, bank * 512:(bank + 1) * 512], PSA[:, bank * 512:(bank + 1) * 512],
                zsc[:, bank * 512:(bank + 1) * 512], ALU.mult), inc="VE")
        ve_graw[c] = p.sems["VE"].n
        p.op(V, lambda e: e.tensor_tensor(g2b[:, 0:1024], graw[:, 0:1024], graw[:, 0:1024], ALU.mult), inc="VE")
        ve_g2b_h[(c, 0)] = p.sems["VE"].n
        p.op(V, lambda e: e.tensor_tensor(g2b[:, 1024:2048], graw[:, 1024:2048], graw[:, 1024:2048], ALU.mult), inc="VE")
        ve_g2b[c] = p.sems["VE"].n

    def emitSumsq(c):
        p.wait("tensor", "VE", ve_g2b_h[(c, 0)])
        if c >= 1:
            p.wait("tensor", "AC", ac_rsq[c - 1])
        for tt in range(16):
            if tt == 8:
                p.wait("tensor", "VE", ve_g2b[c])
            p.op("tensor", lambda e, tt=tt: e.matmul(
                PSB[0:1, 672:800], onesb[:, :], g2b[:, tt * 128:(tt + 1) * 128],
                start=(tt == 0), stop=(tt == 15)), inc="MM" if tt == 15 else None)
        mm_sumsq[c] = p.sems["MM"].n
        p.wait("scalar", "MM", mm_sumsq[c])
        p.op("scalar", lambda e: e.activation(
            PSB[0:1, 672:800], PSB[0:1, 672:800], ACTF.Sqrt, bias=c1e5[0:1, 0:1], scale=1.0 / 2048.0), inc="AC")
        ac_rsq[c] = p.sems["AC"].n

    def emitTail(c):
        p.wait(V, "AC", ac_rsq[c])
        p.op(V, lambda e: e.reciprocal(rrow[:, :], PSB[0:1, 672:800]), inc="VE")
        ve_rrec[c] = p.sems["VE"].n
        p.wait("tensor", "VE", ve_rrec[c])
        p.op("tensor", lambda e: e.matmul(PSB[:, 800:928], onesrF[:, :], rrow[:, :], start=True, stop=True), inc="MM")
        mm_rsb[c] = p.sems["MM"].n
        p.wait(V, "MM", mm_rsb[c])
        if mm_op_last[0] is not None:
            p.wait(V, "MM", mm_op_last[0])
        p.op(V, lambda e, c=c: e.tensor_tensor(
            ggrp.rearrange("p (t g l) -> p t g l", t=16, g=4)[:, :, c % 4, :],
            graw.rearrange("p (t l) -> p t l", t=16),
            PSB[:, 800:928].unsqueeze(1).to_broadcast([128, 16, 128]), ALU.mult), inc="VE")
        ve_ggrp[c] = p.sems["VE"].n

    def bounce(c):
        if c >= 4:
            return
        for mi in (2 * c, 2 * c + 1):
            inb = wostg if mi % 2 == 0 else obuf2
            outb = wobf1 if mi % 2 == 0 else wobf2
            p.ewait(V, wi_tok[mi])
            if mi - 2 >= 0:
                p.ewait(V, wb_tok[mi - 2])
            p.op(V, lambda e, inb=inb, outb=outb: e.tensor_tensor(
                outb.rearrange("p (k c2) -> p k c2", k=16),
                inb.rearrange("p (k c2) -> p k c2", k=16),
                nw_s[:, :].unsqueeze(2).to_broadcast([128, 16, 128]), ALU.mult), inc="VE")
            ve_wfold[mi] = p.sems["VE"].n
            p.wait(G, "VE", ve_wfold[mi])
            p.dma(G, d_woutb.ap()[mi], outb)
            wb_tok[mi] = p.dclose("gpsimd")
        for mi in (2 * c + 2, 2 * c + 3):
            if mi < 8:
                p.wait(S, "VE", ve_wfold[mi - 2])
                p.dma(S, wostg if mi % 2 == 0 else obuf2, d_wout.ap()[mi].rearrange("p k c -> p (k c)"))
                wi_tok[mi] = p.dclose("sync")

    def wo_dma_issue(gidx, m, wo_tok, mm_op):
        if m >= 8:
            return
        slot = m % 4
        if gidx == 0:
            p.ewait(S, wb_tok[6])
            p.ewait(S, wb_tok[7])
            p.ewait(S, wb_tok[m])
            if slot >= 2:
                p.wait(S, "VE", ve_wfold[6])
                p.wait(G, "VE", ve_wfold[6])
        if m - 4 in mm_op:
            p.wait(S, "MM", mm_op[m - 4])
        elif slot in mm_wo_slot:
            p.wait(S, "MM", mm_wo_slot[slot])
        wob = WOBS[slot]
        if slot % 2 == 1:
            if m - 4 in mm_op:
                p.wait(G, "MM", mm_op[m - 4])
            elif slot in mm_wo_slot:
                p.wait(G, "MM", mm_wo_slot[slot])
            p.dma(G, wob, d_woutb.ap()[m])
            wo_tok[m] = [p.dclose("gpsimd")]
        else:
            p.dma(S, wob, d_woutb.ap()[m])
            wo_tok[m] = [p.dclose("sync")]

    def emitOutproj(c):
        gidx = c // 4
        mm_op = {}
        if ("tok", gidx) in wo_pre:
            wo_tok = wo_pre.pop(("tok", gidx))
        else:
            wo_tok = {}
            for mq in range(4):
                wo_dma_issue(gidx, mq, wo_tok, mm_op)

        def wo_dma(m):
            wo_dma_issue(gidx, m, wo_tok, mm_op)
        for m in range(8):
            wob = WOBS[m % 4]
            for tk in wo_tok[m]:
                p.ewait("tensor", tk)
            if m == 0:
                p.wait("tensor", "VE", ve_ggrp[c])
            if (m % 2) in ve_oc_slot:
                p.wait("tensor", "VE", ve_oc_slot[m % 2])
            pv = PSB[:, 1024 + (m % 2) * 512:1024 + (m % 2) * 512 + 512]
            for k in range(16):
                p.op("tensor", lambda e, k=k, pv=pv, wob=wob: e.matmul(
                    pv, wob[:, k * 128:(k + 1) * 128], ggrp[:, k * 512:(k + 1) * 512],
                    start=(k == 0), stop=(k == 15)), inc="MM" if k == 15 else None)
            mm_op[m] = p.sems["MM"].n
            mm_wo_slot[m % 4] = mm_op[m]
            wo_dma(m + 4)
            p.wait(V, "MM", mm_op[m])
            if m == 0:
                p.ewait(V, out_tok[1])
            if m == 4:
                p.ewait(V, out_tok[0])
            p.op(V, lambda e, m=m, pv=pv: e.tensor_copy(obuf2[:, (m % 4) * 512:(m % 4) * 512 + 512], pv), inc="VE")
            ve_oc_slot[m % 2] = p.sems["VE"].n
            if m % 4 == 3:
                p.wait(G, "VE", ve_oc_slot[m % 2])
                p.dma(G, bass.AP(d_out, gidx * 512 + (m // 4) * 4 * 128 * T, [[T, 128], [128 * T, 4], [1, 512]]),
                      obuf2.rearrange("p (m l) -> p m l", m=4), accum=True)
                out_tok[m // 4] = p.dclose("gpsimd")
        mm_op_last[0] = mm_op[7]

    WOBS = [wobf1, wobf2, wos3, wos4]
    streams(0)
    streams(1)
    relb_pre(0)
    for mi in range(2):
        p.dma(S, wostg if mi % 2 == 0 else obuf2, d_wout.ap()[mi].rearrange("p k c -> p (k c)"))
        wi_tok[mi] = p.dclose("sync")
    emitScores(0)
    emitF1(0)
    emitAC(0)
    relb_pre(1)
    emitF2(0)
    for c in range(NCH):
        bounce(c)
        emitMT(c)
        emitScores(c + 1)
        emitY(c)
        emitF1(c + 1)
        emitAC(c + 1)
        relb_pre(c + 2)
        emitG(c)
        streams(c + 2)
        emitSumsq(c)
        emitF2(c + 1)
        if c >= 1 and (c - 1) % 4 == 3:
            emitOutproj(c - 1)
        emitTail(c)
        if c % 4 == 2 and c // 4 >= 1:
            wo_tok_pre = {}
            wo_pre[("tok", c // 4)] = wo_tok_pre
            for mq in range(4):
                wo_dma_issue(c // 4, mq, wo_tok_pre, {})
    emitOutproj(NCH - 1)
    p.barrier("GD", "LD", "VE", "MM", "AC", "GP")
    p.dma(G, d_out.ap(), d_u.ap()[:, H:TW], accum=True)
    p.dma_barrier(S)
    for snm in ("VE", "MM", "AC", "GP", "CC"):
        p.wait(S, snm)

    p.build()
    p.es.close()
    return nc


def kernel(**inputs):
    u = np.ascontiguousarray(inputs["L0_attn_u"])
    win = inputs["L0_attn_in_proj_w"]
    w1 = inputs["L0_ffn_w1"]
    wout = inputs["L0_attn_out_proj_w"]
    w1m = np.ascontiguousarray(w1.reshape(KT, 128, 32, 128).transpose(2, 1, 0, 3))
    woutm = np.ascontiguousarray(wout.reshape(16, 128, 8, 128).transpose(2, 1, 0, 3))
    in_maps = []
    for core in range(8):
        b, q = core // 4, core % 4
        t0 = q * T
        halo = np.zeros((H, D), np.float32) if q == 0 else u[b, t0 - H:t0]
        u_r = np.ascontiguousarray(np.concatenate([halo, u[b, t0:t0 + T]], axis=0).T)
        sel = np.zeros((8, 1), np.float32)
        btwT = np.zeros((8, 8), np.float32)
        for r in range(8):
            rb, rq = r // 4, r % 4
            if rb == b and rq < q:
                sel[r, 0] = 1.0
            for j in range(8):
                jb, jq = j // 4, j % 4
                if jb == b and rb == b and rq < jq < q:
                    btwT[j, r] = 1.0
        in_maps.append({
            "u_r": u_r, "win": win, "convw": inputs["L0_attn_conv_w"],
            "convb": inputs["L0_attn_conv_b"], "dtb": inputs["L0_attn_dt_bias"],
            "alog": inputs["L0_attn_A_log"], "dpar": inputs["L0_attn_D_param"],
            "normw": inputs["L0_attn_norm_w"], "w1m": w1m, "b1": inputs["L0_ffn_b1"],
            "w2": inputs["L0_ffn_w2"], "b2": inputs["L0_ffn_b2"], "woutm": woutm,
            "sel": sel, "btwT": btwT,
        })
    nc = build_nc()
    res = run_bass_kernel_spmd(nc, in_maps, core_ids=list(range(8)))
    out = np.zeros((2, 8192, D), np.float32)
    for core in range(8):
        b, q = core // 4, core % 4
        out[b, q * T:(q + 1) * T] = np.ascontiguousarray(res.results[core]["out"]).T
    return out


